# version 05
# baseline (speedup 1.0000x reference)
import numpy as np
import concourse.bass as bass
import concourse.mybir as mybir
from concourse.bass_utils import run_bass_kernel_spmd
from contextlib import ExitStack

F32 = mybir.dt.float32
BF16 = mybir.dt.bfloat16
U32 = mybir.dt.uint32
AF = mybir.ActivationFunctionType
ALU = mybir.AluOpType
AX = mybir.AxisListType

T_ALL = 8192
C = 2048
E = 16
CAP = 1024
I_DIM = 1408
NC = 8
EPS = 1e-5
SCALE = float(1.0 / np.sqrt(128.0))
LN2H = -float(np.log(2.0) / 2.0)
BIG = 1.0e9
QUEUES = ("pe", "act", "dve", "pool", "sp")
SERIAL_QUEUES = ("dve", "act", "pool")
ENG_ATTR = {"pe": "tensor", "act": "scalar", "dve": "vector", "pool": "gpsimd", "sp": "sync"}


class Prog:
    def __init__(self, nc):
        self.nc = nc
        self.q = {k: [] for k in QUEUES}
        self._sems = {}
        self._prev = {}

    def sem(self, name):
        if name not in self._sems:
            h = self.nc.semaphore(name).__enter__()
            self._sems[name] = [h, 0]
        return self._sems[name]

    def cnt(self, name):
        return self.sem(name)[1]

    def mark(self, name):
        h, c = self.sem(name)
        return (name, c)

    def emit(self, queue, fn, waits=(), inc=()):
        wlist = []
        for w in waits:
            if w is None:
                continue
            if isinstance(w, str):
                h, c = self.sem(w)
                if c > 0:
                    wlist.append((h, c))
            else:
                name, val = w
                if val > 0:
                    h, _ = self.sem(name)
                    wlist.append((h, val))
        if queue in SERIAL_QUEUES and queue in self._prev:
            wlist.append(self._prev[queue])
        ilist = []
        inc = list(inc)
        if queue in SERIAL_QUEUES and not inc:
            inc = [queue + "_ser"]
        for i in inc:
            name, n = (i, 1) if isinstance(i, str) else i
            rec = self.sem(name)
            rec[1] += n
            ilist.append((rec[0], n))
        if queue in SERIAL_QUEUES and ilist:
            h0 = ilist[0][0]
            for name, rec in self._sems.items():
                if rec[0] is h0:
                    self._prev[queue] = (h0, rec[1])
                    break
        self.q[queue].append((fn, wlist, ilist))

    def dma(self, queue, out, in_, waits=(), sem=None, **kw):
        name = sem or (queue + "_dma")
        self.emit(queue, lambda e: e.dma_start(out=out, in_=in_, **kw),
                  waits=waits, inc=[(name, 16)])
        return self.mark(name)

    def barrier(self, name="bar"):
        allw = [(nm, rec[1]) for nm, rec in self._sems.items() if rec[1] > 0 and nm != name]
        for qn in QUEUES:
            self.emit(qn, lambda e: e.nop(), waits=allw, inc=[name])
        total = self.sem(name)[1]
        for qn in QUEUES:
            self.emit(qn, lambda e: e.nop(), waits=[(name, total)])

    def build_blocks(self):
        nc = self.nc
        with nc.Block() as blk:
            for qn in QUEUES:
                ops = self.q[qn]
                def mk(ops):
                    def body(eng):
                        for fn, waits, incs in ops:
                            for h, v in waits:
                                eng.wait_ge(h, v)
                            inst = fn(eng)
                            for j, (h, n) in enumerate(incs):
                                if inst is None or j > 0:
                                    eng.nop().then_inc(h, n)
                                else:
                                    inst.then_inc(h, n)
                    return body
                getattr(blk, ENG_ATTR[qn])(mk(ops))


def chunk_owner(g):
    gb, j = divmod(g, 32)
    return gb * 4 + j % 4, j // 4


def build_core():
    nc = bass.Bass("TRN2", target_bir_lowering=False)
    def dram(n, s, d=F32, k=None):
        t = nc.dram_tensor(n, s, d, kind=k) if k else nc.dram_tensor(n, s, d)
        return t.ap()
    x_d = dram("x", [T_ALL, C], F32, "ExternalInput")
    xo_d = dram("xo", [1024, C], F32, "ExternalInput")
    tok_d = dram("tok", [128, 8], U32, "ExternalInput")
    eix_d = dram("eix", [128, 16], U32, "ExternalInput")
    rv_d = dram("rv", [128, 1], F32, "ExternalInput")
    wq_d = dram("wq", [C, C], F32, "ExternalInput")
    wk_d = dram("wk", [C, C], F32, "ExternalInput")
    wv_d = dram("wv", [C, C], F32, "ExternalInput")
    wo_d = dram("wo", [C, C], F32, "ExternalInput")
    bq_d = dram("bq", [1, C], F32, "ExternalInput")
    bk_d = dram("bk", [1, C], F32, "ExternalInput")
    bv_d = dram("bv", [1, C], F32, "ExternalInput")
    bo_d = dram("bo", [1, C], F32, "ExternalInput")
    gw_d = dram("gw", [E, C], F32, "ExternalInput")
    wg_d = dram("wg", [2, C, I_DIM], F32, "ExternalInput")
    wu_d = dram("wu", [2, C, I_DIM], F32, "ExternalInput")
    wd_d = dram("wd", [2, I_DIM, C], F32, "ExternalInput")
    wsg_d = dram("wsg", [C, 2816], F32, "ExternalInput")
    wsu_d = dram("wsu", [C, 2816], F32, "ExternalInput")
    wsd_d = dram("wsd", [2816, C], F32, "ExternalInput")
    y_d = dram("y", [1024, C], F32, "ExternalOutput")

    xn_d = dram("xn_dd", [1024, C], BF16)
    lo_d = dram("lo_dd", [1024, C], BF16)
    r_in = dram("r_in", [1024, 4], F32)
    r_all = dram("r_all", [T_ALL, 4], F32)
    q_dd = dram("q_dd", [C, 1024], BF16)
    kvk_in = dram("kvk_in", [C, 1024], BF16)
    kvv_in = dram("kvv_in", [1024, C], BF16)
    kt_all = dram("kt_all", [4 * C, 1024], BF16)
    v_all = dram("v_all", [4096, C], BF16)
    o_dd = dram("o_dd", [1024, C], BF16)
    g_dd = dram("g_dd", [E * CAP, 4], U32)
    ci_dd = dram("ci_dd", [T_ALL, 2], U32)
    cw_dd = dram("cw_dd", [T_ALL, 2], F32)
    e_dd = dram("e_dd", [CAP, C], BF16)
    eo_in = dram("eo_in", [2 * CAP, C], BF16)
    eo_all = dram("eo_all", [NC * 2 * CAP, C], BF16)

    es = ExitStack()
    T = lambda n, s, d: es.enter_context(nc.sbuf_tensor(n, s, d))
    PT = lambda n: es.enter_context(nc.psum_tensor(n, [128, 512], F32))

    out_res = T("out_res", [128, 8, C], F32)
    xnT = T("xnT", [128, 16, 1024], BF16)
    a_at = T("a_at", [128, 11264], BF16)
    at_v = a_at[:, 0:11264].rearrange("p (a t) -> p a t", a=11)
    kth = a_at[:, 0:4096].rearrange("p (a t) -> p a t", a=4)
    vh = a_at[:, 4096:8224].rearrange("p (a m t) -> p a m t", a=4, m=8)
    pt_v = a_at[:, 8224:9248].rearrange("p (a t) -> p a t", a=8)
    big16 = T("big16", [128, 16, 512], BF16)
    wrhs = T("wrhs", [128, 16, 512], BF16)
    wlhs = T("wlhs", [128, 2, 16, 128], BF16)
    sx = T("sx", [128, C], F32)
    sbf = T("sbf", [128, C], BF16)
    f32s = T("f32s", [128, 1024], F32)
    lgt = f32s[0:16, 0:1024]
    sfp = f32s[:, 0:128]
    qh = T("qh", [128, 2, 1024], BF16)
    brow = T("brow", [1, 4 * C], BF16)
    alib = T("alib", [128, 512], F32)
    mtab = T("mtab", [128, 4, 128], F32)
    ltri = T("ltri", [128, 128], F32)
    e16 = T("e16", [128, 16], F32)
    e16b = T("e16b", [128, 16], F32)
    iop = T("iop", [128, 1], F32)
    rvp = T("rvp", [128, 1], F32)
    idf = T("idf", [128, 16], F32)
    ones_rb = T("ones_rb", [1, 512], BF16)
    ones_rf = T("ones_rf", [1, 128], F32)
    ones_cf = T("ones_cf", [128, 1], F32)
    ones128f = T("ones128f", [128, 128], F32)
    gthi = T("gthi", [128, 16, 16], BF16)
    gtlo = T("gtlo", [128, 16, 16], BF16)
    s16a = T("s16a", [128, 16], F32)
    s16b = T("s16b", [128, 16], F32)
    s16c = T("s16c", [128, 16], F32)
    w8 = T("w8", [128, 8], F32)
    rt4 = T("rt4", [128, 2, 4], F32)
    srun = T("srun", [1, 16], F32)
    cisb = T("cisb", [128, 64, 2], U32)
    cwsb = T("cwsb", [128, 64, 2], F32)
    tok_sb = T("tok_sb", [128, 8], U32)
    eix_sb = T("eix_sb", [128, 16], U32)
    ci2 = T("ci2", [128, 2], U32)
    cw2 = T("cw2", [128, 2], F32)
    su32 = T("su32", [128, 4], U32)
    su32b = T("su32b", [128, 4], U32)
    gi_t = T("gi_t", [128, 4], U32)
    stile = T("stile", [128, 512], BF16)
    vec1 = T("vec1", [128, 1], F32)
    vec2 = T("vec2", [128, 1], F32)
    vec3 = T("vec3", [128, 1], F32)
    vec4 = T("vec4", [128, 1], F32)
    dvec = T("dvec", [128, 2], F32)
    dn = T("dn", [128, 16], F32)
    vals = T("vals", [128, 4], U32)
    zv = T("zv", [128, 1], F32)
    epsv = T("epsv", [128, 1], F32)
    zpad = T("zpad", [128, 512], U32)

    psA = PT("psA"); psB = PT("psB")
    psO = [PT("psO%d" % i) for i in range(4)]
    psX = PT("psX"); psT = PT("psT")

    p = Prog(nc)
    regs = {}
    def setup_regs(e):
        r1 = e.alloc_register()
        e.reg_mov(r1, E * CAP - 1)
        regs["b16k"] = r1
        r2 = e.alloc_register()
        e.reg_mov(r2, T_ALL - 1)
        regs["b8k"] = r2
        return None
    p.emit("pool", setup_regs)

    p.emit("pool", lambda e: e.iota(e16[:, :], [[1, 16]], base=0, channel_multiplier=0,
                                    allow_small_or_imprecise_dtypes=True), inc=["k"])
    p.emit("pool", lambda e: e.iota(iop[:, :], [[1, 1]], base=0, channel_multiplier=1,
                                    allow_small_or_imprecise_dtypes=True), inc=["k"])
    p.emit("pool", lambda e: e.memset(ones_rb[:, :], 1.0), inc=["k"])
    p.emit("pool", lambda e: e.memset(ones_rf[:, :], 1.0), inc=["k"])
    p.emit("pool", lambda e: e.memset(ones_cf[:, :], 1.0), inc=["k"])
    p.emit("pool", lambda e: e.memset(ones128f[:, :], 1.0), inc=["k"])
    p.emit("pool", lambda e: e.memset(srun[:, :], 0.0), inc=["k"])
    p.emit("pool", lambda e: e.memset(zv[:, :], 0.0), inc=["k"])
    p.emit("pool", lambda e: e.memset(epsv[:, :], EPS), inc=["k"])
    p.emit("pool", lambda e: e.memset(zpad[:, :], 0xFFFFFFFF), inc=["k"])
    p.emit("pool", lambda e: e.iota(sx[:, 0:128], [[1, 128]], base=0, channel_multiplier=0,
                                    allow_small_or_imprecise_dtypes=True), inc=["k"])
    mrv = p.dma("sp", rvp[:, :], rv_d)
    m_tok = p.dma("sp", tok_sb[:, :], tok_d)
    m_eix = p.dma("sp", eix_sb[:, :], eix_d)
    kc0 = p.mark("k")
    p.emit("dve", lambda e: e.tensor_scalar(e16b[:, :], e16[:, :], 65536.0, None, ALU.add),
           waits=[kc0], inc=["kd"])
    p.emit("dve", lambda e: e.tensor_scalar(idf[:, :], e16[:, :], iop[:, :], None, ALU.is_equal), inc=["kd"])
    p.emit("dve", lambda e: e.tensor_scalar(ltri[:, :], sx[:, 0:128], iop[:, :], None, ALU.is_gt), inc=["kd"])
    for md in range(4):
        p.emit("dve", lambda e, md=md: e.tensor_scalar(vec1[:, :], iop[:, :], rvp[:, :], float(128 * md),
                                                       ALU.subtract, ALU.add), waits=[mrv], inc=["kd"])
        p.emit("dve", lambda e, md=md: e.tensor_scalar(mtab[:, md, :], sx[:, 0:128], vec1[:, :], -BIG,
                                                       ALU.is_lt, ALU.mult), inc=["kd"])
    mdc = p.mark("kd")
    p.emit("pool", lambda e: e.iota(sx[:, 128:640], [[0, 16], [1, 32]], base=-3, channel_multiplier=0,
                                    allow_small_or_imprecise_dtypes=True), waits=[mdc], inc=["k"])
    p.emit("pool", lambda e: e.iota(sx[:, 640:1152], [[1, 16], [0, 32]], base=1, channel_multiplier=0,
                                    allow_small_or_imprecise_dtypes=True), inc=["k"])
    p.emit("pool", lambda e: e.iota(sx[:, 1152:1664], [[0, 512]], base=0, channel_multiplier=1,
                                    allow_small_or_imprecise_dtypes=True), inc=["k"])
    kc1 = p.mark("k")
    p.emit("act", lambda e: e.activation(sx[:, 640:1152], sx[:, 640:1152], AF.Exp, bias=zv[:, :], scale=LN2H),
           waits=[kc1], inc=["ka"])
    p.emit("dve", lambda e: e.tensor_scalar(sx[:, 1152:1664], sx[:, 1152:1664], rvp[:, :], -64.0,
                                            ALU.subtract, ALU.add), waits=[kc1], inc=["kd"])
    p.emit("dve", lambda e: e.scalar_tensor_tensor(alib[:, :], sx[:, 128:640], -128.0, sx[:, 1152:1664],
                                                   ALU.mult, ALU.add), inc=["kd"])
    p.emit("dve", lambda e: e.tensor_tensor(alib[:, :], alib[:, :], sx[:, 640:1152], ALU.mult),
           waits=["ka"], inc=["kd"])
    mbq = p.dma("pool", brow[0:1, 0:C], bq_d)
    p.dma("pool", brow[0:1, C:2 * C], bk_d)
    p.dma("pool", brow[0:1, 2 * C:3 * C], bv_d)
    mbr = p.dma("pool", brow[0:1, 3 * C:4 * C], bo_d)
    mg_ = p.dma("sp", out_res[0:16, 0, :], gw_d)
    idm = p.mark("kd")
    for fc in range(16):
        def trg(e, fc=fc):
            return e.transpose(psB[0:128, 0:16], out_res[0:16, 0, fc * 128:(fc + 1) * 128], idf[0:16, 0:16])
        p.emit("pe", trg, waits=[mg_, idm, ("kgt", 2 * fc)], inc=["kp"])
        p.emit("act", lambda e, fc=fc: e.activation(gthi[:, fc, :], psB[:, 0:16], AF.Copy, bias=0.0, scale=1.0),
               waits=[("kp", fc + 1)], inc=["kgt"])
        p.emit("dve", lambda e, fc=fc: e.scalar_tensor_tensor(gtlo[:, fc, :], gthi[:, fc, :], -1.0, psB[:, 0:16],
                                                              ALU.mult, ALU.add),
               waits=[("kgt", 2 * fc + 1)], inc=["kgt"])
    mzp = p.mark("k")
    p.dma("pool", g_dd.rearrange("(p a) f -> p (a f)", p=128), zpad[:, :],
          waits=[mzp], sem="g_init")
    p.barrier("b0")

    def emit_ln(src_ap, waitm):
        p.emit("dve", lambda e: e.tensor_reduce(vec1[:, :], src_ap, axis=AX.X, op=ALU.add),
               waits=[waitm], inc=["ln_d"])
        p.emit("act", lambda e: e.activation(psO[0][:, :], src_ap[:, 0:512], AF.Square, bias=zv[:, :], scale=1.0,
                                             accum_out=vec2[:, :]), waits=[waitm], inc=["ln_a"])
        p.emit("act", lambda e: e.activation(psO[0][:, :], src_ap[:, 512:1024], AF.Square, bias=zv[:, :], scale=1.0,
                                             accum_out=vec3[:, :]), inc=["ln_a"])
        p.emit("act", lambda e: e.activation(psO[1][:, :], src_ap[:, 1024:1536], AF.Square, bias=zv[:, :], scale=1.0,
                                             accum_out=vec4[:, :]), inc=["ln_a"])
        p.emit("act", lambda e: e.activation(psO[1][:, :], src_ap[:, 1536:2048], AF.Square, bias=zv[:, :], scale=1.0,
                                             accum_out=dvec[:, 0:1]), inc=["ln_a"])
        p.emit("dve", lambda e: e.tensor_scalar(vec1[:, :], vec1[:, :], 1.0 / C, None, ALU.mult), inc=["ln_d"])
        am = p.mark("ln_a")
        p.emit("dve", lambda e: e.tensor_tensor(vec2[:, :], vec2[:, :], vec3[:, :], ALU.add), waits=[am], inc=["ln_d"])
        p.emit("dve", lambda e: e.tensor_tensor(vec4[:, :], vec4[:, :], dvec[:, 0:1], ALU.add), inc=["ln_d"])
        p.emit("dve", lambda e: e.tensor_tensor(vec2[:, :], vec2[:, :], vec4[:, :], ALU.add), inc=["ln_d"])
        p.emit("dve", lambda e: e.tensor_tensor(vec3[:, :], vec1[:, :], vec1[:, :], ALU.mult), inc=["ln_d"])
        p.emit("dve", lambda e: e.scalar_tensor_tensor(vec3[:, :], vec2[:, :], 1.0 / C, vec3[:, :],
                                                       ALU.mult, ALU.subtract), inc=["ln_d"])
        p.emit("act", lambda e: e.activation(vec3[:, :], vec3[:, :], AF.Sqrt, bias=epsv[:, :], scale=1.0),
               waits=["ln_d"], inc=["ln_a"])
        p.emit("dve", lambda e: e.reciprocal(vec4[:, :], vec3[:, :]), waits=["ln_a"], inc=["ln_d"])
        p.emit("dve", lambda e: e.tensor_scalar(sbf[:, :], src_ap, vec1[:, :], vec4[:, :],
                                                ALU.subtract, ALU.mult), waits=["sbf_fr"], inc=["ln_d"])
        return p.mark("ln_d")

    sx_free = None
    for m in range(8):
        mload = p.dma("sp", sx[:, :], xo_d[m * 128:(m + 1) * 128, :], waits=[sx_free])
        p.emit("dve", lambda e, m=m: e.tensor_copy(out_res[:, m, :], sx[:, :]), waits=[mload], inc=["ln_d"])
        mhi = emit_ln(sx[:, :], mload)
        mw1 = p.dma("sp", xn_d[m * 128:(m + 1) * 128, :], sbf[:, :], waits=[mhi], sem="sbf_fr")
        p.emit("dve", lambda e: e.tensor_scalar(sx[:, :], sx[:, :], vec1[:, :], vec4[:, :],
                                                ALU.subtract, ALU.mult), inc=["ln_d"])
        p.emit("dve", lambda e: e.tensor_tensor(sx[:, :], sx[:, :], sbf[:, :], ALU.subtract),
               waits=[mw1], inc=["ln_d"])
        p.emit("act", lambda e: e.activation(sbf[:, :], sx[:, :], AF.Copy, bias=0.0, scale=1.0),
               waits=["ln_d"], inc=["ln_a"])
        mlo = p.mark("ln_a")
        mw2 = p.dma("sp", lo_d[m * 128:(m + 1) * 128, :], sbf[:, :], waits=[mlo], sem="sbf_fr")
        sx_free = p.mark("ln_d")
    p.barrier("b1")

    m_st1 = p.mark("sbf_fr")
    for fc in range(16):
        p.dma("act", xnT[:, fc, :], xn_d[0:1024, fc * 128:(fc + 1) * 128], waits=[m_st1], sem="tp_dma", transpose=True)
        p.dma("act", big16[:, fc, :], lo_d[0:512, fc * 128:(fc + 1) * 128], sem="tp_dma", transpose=True)
        p.dma("act", wrhs[:, fc, :], lo_d[512:1024, fc * 128:(fc + 1) * 128], sem="tp_dma", transpose=True)
    mtp = p.mark("tp_dma")
    for m in range(8):
        loT = big16 if m < 4 else wrhs
        mloc = m % 4
        first = True
        for fc in range(16):
            for which in range(3):
                def mml(e, which=which, fc=fc, m=m, mloc=mloc, loT=loT, first=first):
                    gt = gthi if which != 2 else gtlo
                    lh = xnT[:, fc, m * 128:(m + 1) * 128] if which != 1 else loT[:, fc, mloc * 128:(mloc + 1) * 128]
                    return e.matmul(psB[:, 0:16], lh, gt[:, fc, :], start=first, stop=False)
                p.emit("pe", mml, waits=[mtp, "rt_d"] if first else [],
                       inc=["lg_p"] if (fc == 15 and which == 2) else [])
                first = False
        p.emit("dve", lambda e: e.tensor_copy(s16a[:, :], psB[:, 0:16]), waits=["lg_p"], inc=["rt_d"])
        p.emit("dve", lambda e: e.tensor_reduce(vec1[:, :], s16a[:, :], axis=AX.X, op=ALU.max, negate=True),
               inc=["rt_d"])
        p.emit("act", lambda e: e.activation(s16b[:, :], s16a[:, :], AF.Exp, bias=vec1[:, :], scale=1.0),
               waits=["rt_d"], inc=["rt_a"])
        p.emit("dve", lambda e: e.tensor_reduce(vec2[:, :], s16b[:, :], axis=AX.X, op=ALU.add),
               waits=["rt_a"], inc=["rt_d"])
        p.emit("dve", lambda e: e.reciprocal(vec3[:, :], vec2[:, :]), inc=["rt_d"])
        p.emit("dve", lambda e: e.tensor_scalar(s16a[:, :], s16b[:, :], vec3[:, :], None, ALU.mult), inc=["rt_d"])
        p.emit("dve", lambda e: e.max(out=w8[:, :], in_=s16a[:, :]), inc=["rt_d"])
        mm_ = m % 2
        p.emit("dve", lambda e: e.tensor_scalar(s16b[:, :], s16a[:, :], w8[:, 0:1], None, ALU.is_equal), inc=["rt_d"])
        p.emit("dve", lambda e: e.scalar_tensor_tensor(s16c[:, :], s16b[:, :], -65536.0, e16b[:, :], ALU.mult, ALU.add),
               inc=["rt_d"])
        p.emit("dve", lambda e: e.tensor_reduce(vec1[:, :], s16c[:, :], axis=AX.X, op=ALU.min), inc=["rt_d"])
        p.emit("dve", lambda e, mm_=mm_: e.tensor_copy(rt4[:, mm_, 0:1], vec1[:, :]),
               inc=["rt_d"])
        p.emit("dve", lambda e, mm_=mm_: e.tensor_scalar(s16b[:, :], e16[:, :], rt4[:, mm_, 0:1], None, ALU.is_equal),
               inc=["rt_d"])
        p.emit("dve", lambda e: e.scalar_tensor_tensor(s16c[:, :], s16b[:, :], -BIG, s16a[:, :], ALU.mult, ALU.add),
               inc=["rt_d"])
        p.emit("dve", lambda e, mm_=mm_: e.tensor_reduce(rt4[:, mm_, 3:4], s16c[:, :], axis=AX.X, op=ALU.max),
               inc=["rt_d"])
        p.emit("dve", lambda e, mm_=mm_: e.tensor_copy(rt4[:, mm_, 2:3], w8[:, 0:1]), inc=["rt_d"])
        p.emit("dve", lambda e, mm_=mm_: e.tensor_scalar(s16b[:, :], s16c[:, :], rt4[:, mm_, 3:4], None, ALU.is_equal),
               inc=["rt_d"])
        p.emit("dve", lambda e: e.scalar_tensor_tensor(s16c[:, :], s16b[:, :], -65536.0, e16b[:, :], ALU.mult, ALU.add),
               inc=["rt_d"])
        p.emit("dve", lambda e: e.tensor_reduce(vec1[:, :], s16c[:, :], axis=AX.X, op=ALU.min), inc=["rt_d"])
        p.emit("dve", lambda e, mm_=mm_: e.tensor_copy(rt4[:, mm_, 1:2], vec1[:, :]),
               inc=["rt_d"])
        mrt = p.mark("rt_d")
        p.dma("sp", r_in[m * 128:(m + 1) * 128, :], rt4[:, mm_, :], waits=[mrt], sem="rt_w")
    p.emit("pool", lambda e: e.collective_compute(
        "AllGather", ALU.bypass, replica_groups=[list(range(NC))],
        ins=[r_in.tensor.ap().opt()], outs=[r_all.tensor.ap().opt()]),
        waits=["rt_w"], inc=["cc"])
    cc_rt = p.mark("cc")

    ps_free = {0: p.mark("ps_ab"), 1: p.mark("rt_d")}
    wl_free = [None, None]
    for wi, (w_src, brow_i, dst_dram) in enumerate(((wq_d, 0, q_dd), (wk_d, 1, kvk_in))):
        for hdc in range(16):
            slot = hdc % 2
            mw = p.dma("pool", wlhs[:, slot, :, :],
                       w_src[:, hdc * 128:(hdc + 1) * 128].rearrange("(a p) c -> p a c", p=128),
                       waits=[wl_free[slot]], sem="w_dma")
            for tc in range(2):
                pi = tc
                ps = psA if pi == 0 else psB
                def mmb(e, hdc=hdc, brow_i=brow_i, ps=ps):
                    return e.matmul(ps[:, :], brow[0:1, brow_i * C + hdc * 128: brow_i * C + (hdc + 1) * 128],
                                    ones_rb[0:1, :], start=True, stop=False)
                p.emit("pe", mmb, waits=[mw, mbr, ps_free[pi]], inc=[])
                for fc in range(16):
                    def mmq(e, fc=fc, tc=tc, slot=slot, ps=ps):
                        return e.matmul(ps[:, :], wlhs[:, slot, fc, :], xnT[:, fc, tc * 512:(tc + 1) * 512],
                                        start=False, stop=(fc == 15))
                    p.emit("pe", mmq, inc=["qk_p"] if fc == 15 else [])
                mqk = p.mark("qk_p")
                bslot = tc
                def ev(e, bslot=bslot, ps=ps):
                    return e.activation(sbf[:, bslot * 512:(bslot + 1) * 512], ps[:, :], AF.Copy, bias=0.0, scale=1.0)
                p.emit("act", ev, waits=[mqk, ("qkv_st", max(0, p.cnt("qkv_st") - 16))], inc=["qkv_ev"])
                ps_free[pi] = p.mark("qkv_ev")
                p.dma("sp", dst_dram[hdc * 128:(hdc + 1) * 128, tc * 512:(tc + 1) * 512],
                      sbf[:, bslot * 512:(bslot + 1) * 512], waits=[ps_free[pi]], sem="qkv_st")
            wl_free[slot] = p.mark("qk_p")
    for hg in range(4):
        mwv = p.dma("pool", wrhs[:, :, :],
                    wv_d[:, hg * 512:(hg + 1) * 512].rearrange("(a p) c -> p a c", p=128),
                    waits=[("v_p", p.cnt("v_p")), ("lg_p", p.cnt("lg_p"))], sem="w_dma")
        for tchunk in range(8):
            pi = tchunk % 2
            ps = psA if pi == 0 else psB
            def mmbv(e, hg=hg, ps=ps):
                return e.matmul(ps[:, :], ones_rb[0:1, 0:128], brow[0:1, 2 * C + hg * 512: 2 * C + (hg + 1) * 512],
                                start=True, stop=False)
            p.emit("pe", mmbv, waits=[mwv, ps_free[pi]], inc=[])
            for fc in range(16):
                def mmv(e, fc=fc, tchunk=tchunk, ps=ps):
                    return e.matmul(ps[:, :], xnT[:, fc, tchunk * 128:(tchunk + 1) * 128], wrhs[:, fc, :],
                                    start=False, stop=(fc == 15))
                p.emit("pe", mmv, inc=["v_p"] if fc == 15 else [])
            mv = p.mark("v_p")
            bslot = 2 + pi
            def evv(e, bslot=bslot, ps=ps):
                return e.activation(sbf[:, bslot * 512:(bslot + 1) * 512], ps[:, :], AF.Copy, bias=0.0, scale=1.0)
            p.emit("act", evv, waits=[mv, ("qkv_st", max(0, p.cnt("qkv_st") - 16))], inc=["qkv_ev"])
            ps_free[pi] = p.mark("qkv_ev")
            p.dma("sp", kvv_in[tchunk * 128:(tchunk + 1) * 128, hg * 512:(hg + 1) * 512],
                  sbf[:, bslot * 512:(bslot + 1) * 512], waits=[ps_free[pi]], sem="qkv_st")
    p.emit("pool", lambda e: e.collective_compute(
        "AllGather", ALU.bypass, replica_groups=[[0, 1, 2, 3], [4, 5, 6, 7]],
        ins=[kvk_in.tensor.ap().opt()], outs=[kt_all.tensor.ap().opt()]),
        waits=["qkv_st"], inc=["cc"])
    p.emit("pool", lambda e: e.collective_compute(
        "AllGather", ALU.bypass, replica_groups=[[0, 1, 2, 3], [4, 5, 6, 7]],
        ins=[kvv_in.tensor.ap().opt()], outs=[v_all.tensor.ap().opt()]),
        inc=["cc"])
    cc_kv = p.mark("cc")
    p.barrier("b3")

    p.emit("pool", lambda e: e.memset(vh[:, :, :, 128:129], 1.0), inc=["k5"])
    run_step = [0]
    for g in range(64):
        rk, mg = chunk_owner(g)
        gm = g % 2
        mrt = p.dma("sp", rt4[:, gm, :], r_all[rk * 1024 + mg * 128: rk * 1024 + (mg + 1) * 128, :],
                    waits=[cc_rt, ("d4", p.cnt("d4"))], sem="r_ld")
        p.emit("dve", lambda e, gm=gm: e.tensor_scalar(s16a[:, :], e16[:, :], rt4[:, gm, 0:1], None, ALU.is_equal),
               waits=[mrt], inc=["d4"])
        p.emit("dve", lambda e, gm=gm: e.tensor_scalar(s16b[:, :], e16[:, :], rt4[:, gm, 1:2], None, ALU.is_equal),
               inc=["d4"])
        p.emit("dve", lambda e: e.tensor_tensor(s16c[:, :], s16a[:, :], s16b[:, :], ALU.add), inc=["d4"])
        mB = p.mark("d4")
        def mmbc(e):
            return e.matmul(psX[0:128, 0:16], ones_rf[0:1, :], srun[0:1, :], start=True, stop=False)
        p.emit("pe", mmbc, waits=[mB, ("d4run", run_step[0])], inc=[])
        def mmtri(e):
            return e.matmul(psX[0:128, 0:16], ltri[:, :], s16c[:, :], start=False, stop=True)
        p.emit("pe", mmtri, inc=["p4"])
        def mmtot(e, g=g):
            return e.matmul(psT[0:128, 0:16], ones128f[:, :], s16c[:, :], start=(g == 0), stop=(g == 63))
        p.emit("pe", mmtot, inc=["p4"])
        mp4 = p.mark("p4")
        p.emit("dve", lambda e: e.tensor_copy(s16c[:, :], psX[:, 0:16]), waits=[mp4], inc=["d4"])
        p.emit("dve", lambda e: e.tensor_copy(srun[0:1, :], psT[0:1, 0:16]), inc=["d4run"])
        run_step[0] += 1
        p.emit("dve", lambda e: e.tensor_tensor(s16a[:, :], s16a[:, :], s16c[:, :], ALU.mult), inc=["d4"])
        p.emit("dve", lambda e: e.tensor_reduce(vec1[:, :], s16a[:, :], axis=AX.X, op=ALU.add), inc=["d4"])
        p.emit("dve", lambda e: e.tensor_tensor(s16b[:, :], s16b[:, :], s16c[:, :], ALU.mult), inc=["d4"])
        p.emit("dve", lambda e: e.tensor_reduce(vec2[:, :], s16b[:, :], axis=AX.X, op=ALU.add), inc=["d4"])
        for kk, posv, dst in ((0, vec1, su32), (1, vec2, su32b)):
            p.emit("dve", lambda e, posv=posv: e.tensor_scalar(vec3[:, :], posv[:, :], float(CAP), 1.0e6,
                                                               ALU.is_ge, ALU.mult),
                   waits=[("sc4", p.cnt("sc4"))], inc=["d4"])
            p.emit("dve", lambda e, gm=gm, kk=kk, posv=posv: e.scalar_tensor_tensor(
                vec4[:, :], rt4[:, gm, kk:kk + 1], float(CAP), posv[:, :], ALU.mult, ALU.add), inc=["d4"])
            p.emit("dve", lambda e: e.tensor_tensor(vec4[:, :], vec4[:, :], vec3[:, :], ALU.add), inc=["d4"])
            p.emit("dve", lambda e, dst=dst: e.tensor_copy(dst[:, 0:1], vec4[:, :]), inc=["d4"])
        mid = p.mark("d4")
        p.emit("pool", lambda e, g=g: e.iota(vals[:, :], [[0, 4]], base=g * 128, channel_multiplier=1),
               waits=[("sc4", p.cnt("sc4"))], inc=["k4"])
        mvv = p.mark("k4")
        def sc1(e):
            return e.indirect_dma_start(g_dd, bass.IndirectOffsetOnAxis(su32[:, 0:1], 0), vals[:, :], None,
                                        bounds_check=regs["b16k"], oob_is_err=False)
        def sc2(e):
            return e.indirect_dma_start(g_dd, bass.IndirectOffsetOnAxis(su32b[:, 0:1], 0), vals[:, :], None,
                                        bounds_check=regs["b16k"], oob_is_err=False)
        p.emit("pool", sc1, waits=[mid, mvv, "g_init"], inc=[("sc4", 16)])
        p.emit("pool", sc2, inc=[("sc4", 16)])
        p.emit("dve", lambda e, g=g, gm=gm: e.tensor_copy(cwsb[:, g, 0:1], rt4[:, gm, 2:3]), inc=["d4"])
        p.emit("dve", lambda e, g=g, gm=gm: e.tensor_copy(cwsb[:, g, 1:2], rt4[:, gm, 3:4]), inc=["d4"])
        for kk, posv in ((0, vec1), (1, vec2)):
            p.emit("dve", lambda e, posv=posv: e.tensor_scalar(vec3[:, :], posv[:, :], float(CAP - 1), None,
                                                               ALU.min), inc=["d4"])
            p.emit("dve", lambda e, gm=gm, kk=kk: e.scalar_tensor_tensor(
                vec3[:, :], rt4[:, gm, kk:kk + 1], float(CAP), vec3[:, :], ALU.mult, ALU.add), inc=["d4"])
            p.emit("dve", lambda e, g=g, kk=kk: e.tensor_copy(cisb[:, g, kk:kk + 1], vec3[:, :]), inc=["d4"])
    mtab4 = p.mark("d4")
    p.dma("sp", ci_dd.rearrange("(a p) k -> p a k", p=128), cisb[:, :, :], waits=[mtab4], sem="ct_w")
    p.dma("sp", cw_dd.rearrange("(a p) k -> p a k", p=128), cwsb[:, :, :], waits=[mtab4], sem="ct_w")
    p.barrier("b4")

    qh_free = [None, None]
    kv_free = None
    exp_marks = {}
    psO_free = [None] * 4
    psS_free = {0: None, 1: None}
    stile_free = None
    for h in range(16):
        par = h % 2
        mq = p.dma("sp", qh[:, par, :], q_dd[h * 128:(h + 1) * 128, :], waits=[qh_free[par]], sem="kv_ld")
        mks = []
        for rho in range(4):
            mk1 = p.dma("sp", kth[:, rho, :], kt_all[rho * C + h * 128: rho * C + (h + 1) * 128, :],
                        waits=[cc_kv, kv_free], sem="kv_ld")
            mk2 = p.dma("sp", vh[:, rho, :, 0:128],
                        v_all[rho * 1024:(rho + 1) * 1024, h * 128:(h + 1) * 128]
                        .rearrange("(m p) d -> p m d", p=128),
                        waits=[cc_kv, kv_free, ("k5", 1)], sem="kv_ld")
            mks += [mk1, mk2]
        for qg in range(2):
            cis = list(range(4))
            mis = [qg * 4 + i for i in range(4)]
            kmax = 4 * mis[-1] + 3
            for kc in range(kmax + 1):
                pi = kc % 2
                ps = psA if pi == 0 else psB
                def mms(e, kc=kc, qg=qg, par=par, ps=ps):
                    return e.matmul(ps[:, :], kth[:, kc % 4, (kc // 4) * 128:(kc // 4 + 1) * 128],
                                    qh[:, par, qg * 512:(qg + 1) * 512], start=True, stop=True)
                p.emit("pe", mms, waits=[mq] + mks + [psS_free[pi]], inc=["p5"])
                msc = p.mark("p5")
                for ci, mi in zip(cis, mis):
                    dcp = 4 * mi - kc
                    if dcp < -3:
                        continue
                    slot = (kc % 2) * 4 + ci
                    dcol = h * 32 + dcp + 3
                    if dcp <= 0:
                        p.emit("dve", lambda e, ci=ci, ps=ps, dcp=dcp: e.tensor_tensor(
                            sfp[:, :], ps[:, ci * 128:(ci + 1) * 128], mtab[:, -dcp, :], ALU.add),
                            waits=[msc, exp_marks.get(("pv", slot))], inc=["d5"])
                        md = p.mark("d5")
                        p.emit("act", lambda e, slot=slot, dcol=dcol: e.activation(
                            pt_v[:, slot, :], sfp[:, :], AF.Exp, bias=alib[:, dcol:dcol + 1], scale=SCALE),
                            waits=[md], inc=["a5"])
                    else:
                        p.emit("act", lambda e, slot=slot, dcol=dcol, ci=ci, ps=ps: e.activation(
                            pt_v[:, slot, :], ps[:, ci * 128:(ci + 1) * 128], AF.Exp,
                            bias=alib[:, dcol:dcol + 1], scale=SCALE),
                            waits=[msc, exp_marks.get(("pv", slot))], inc=["a5"])
                    mex = p.mark("a5")
                    def mmpv(e, slot=slot, ci=ci, kc=kc, kmax=kmax, mi=mi):
                        return e.matmul(psO[ci][:, 0:129], pt_v[:, slot, :], vh[:, kc % 4, kc // 4, :],
                                        start=(kc == 0),
                                        stop=(kc == min(kmax, 4 * mi + 3)))
                    p.emit("pe", mmpv, waits=[mex] + ([psO_free[ci]] if kc == 0 else []), inc=["p5"])
                    exp_marks[("pv", slot)] = p.mark("p5")
                psS_free[pi] = p.mark("a5")
            for ci, mi in zip(cis, mis):
                last_kc = min(kmax, 4 * mi + 3)
                mpv = exp_marks[("pv", (last_kc % 2) * 4 + ci)]
                p.emit("dve", lambda e, ci=ci: e.tensor_copy(dn[:, ci:ci + 1], psO[ci][:, 128:129]),
                       waits=[mpv], inc=["d5"])
                p.emit("dve", lambda e, ci=ci: e.reciprocal(dn[:, 8 + ci:9 + ci], dn[:, ci:ci + 1]), inc=["d5"])
                md5 = p.mark("d5")
                p.emit("act", lambda e, ci=ci: e.activation(
                    stile[:, ci * 128:(ci + 1) * 128], psO[ci][:, 0:128], AF.Copy,
                    bias=0.0, scale=dn[:, 8 + ci:9 + ci]),
                    waits=[md5, stile_free], inc=["a5"])
                psO_free[ci] = p.mark("a5")
            mev = p.mark("a5")
            stile_free = p.dma("sp", o_dd[qg * 512: (qg + 1) * 512, h * 128:(h + 1) * 128]
                               .rearrange("(a p) d -> p a d", p=128),
                               stile[:, :].rearrange("p (a d) -> p a d", d=128),
                               waits=[mev], sem="o_st")
        qh_free[par] = p.mark("p5")
        kv_free = p.mark("p5")
    p.barrier("b5")

    wl_free2 = None
    wr_free2 = None
    evst_free = None
    for le in range(2):
        egath = None
        for st in range(8):
            def ggi(e, le=le, st=st):
                return e.indirect_dma_start(gi_t[:, :], None, g_dd,
                                            bass.IndirectOffsetOnAxis(eix_sb[:, le * 8 + st: le * 8 + st + 1], 0),
                                            bounds_check=regs["b16k"], oob_is_err=False)
            p.emit("pool", ggi, waits=[("sc4", p.cnt("sc4")), egath, m_eix], inc=[("g6", 16)])
            p.emit("pool", lambda e: e.memset(sx[:, :], 0.0), waits=[egath, "ln_d"], inc=["k6"])
            mz = p.mark("k6")
            def gat(e):
                return e.indirect_dma_start(sx[:, :], None, x_d,
                                            bass.IndirectOffsetOnAxis(gi_t[:, 0:1], 0),
                                            bounds_check=regs["b8k"], oob_is_err=False)
            p.emit("pool", gat, waits=[("g6", p.cnt("g6")), mz], inc=[("g6", 16)])
            mgat = p.mark("g6")
            mhi = emit_ln(sx[:, :], mgat)
            egath = p.dma("sp", e_dd[st * 128:(st + 1) * 128, :], sbf[:, :], waits=[mhi], sem="sbf_fr")
        for sc in range(2):
            et_ld = None
            for fc in range(16):
                et_ld = p.dma("act", big16[:, fc, :], e_dd[sc * 512:(sc + 1) * 512, fc * 128:(fc + 1) * 128],
                              waits=[("sbf_fr", p.cnt("sbf_fr")), ("p6", p.cnt("p6"))],
                              sem="tp_dma", transpose=True)
            for ic in range(11):
                mwg = p.dma("pool", wlhs[:, 0, :, :],
                            wg_d[le, :, ic * 128:(ic + 1) * 128].rearrange("(a p) i -> p a i", p=128),
                            waits=[wl_free2], sem="w_dma")
                mwu = p.dma("pool", wlhs[:, 1, :, :],
                            wu_d[le, :, ic * 128:(ic + 1) * 128].rearrange("(a p) i -> p a i", p=128),
                            sem="w_dma")
                for fc in range(16):
                    def mmg(e, fc=fc):
                        return e.matmul(psA[:, :], wlhs[:, 0, fc, :], big16[:, fc, :],
                                        start=(fc == 0), stop=(fc == 15))
                    p.emit("pe", mmg, waits=([mwg, et_ld, psS_free[0]] if fc == 0 else []), inc=[])
                    def mmu(e, fc=fc):
                        return e.matmul(psB[:, :], wlhs[:, 1, fc, :], big16[:, fc, :],
                                        start=(fc == 0), stop=(fc == 15))
                    p.emit("pe", mmu, waits=([mwu, psS_free[1]] if fc == 0 else []),
                           inc=["p6"] if fc == 15 else [])
                mgu = p.mark("p6")
                p.emit("act", lambda e: e.activation(stile[:, :], psA[:, :], AF.Silu, bias=zv[:, :], scale=1.0),
                       waits=[mgu, "d6"], inc=["a6"])
                p.emit("dve", lambda e, ic=ic, sc=sc: e.tensor_tensor(
                    at_v[:, ic, sc * 512:(sc + 1) * 512], stile[:, :], psB[:, :], ALU.mult),
                    waits=["a6", ("p6", p.cnt("p6"))], inc=["d6"])
                psS_free[0] = p.mark("a6")
                psS_free[1] = p.mark("d6")
                wl_free2 = p.mark("p6")
        for hg in range(4):
            mwd = p.dma("pool", wrhs[:, 0:11, :],
                        wd_d[le, :, hg * 512:(hg + 1) * 512].rearrange("(a p) c -> p a c", p=128),
                        waits=[wr_free2, ("p6", p.cnt("p6"))], sem="w_dma")
            for st in range(8):
                pi = st % 2
                ps = psA if pi == 0 else psB
                for ic in range(11):
                    def mmd(e, ic=ic, st=st, ps=ps):
                        return e.matmul(ps[:, :], at_v[:, ic, st * 128:(st + 1) * 128], wrhs[:, ic, :],
                                        start=(ic == 0), stop=(ic == 10))
                    p.emit("pe", mmd, waits=([mwd, "d6", psS_free[pi]] if ic == 0 else []),
                           inc=["p6"] if ic == 10 else [])
                mdn6 = p.mark("p6")
                p.emit("act", lambda e, ps=ps: e.activation(stile[:, :], ps[:, :], AF.Copy, bias=0.0, scale=1.0),
                       waits=[mdn6, evst_free], inc=["a6"])
                psS_free[pi] = p.mark("a6")
                evst_free = p.dma("sp", eo_in[le * CAP + st * 128: le * CAP + (st + 1) * 128,
                                              hg * 512:(hg + 1) * 512],
                                  stile[:, :], waits=[psS_free[pi]], sem="eo_st")
            wr_free2 = p.mark("p6")
    p.emit("pool", lambda e: e.collective_compute(
        "AllGather", ALU.bypass, replica_groups=[list(range(NC))],
        ins=[eo_in.tensor.ap().opt()], outs=[eo_all.tensor.ap().opt()]),
        waits=["eo_st"], inc=["cc"])
    cc_eo = p.mark("cc")
    p.barrier("b6")

    for mig in range(2):
        ot_ld = None
        for fc in range(16):
            ot_ld = p.dma("act", big16[:, fc, :], o_dd[mig * 512:(mig + 1) * 512, fc * 128:(fc + 1) * 128],
                          waits=[("o_st", p.cnt("o_st")), ("p7", p.cnt("p7"))], sem="tp_dma", transpose=True)
        for cg in range(4):
            mwo = p.dma("pool", wrhs[:, :, :],
                        wo_d[:, cg * 512:(cg + 1) * 512].rearrange("(a p) c -> p a c", p=128),
                        waits=[("p7", p.cnt("p7"))], sem="w_dma")
            for mi2 in range(4):
                mi = mig * 4 + mi2
                pi = mi2 % 2
                ps = psA if pi == 0 else psB
                def mmbo(e, cg=cg, ps=ps):
                    return e.matmul(ps[:, :], ones_rb[0:1, 0:128], brow[0:1, 3 * C + cg * 512: 3 * C + (cg + 1) * 512],
                                    start=True, stop=False)
                p.emit("pe", mmbo, waits=[mwo, ot_ld, psS_free[pi]], inc=[])
                for fc in range(16):
                    def mmo(e, fc=fc, mi2=mi2, ps=ps):
                        return e.matmul(ps[:, :], big16[:, fc, mi2 * 128:(mi2 + 1) * 128], wrhs[:, fc, :],
                                        start=False, stop=(fc == 15))
                    p.emit("pe", mmo, inc=["p7"] if fc == 15 else [])
                mo7 = p.mark("p7")
                p.emit("dve", lambda e, mi=mi, cg=cg, ps=ps: e.tensor_tensor(
                    out_res[:, mi, cg * 512:(cg + 1) * 512], out_res[:, mi, cg * 512:(cg + 1) * 512],
                    ps[:, :], ALU.add), waits=[mo7], inc=["d7"])
                psS_free[pi] = p.mark("d7")
    p.barrier("b7")

    for ih in range(2):
        for ic in range(11):
            ica = ih * 11 + ic
            mwg = p.dma("pool", wlhs[:, 0, :, :],
                        wsg_d[:, ica * 128:(ica + 1) * 128].rearrange("(a p) i -> p a i", p=128),
                        waits=[wl_free2], sem="w_dma")
            mwu = p.dma("pool", wlhs[:, 1, :, :],
                        wsu_d[:, ica * 128:(ica + 1) * 128].rearrange("(a p) i -> p a i", p=128),
                        sem="w_dma")
            for sc in range(2):
                for fc in range(16):
                    def mmg(e, fc=fc, sc=sc):
                        return e.matmul(psA[:, :], wlhs[:, 0, fc, :], xnT[:, fc, sc * 512:(sc + 1) * 512],
                                        start=(fc == 0), stop=(fc == 15))
                    p.emit("pe", mmg, waits=([mwg, psS_free[0]] if fc == 0 else []), inc=[])
                    def mmu(e, fc=fc, sc=sc):
                        return e.matmul(psB[:, :], wlhs[:, 1, fc, :], xnT[:, fc, sc * 512:(sc + 1) * 512],
                                        start=(fc == 0), stop=(fc == 15))
                    p.emit("pe", mmu, waits=([mwu, psS_free[1]] if fc == 0 else []),
                           inc=["p8"] if fc == 15 else [])
                mgu = p.mark("p8")
                p.emit("act", lambda e: e.activation(stile[:, :], psA[:, :], AF.Silu, bias=zv[:, :], scale=1.0),
                       waits=[mgu, "d8"], inc=["a8"])
                p.emit("dve", lambda e, ic=ic, sc=sc: e.tensor_tensor(
                    at_v[:, ic, sc * 512:(sc + 1) * 512], stile[:, :], psB[:, :], ALU.mult),
                    waits=["a8", ("p8", p.cnt("p8"))], inc=["d8"])
                psS_free[0] = p.mark("a8")
                psS_free[1] = p.mark("d8")
            wl_free2 = p.mark("p8")
        for hg in range(4):
            mwd = p.dma("pool", wrhs[:, 0:11, :],
                        wsd_d[ih * 11 * 128:(ih + 1) * 11 * 128, hg * 512:(hg + 1) * 512]
                        .rearrange("(a p) c -> p a c", p=128),
                        waits=[wr_free2, ("p8", p.cnt("p8"))], sem="w_dma")
            for st in range(8):
                pi = st % 2
                ps = psA if pi == 0 else psB
                for ic in range(11):
                    def mmd(e, ic=ic, st=st, ps=ps):
                        return e.matmul(ps[:, :], at_v[:, ic, st * 128:(st + 1) * 128], wrhs[:, ic, :],
                                        start=(ic == 0), stop=(ic == 10))
                    p.emit("pe", mmd, waits=([mwd, "d8", psS_free[pi]] if ic == 0 else []),
                           inc=["p8"] if ic == 10 else [])
                mdn8 = p.mark("p8")
                p.emit("dve", lambda e, st=st, hg=hg, ps=ps: e.tensor_tensor(
                    out_res[:, st, hg * 512:(hg + 1) * 512], out_res[:, st, hg * 512:(hg + 1) * 512],
                    ps[:, :], ALU.add), waits=[mdn8], inc=["d8"])
                psS_free[pi] = p.mark("d8")
            wr_free2 = p.mark("p8")
    p.barrier("b8")

    gath_free = None
    for m in range(8):
        def gci(e, m=m):
            return e.indirect_dma_start(ci2[:, :], None, ci_dd,
                                        bass.IndirectOffsetOnAxis(tok_sb[:, m:m + 1], 0),
                                        bounds_check=regs["b8k"], oob_is_err=False)
        def gcw(e, m=m):
            return e.indirect_dma_start(cw2[:, :], None, cw_dd,
                                        bass.IndirectOffsetOnAxis(tok_sb[:, m:m + 1], 0),
                                        bounds_check=regs["b8k"], oob_is_err=False)
        p.emit("pool", gci, waits=["ct_w", gath_free, m_tok], inc=[("g9", 16)])
        p.emit("pool", gcw, inc=[("g9", 16)])
        mci = p.mark("g9")
        for kk in range(2):
            p.emit("dve", lambda e, kk=kk: e.tensor_copy(su32[:, 0:1], ci2[:, kk:kk + 1]),
                   waits=[mci, ("g9", p.cnt("g9"))], inc=["d9"])
            mi9 = p.mark("d9")
            def gat9(e):
                return e.indirect_dma_start(sbf[:, :], None, eo_all,
                                            bass.IndirectOffsetOnAxis(su32[:, 0:1], 0),
                                            bounds_check=regs["b16k"], oob_is_err=False)
            p.emit("pool", gat9, waits=[mi9, cc_eo, ("d9", p.cnt("d9"))], inc=[("g9", 16)])
            mg9 = p.mark("g9")
            p.emit("dve", lambda e, m=m, kk=kk: e.scalar_tensor_tensor(
                out_res[:, m, :], sbf[:, :], cw2[:, kk:kk + 1], out_res[:, m, :], ALU.mult, ALU.add),
                waits=[mg9], inc=["d9"])
            gath_free = p.mark("d9")
        p.dma("sp", y_d[m * 128:(m + 1) * 128, :], out_res[:, m, :], waits=[("d9", p.cnt("d9"))], sem="y_st")
    p.emit("sp", lambda e: e.nop(), waits=[("y_st", p.cnt("y_st"))])
    p.emit("pool", lambda e: e.nop(), waits=[("y_st", p.cnt("y_st"))])

    p.build_blocks()
    es.close()
    return nc


_CACHE = {}


def kernel(**inputs):
    xs = np.ascontiguousarray(np.asarray(inputs["L0_attn_x"], np.float32).reshape(T_ALL, C))
    shared = {
        "x": xs,
        "wq": np.ascontiguousarray(inputs["L0_attn_Wq"], np.float32),
        "wk": np.ascontiguousarray(inputs["L0_attn_Wk"], np.float32),
        "wv": np.ascontiguousarray(inputs["L0_attn_Wv"], np.float32),
        "wo": np.ascontiguousarray(inputs["L0_attn_Wo"], np.float32),
        "bq": np.ascontiguousarray(inputs["L0_attn_bq"], np.float32).reshape(1, C),
        "bk": np.ascontiguousarray(inputs["L0_attn_bk"], np.float32).reshape(1, C),
        "bv": np.ascontiguousarray(inputs["L0_attn_bv"], np.float32).reshape(1, C),
        "bo": np.ascontiguousarray(inputs["L0_attn_bo"], np.float32).reshape(1, C),
        "gw": np.ascontiguousarray(inputs["L0_ffn_gate_w"], np.float32),
        "wsg": np.ascontiguousarray(inputs["L0_ffn_ws_gate"], np.float32),
        "wsu": np.ascontiguousarray(inputs["L0_ffn_ws_up"], np.float32),
        "wsd": np.ascontiguousarray(inputs["L0_ffn_ws_down"], np.float32),
    }
    in_maps = []
    own_all = []
    for core in range(NC):
        b, r = core // 4, core % 4
        own = np.array([b * 32 + r + 4 * m for m in range(8)], np.int64)
        own_all.append(own)
        tok = (own[None, :] * 128 + np.arange(128)[:, None]).astype(np.uint32)
        eix = (2 * core * CAP + (np.arange(16)[None, :] * 128 + np.arange(128)[:, None])).astype(np.uint32)
        rv = np.full((128, 1), 128.0 * r, np.float32)
        m = dict(shared)
        xo_rows = (own[:, None] * 128 + np.arange(128)[None, :]).reshape(1024).astype(np.int64)
        m["xo"] = np.ascontiguousarray(xs[xo_rows])
        m["tok"] = tok
        m["eix"] = eix
        m["rv"] = rv
        m["wg"] = np.ascontiguousarray(inputs["L0_ffn_w_gate"][2 * core:2 * core + 2], np.float32)
        m["wu"] = np.ascontiguousarray(inputs["L0_ffn_w_up"][2 * core:2 * core + 2], np.float32)
        m["wd"] = np.ascontiguousarray(inputs["L0_ffn_w_down"][2 * core:2 * core + 2], np.float32)
        in_maps.append(m)
    if "nc" not in _CACHE:
        _CACHE["nc"] = build_core()
    res = run_bass_kernel_spmd(_CACHE["nc"], in_maps, core_ids=list(range(NC)))
    out = np.empty((T_ALL, C), np.float32)
    for core in range(NC):
        y = np.asarray(res.results[core]["y"], np.float32)
        for mi in range(8):
            g = own_all[core][mi]
            out[g * 128:(g + 1) * 128] = y[mi * 128:(mi + 1) * 128]
    return out.reshape(2, 4096, C)


# version 08
# speedup vs baseline: 1.3310x; 1.3310x over previous
import numpy as np
import concourse.bass as bass
import concourse.mybir as mybir
from concourse.bass_utils import run_bass_kernel_spmd
from contextlib import ExitStack

F32 = mybir.dt.float32
BF16 = mybir.dt.bfloat16
U32 = mybir.dt.uint32
AF = mybir.ActivationFunctionType
ALU = mybir.AluOpType
AX = mybir.AxisListType

T_ALL = 8192
C = 2048
E = 16
CAP = 1024
I_DIM = 1408
NC = 8
EPS = 1e-5
SCALE = float(1.0 / np.sqrt(128.0))
LN2H = -float(np.log(2.0) / 2.0)
BIG = 1.0e9
QUEUES = ("pe", "act", "dve", "pool", "sp")
SERIAL_QUEUES = ("dve", "act", "pool")
ENG_ATTR = {"pe": "tensor", "act": "scalar", "dve": "vector", "pool": "gpsimd", "sp": "sync"}


class Prog:
    def __init__(self, nc):
        self.nc = nc
        self.q = {k: [] for k in QUEUES}
        self._sems = {}
        self._prev = {}

    def sem(self, name):
        if name not in self._sems:
            h = self.nc.semaphore(name).__enter__()
            self._sems[name] = [h, 0]
        return self._sems[name]

    def cnt(self, name):
        return self.sem(name)[1]

    def mark(self, name):
        h, c = self.sem(name)
        return (name, c)

    def emit(self, queue, fn, waits=(), inc=()):
        wlist = []
        for w in waits:
            if w is None:
                continue
            if isinstance(w, str):
                h, c = self.sem(w)
                if c > 0:
                    wlist.append((h, c))
            else:
                name, val = w
                if val > 0:
                    h, _ = self.sem(name)
                    wlist.append((h, val))
        if queue in SERIAL_QUEUES and queue in self._prev:
            wlist.append(self._prev[queue])
        ilist = []
        inc = list(inc)
        if queue in SERIAL_QUEUES and not inc:
            inc = [queue + "_ser"]
        for i in inc:
            name, n = (i, 1) if isinstance(i, str) else i
            rec = self.sem(name)
            rec[1] += n
            ilist.append((rec[0], n))
        if queue in SERIAL_QUEUES and ilist:
            h0 = ilist[0][0]
            for name, rec in self._sems.items():
                if rec[0] is h0:
                    self._prev[queue] = (h0, rec[1])
                    break
        self.q[queue].append((fn, wlist, ilist))

    def dma(self, queue, out, in_, waits=(), sem=None, **kw):
        name = sem or (queue + "_dma")
        self.emit(queue, lambda e: e.dma_start(out=out, in_=in_, **kw),
                  waits=waits, inc=[(name, 16)])
        return self.mark(name)

    def barrier(self, name="bar"):
        allw = [(nm, rec[1]) for nm, rec in self._sems.items() if rec[1] > 0 and nm != name]
        for qn in QUEUES:
            self.emit(qn, lambda e: e.nop(), waits=allw, inc=[name])
        total = self.sem(name)[1]
        for qn in QUEUES:
            self.emit(qn, lambda e: e.nop(), waits=[(name, total)])

    def build_blocks(self):
        nc = self.nc
        with nc.Block() as blk:
            for qn in QUEUES:
                ops = self.q[qn]
                def mk(ops):
                    def body(eng):
                        for fn, waits, incs in ops:
                            for h, v in waits:
                                eng.wait_ge(h, v)
                            inst = fn(eng)
                            for j, (h, n) in enumerate(incs):
                                if inst is None or j > 0:
                                    eng.nop().then_inc(h, n)
                                else:
                                    inst.then_inc(h, n)
                    return body
                getattr(blk, ENG_ATTR[qn])(mk(ops))


def chunk_owner(g):
    gb, j = divmod(g, 32)
    return gb * 4 + j % 4, j // 4


def build_core():
    nc = bass.Bass("TRN2", target_bir_lowering=False)
    def dram(n, s, d=F32, k=None):
        t = nc.dram_tensor(n, s, d, kind=k) if k else nc.dram_tensor(n, s, d)
        return t.ap()
    x_d = dram("x", [T_ALL, C], F32, "ExternalInput")
    xo_d = dram("xo", [1024, C], F32, "ExternalInput")
    tok_d = dram("tok", [128, 8], U32, "ExternalInput")
    eix_d = dram("eix", [128, 16], U32, "ExternalInput")
    rv_d = dram("rv", [128, 1], F32, "ExternalInput")
    wq_d = dram("wq", [C, C], F32, "ExternalInput")
    wk_d = dram("wk", [C, C], F32, "ExternalInput")
    wv_d = dram("wv", [C, C], F32, "ExternalInput")
    wo_d = dram("wo", [C, C], F32, "ExternalInput")
    bq_d = dram("bq", [1, C], F32, "ExternalInput")
    bk_d = dram("bk", [1, C], F32, "ExternalInput")
    bv_d = dram("bv", [1, C], F32, "ExternalInput")
    bo_d = dram("bo", [1, C], F32, "ExternalInput")
    gw_d = dram("gw", [E, C], F32, "ExternalInput")
    wg_d = dram("wg", [2, C, I_DIM], F32, "ExternalInput")
    wu_d = dram("wu", [2, C, I_DIM], F32, "ExternalInput")
    wd_d = dram("wd", [2, I_DIM, C], F32, "ExternalInput")
    wsg_d = dram("wsg", [C, 2816], F32, "ExternalInput")
    wsu_d = dram("wsu", [C, 2816], F32, "ExternalInput")
    wsd_d = dram("wsd", [2816, C], F32, "ExternalInput")
    y_d = dram("y", [1024, C], F32, "ExternalOutput")

    xn_d = dram("xn_dd", [1024, C], BF16)
    lo_d = dram("lo_dd", [1024, C], BF16)
    r_in = dram("r_in", [1024, 4], F32)
    r_all = dram("r_all", [T_ALL, 4], F32)
    q_dd = dram("q_dd", [C, 1024], BF16)
    kvk_in = dram("kvk_in", [C, 1024], BF16)
    kvv_in = dram("kvv_in", [1024, C], BF16)
    kt_all = dram("kt_all", [4 * C, 1024], BF16)
    v_all = dram("v_all", [4096, C], BF16)
    o_dd = dram("o_dd", [1024, C], BF16)
    g_dd = dram("g_dd", [E * CAP, 4], U32)
    ci_dd = dram("ci_dd", [T_ALL, 2], U32)
    cw_dd = dram("cw_dd", [T_ALL, 2], F32)
    e_dd = dram("e_dd", [CAP, C], BF16)
    eo_in = dram("eo_in", [2 * CAP, C], BF16)
    eo_all = dram("eo_all", [NC * 2 * CAP, C], BF16)

    es = ExitStack()
    T = lambda n, s, d: es.enter_context(nc.sbuf_tensor(n, s, d))
    PT = lambda n: es.enter_context(nc.psum_tensor(n, [128, 512], F32))

    out_res = T("out_res", [128, 8, C], F32)
    xnT = T("xnT", [128, 16, 1024], BF16)
    a_at = T("a_at", [128, 11264], BF16)
    at_v = a_at[:, 0:11264].rearrange("p (a t) -> p a t", a=11)
    brow3 = a_at.tensor.bitcast(BF16) if False else a_at[0:1, 0:6144]
    kth = a_at[:, 0:4096].rearrange("p (a t) -> p a t", a=4)
    vh = a_at[:, 4096:8224].rearrange("p (a m t) -> p a m t", a=4, m=8)
    pt_v = a_at[:, 8224:9248].rearrange("p (a t) -> p a t", a=8)
    big16 = T("big16", [128, 16, 512], BF16)
    wrhs = T("wrhs", [128, 2, 16, 512], BF16)
    wlhs = T("wlhs", [128, 4, 16, 128], BF16)
    sx = T("sx", [128, C], F32)
    sbf = T("sbf", [128, C], BF16)
    f32s = T("f32s", [128, 128], F32)
    sfp = f32s[:, 0:128]
    qh = T("qh", [128, 1, 1024], BF16)
    brow_o = T("brow_o", [1, C], BF16)
    alib = T("alib", [128, 512], F32)
    mtab = T("mtab", [128, 4, 128], BF16)
    ltri = T("ltri", [128, 128], F32)
    e16 = T("e16", [128, 16], F32)
    e16b = T("e16b", [128, 16], F32)
    iop = T("iop", [128, 1], F32)
    rvp = T("rvp", [128, 1], F32)
    idf = T("idf", [128, 16], F32)
    ones_rb = T("ones_rb", [1, 512], BF16)
    ones_cf = T("ones_cf", [128, 1], F32)
    gthi = a_at[:, 6144:6400].rearrange("p (a t) -> p a t", a=16)
    gtlo = a_at[:, 6400:6656].rearrange("p (a t) -> p a t", a=16)
    s16a = f32s[:, 0:16]
    s16b = f32s[:, 16:32]
    s16c = f32s[:, 32:48]
    w8 = f32s[:, 48:56]
    rt4 = f32s[:, 56:64].rearrange("p (a t) -> p a t", a=2)
    srun = f32s[0:1, 64:80]
    cisb = T("cisb", [128, 64, 2], U32)
    cwsb = T("cwsb", [128, 64, 2], F32)
    tok_sb = T("tok_sb", [128, 8], U32)
    eix_sb = T("eix_sb", [128, 16], U32)
    ci2 = T("ci2", [128, 2], U32)
    cw2 = T("cw2", [128, 2], F32)
    su32 = T("su32", [128, 4], U32)
    su32b = T("su32b", [128, 4], U32)
    gi_t = T("gi_t", [128, 4], U32)
    stile = T("stile", [128, 512], BF16)
    vec1 = T("vec1", [128, 1], F32)
    vec2 = T("vec2", [128, 1], F32)
    vec3 = T("vec3", [128, 1], F32)
    vec4 = T("vec4", [128, 1], F32)
    dvec = T("dvec", [128, 2], F32)
    dn = T("dn", [128, 16], F32)
    vals = T("vals", [128, 4], U32)
    zv = T("zv", [128, 1], F32)
    epsv = T("epsv", [128, 1], F32)

    psA = PT("psA"); psB = PT("psB")
    psO = [PT("psO%d" % i) for i in range(4)]
    psX = PT("psX"); psT = PT("psT")

    p = Prog(nc)
    def warm(func):
        p.emit("act", lambda e, func=func: e.activation(dn[:, 14:15], epsv[:, :], func,
                                                        bias=zv[:, :], scale=1.0), inc=["warmx"])
    regs = {}
    def setup_regs(e):
        r1 = e.alloc_register()
        e.reg_mov(r1, E * CAP - 1)
        regs["b16k"] = r1
        r2 = e.alloc_register()
        e.reg_mov(r2, T_ALL - 1)
        regs["b8k"] = r2
        return None
    p.emit("pool", setup_regs)

    p.emit("pool", lambda e: e.iota(e16[:, :], [[1, 16]], base=0, channel_multiplier=0,
                                    allow_small_or_imprecise_dtypes=True), inc=["k"])
    p.emit("pool", lambda e: e.iota(iop[:, :], [[1, 1]], base=0, channel_multiplier=1,
                                    allow_small_or_imprecise_dtypes=True), inc=["k"])
    p.emit("pool", lambda e: e.memset(ones_rb[:, :], 1.0), inc=["k"])
    p.emit("pool", lambda e: e.memset(ones_cf[:, :], 1.0), inc=["k"])
    p.emit("pool", lambda e: e.memset(srun[:, :], 0.0), inc=["k"])
    p.emit("pool", lambda e: e.memset(zv[:, :], 0.0), inc=["k"])
    p.emit("pool", lambda e: e.memset(epsv[:, :], EPS), inc=["k"])
    p.emit("pool", lambda e: e.memset(cisb[:, :, :], 0xFFFFFFFF), inc=["k"])
    p.emit("pool", lambda e: e.iota(sx[:, 0:128], [[1, 128]], base=0, channel_multiplier=0,
                                    allow_small_or_imprecise_dtypes=True), inc=["k"])
    mrv = p.dma("sp", rvp[:, :], rv_d)
    m_tok = p.dma("sp", tok_sb[:, :], tok_d)
    m_eix = p.dma("sp", eix_sb[:, :], eix_d)
    kc0 = p.mark("k")
    p.emit("dve", lambda e: e.tensor_scalar(e16b[:, :], e16[:, :], 65536.0, None, ALU.add),
           waits=[kc0], inc=["kd"])
    p.emit("dve", lambda e: e.tensor_scalar(idf[:, :], e16[:, :], iop[:, :], None, ALU.is_equal), inc=["kd"])
    p.emit("dve", lambda e: e.tensor_scalar(ltri[:, :], sx[:, 0:128], iop[:, :], None, ALU.is_gt), inc=["kd"])
    for md in range(4):
        p.emit("dve", lambda e, md=md: e.tensor_scalar(vec1[:, :], iop[:, :], rvp[:, :], float(128 * md),
                                                       ALU.subtract, ALU.add), waits=[mrv], inc=["kd"])
        p.emit("dve", lambda e, md=md: e.tensor_scalar(mtab[:, md, :], sx[:, 0:128], vec1[:, :], -BIG,
                                                       ALU.is_lt, ALU.mult), inc=["kd"])
    mdc = p.mark("kd")
    p.emit("pool", lambda e: e.iota(sx[:, 128:640], [[0, 16], [1, 32]], base=-3, channel_multiplier=0,
                                    allow_small_or_imprecise_dtypes=True), waits=[mdc], inc=["k"])
    p.emit("pool", lambda e: e.iota(sx[:, 640:1152], [[1, 16], [0, 32]], base=1, channel_multiplier=0,
                                    allow_small_or_imprecise_dtypes=True), inc=["k"])
    p.emit("pool", lambda e: e.iota(sx[:, 1152:1664], [[0, 512]], base=0, channel_multiplier=1,
                                    allow_small_or_imprecise_dtypes=True), inc=["k"])
    kc1 = p.mark("k")
    p.emit("act", lambda e: e.activation(sx[:, 640:1152], sx[:, 640:1152], AF.Exp, bias=zv[:, :], scale=LN2H),
           waits=[kc1], inc=["ka"])
    p.emit("dve", lambda e: e.tensor_scalar(sx[:, 1152:1664], sx[:, 1152:1664], rvp[:, :], -64.0,
                                            ALU.subtract, ALU.add), waits=[kc1], inc=["kd"])
    p.emit("dve", lambda e: e.scalar_tensor_tensor(alib[:, :], sx[:, 128:640], -128.0, sx[:, 1152:1664],
                                                   ALU.mult, ALU.add), inc=["kd"])
    p.emit("dve", lambda e: e.tensor_tensor(alib[:, :], alib[:, :], sx[:, 640:1152], ALU.mult),
           waits=["ka"], inc=["kd"])
    mbq = p.dma("pool", brow3[0:1, 0:C], bq_d)
    p.dma("pool", brow3[0:1, C:2 * C], bk_d)
    p.dma("pool", brow3[0:1, 2 * C:3 * C], bv_d)
    mbr = p.dma("pool", brow_o[0:1, :], bo_d)
    mg_ = p.dma("sp", out_res[0:16, 0, :], gw_d)
    idm = p.mark("kd")
    for fc in range(16):
        def trg(e, fc=fc):
            return e.transpose(psB[0:128, 0:16], out_res[0:16, 0, fc * 128:(fc + 1) * 128], idf[0:16, 0:16])
        p.emit("pe", trg, waits=[mg_, idm, ("kgt", 2 * fc)], inc=["kp"])
        p.emit("act", lambda e, fc=fc: e.activation(gthi[:, fc, :], psB[:, 0:16], AF.Copy, bias=0.0, scale=1.0),
               waits=[("kp", fc + 1)], inc=["kgt"])
        p.emit("dve", lambda e, fc=fc: e.scalar_tensor_tensor(gtlo[:, fc, :], gthi[:, fc, :], -1.0, psB[:, 0:16],
                                                              ALU.mult, ALU.add),
               waits=[("kgt", 2 * fc + 1)], inc=["kgt"])
    mzp = p.mark("k")
    for _q in range(4):
        p.dma("pool", g_dd[_q * 4096:(_q + 1) * 4096, :].rearrange("(p a) f -> p (a f)", p=128), cisb[:, :, :].rearrange("p a k -> p (a k)"),
              waits=[mzp], sem="g_init")
    p.barrier("b0")

    def emit_ln(src_ap, waitm):
        p.emit("dve", lambda e: e.tensor_reduce(vec1[:, :], src_ap, axis=AX.X, op=ALU.add),
               waits=[waitm], inc=["ln_d"])
        p.emit("act", lambda e: e.activation(psX[:, :], src_ap[:, 0:512], AF.Square, bias=zv[:, :], scale=1.0,
                                             accum_out=vec2[:, :]), waits=[waitm, "d4run", "d4"], inc=["ln_a"])
        p.emit("act", lambda e: e.activation(psX[:, :], src_ap[:, 512:1024], AF.Square, bias=zv[:, :], scale=1.0,
                                             accum_out=vec3[:, :]), inc=["ln_a"])
        p.emit("act", lambda e: e.activation(psT[:, :], src_ap[:, 1024:1536], AF.Square, bias=zv[:, :], scale=1.0,
                                             accum_out=vec4[:, :]), inc=["ln_a"])
        p.emit("act", lambda e: e.activation(psT[:, :], src_ap[:, 1536:2048], AF.Square, bias=zv[:, :], scale=1.0,
                                             accum_out=dvec[:, 0:1]), inc=["ln_a"])
        p.emit("dve", lambda e: e.tensor_scalar(vec1[:, :], vec1[:, :], 1.0 / C, None, ALU.mult), inc=["ln_d"])
        am = p.mark("ln_a")
        p.emit("dve", lambda e: e.tensor_tensor(vec2[:, :], vec2[:, :], vec3[:, :], ALU.add), waits=[am], inc=["ln_d"])
        p.emit("dve", lambda e: e.tensor_tensor(vec4[:, :], vec4[:, :], dvec[:, 0:1], ALU.add), inc=["ln_d"])
        p.emit("dve", lambda e: e.tensor_tensor(vec2[:, :], vec2[:, :], vec4[:, :], ALU.add), inc=["ln_d"])
        p.emit("dve", lambda e: e.tensor_tensor(vec3[:, :], vec1[:, :], vec1[:, :], ALU.mult), inc=["ln_d"])
        p.emit("dve", lambda e: e.scalar_tensor_tensor(vec3[:, :], vec2[:, :], 1.0 / C, vec3[:, :],
                                                       ALU.mult, ALU.subtract), inc=["ln_d"])
        p.emit("act", lambda e: e.activation(vec3[:, :], vec3[:, :], AF.Sqrt, bias=epsv[:, :], scale=1.0),
               waits=["ln_d"], inc=["ln_a"])
        p.emit("dve", lambda e: e.reciprocal(vec4[:, :], vec3[:, :]), waits=["ln_a"], inc=["ln_d"])
        p.emit("dve", lambda e: e.tensor_scalar(sbf[:, :], src_ap, vec1[:, :], vec4[:, :],
                                                ALU.subtract, ALU.mult), waits=["sbf_fr", "qkv_st"], inc=["ln_d"])
        return p.mark("ln_d")

    warm(AF.Sqrt)
    sx_free = None
    for m in range(8):
        mload = p.dma("sp", sx[:, :], xo_d[m * 128:(m + 1) * 128, :], waits=[sx_free])
        p.emit("dve", lambda e, m=m: e.tensor_copy(out_res[:, m, :], sx[:, :]), waits=[mload], inc=["ln_d"])
        mhi = emit_ln(sx[:, :], mload)
        mw1 = p.dma("sp", xn_d[m * 128:(m + 1) * 128, :], sbf[:, :], waits=[mhi], sem="sbf_fr")
        p.emit("dve", lambda e: e.tensor_scalar(sx[:, :], sx[:, :], vec1[:, :], vec4[:, :],
                                                ALU.subtract, ALU.mult), inc=["ln_d"])
        p.emit("dve", lambda e: e.tensor_tensor(sx[:, :], sx[:, :], sbf[:, :], ALU.subtract),
               waits=[mw1], inc=["ln_d"])
        p.emit("act", lambda e: e.activation(sbf[:, :], sx[:, :], AF.Copy, bias=0.0, scale=1.0),
               waits=["ln_d"], inc=["ln_a"])
        mlo = p.mark("ln_a")
        mw2 = p.dma("sp", lo_d[m * 128:(m + 1) * 128, :], sbf[:, :], waits=[mlo], sem="sbf_fr")
        sx_free = p.mark("ln_d")

    m_st1 = p.mark("sbf_fr")
    for fc in range(16):
        p.dma("act", xnT[:, fc, :], xn_d[0:1024, fc * 128:(fc + 1) * 128], waits=[m_st1], sem="tp_dma", transpose=True)
        p.dma("act", big16[:, fc, :], lo_d[0:512, fc * 128:(fc + 1) * 128], sem="tp_dma", transpose=True)
        p.dma("act", wrhs[:, 0, fc, :], lo_d[512:1024, fc * 128:(fc + 1) * 128], sem="tp_dma", transpose=True)
    mtp = p.mark("tp_dma")
    warm(AF.Exp)
    for m in range(8):
        loT = big16 if m < 4 else wrhs
        mloc = m % 4
        first = True
        for fc in range(16):
            for which in range(3):
                def mml(e, which=which, fc=fc, m=m, mloc=mloc, loT=loT, first=first):
                    gt = gthi if which != 2 else gtlo
                    lh = (xnT[:, fc, m * 128:(m + 1) * 128] if which != 1 else
                          (big16[:, fc, mloc * 128:(mloc + 1) * 128] if m < 4 else wrhs[:, 0, fc, mloc * 128:(mloc + 1) * 128]))
                    return e.matmul(psB[:, 0:16], lh, gt[:, fc, :], start=first, stop=False)
                p.emit("pe", mml, waits=[mtp, "rt_d", ("kgt", 32), "kd"] if first else [],
                       inc=["lg_p"] if (fc == 15 and which == 2) else [])
                first = False
        p.emit("dve", lambda e: e.tensor_copy(s16a[:, :], psB[:, 0:16]), waits=["lg_p"], inc=["rt_d"])
        p.emit("dve", lambda e: e.tensor_reduce(vec1[:, :], s16a[:, :], axis=AX.X, op=ALU.max, negate=True),
               inc=["rt_d"])
        p.emit("act", lambda e: e.activation(s16b[:, :], s16a[:, :], AF.Exp, bias=vec1[:, :], scale=1.0),
               waits=["rt_d"], inc=["rt_a"])
        p.emit("dve", lambda e: e.tensor_reduce(vec2[:, :], s16b[:, :], axis=AX.X, op=ALU.add),
               waits=["rt_a"], inc=["rt_d"])
        p.emit("dve", lambda e: e.reciprocal(vec3[:, :], vec2[:, :]), inc=["rt_d"])
        p.emit("dve", lambda e: e.tensor_scalar(s16a[:, :], s16b[:, :], vec3[:, :], None, ALU.mult), inc=["rt_d"])
        p.emit("dve", lambda e: e.max(out=w8[:, :], in_=s16a[:, :]), inc=["rt_d"])
        mm_ = m % 2
        p.emit("dve", lambda e: e.tensor_scalar(s16b[:, :], s16a[:, :], w8[:, 0:1], None, ALU.is_equal), inc=["rt_d"])
        p.emit("dve", lambda e: e.scalar_tensor_tensor(s16c[:, :], s16b[:, :], -65536.0, e16b[:, :], ALU.mult, ALU.add),
               inc=["rt_d"])
        p.emit("dve", lambda e: e.tensor_reduce(vec1[:, :], s16c[:, :], axis=AX.X, op=ALU.min), inc=["rt_d"])
        p.emit("dve", lambda e, mm_=mm_: e.tensor_copy(rt4[:, mm_, 0:1], vec1[:, :]),
               inc=["rt_d"])
        p.emit("dve", lambda e, mm_=mm_: e.tensor_scalar(s16b[:, :], e16[:, :], rt4[:, mm_, 0:1], None, ALU.is_equal),
               inc=["rt_d"])
        p.emit("dve", lambda e: e.scalar_tensor_tensor(s16c[:, :], s16b[:, :], -BIG, s16a[:, :], ALU.mult, ALU.add),
               inc=["rt_d"])
        p.emit("dve", lambda e, mm_=mm_: e.tensor_reduce(rt4[:, mm_, 3:4], s16c[:, :], axis=AX.X, op=ALU.max),
               inc=["rt_d"])
        p.emit("dve", lambda e, mm_=mm_: e.tensor_copy(rt4[:, mm_, 2:3], w8[:, 0:1]), inc=["rt_d"])
        p.emit("dve", lambda e, mm_=mm_: e.tensor_scalar(s16b[:, :], s16c[:, :], rt4[:, mm_, 3:4], None, ALU.is_equal),
               inc=["rt_d"])
        p.emit("dve", lambda e: e.scalar_tensor_tensor(s16c[:, :], s16b[:, :], -65536.0, e16b[:, :], ALU.mult, ALU.add),
               inc=["rt_d"])
        p.emit("dve", lambda e: e.tensor_reduce(vec1[:, :], s16c[:, :], axis=AX.X, op=ALU.min), inc=["rt_d"])
        p.emit("dve", lambda e, mm_=mm_: e.tensor_copy(rt4[:, mm_, 1:2], vec1[:, :]),
               inc=["rt_d"])
        mrt = p.mark("rt_d")
        p.dma("sp", r_in[m * 128:(m + 1) * 128, :], rt4[:, mm_, :], waits=[mrt], sem="rt_w")
    p.emit("pool", lambda e: e.collective_compute(
        "AllGather", ALU.bypass, replica_groups=[list(range(NC))],
        ins=[r_in.tensor.ap().opt()], outs=[r_all.tensor.ap().opt()]),
        waits=["rt_w"], inc=["cc"])
    cc_rt = p.mark("cc")

    ps_free = {0: p.mark("ps_ab"), 1: p.mark("rt_d")}
    wl_free = [None, None]
    for wi, (w_src, brow_i, dst_dram) in enumerate(((wq_d, 0, q_dd), (wk_d, 1, kvk_in))):
        for hdc in range(16):
            slot = hdc % 2
            mw = p.dma("pool", wlhs[:, slot, :, :],
                       w_src[:, hdc * 128:(hdc + 1) * 128].rearrange("(a p) c -> p a c", p=128),
                       waits=[wl_free[slot]], sem="w_dma")
            for tc in range(2):
                pi = tc
                ps = psA if pi == 0 else psB
                def mmb(e, hdc=hdc, brow_i=brow_i, ps=ps):
                    return e.matmul(ps[:, :], brow3[0:1, brow_i * C + hdc * 128: brow_i * C + (hdc + 1) * 128],
                                    ones_rb[0:1, :], start=True, stop=False)
                p.emit("pe", mmb, waits=[mw, mbr, ps_free[pi]], inc=[])
                for fc in range(16):
                    def mmq(e, fc=fc, tc=tc, slot=slot, ps=ps):
                        return e.matmul(ps[:, :], wlhs[:, slot, fc, :], xnT[:, fc, tc * 512:(tc + 1) * 512],
                                        start=False, stop=(fc == 15))
                    p.emit("pe", mmq, inc=["qk_p"] if fc == 15 else [])
                mqk = p.mark("qk_p")
                bslot = tc
                def ev(e, bslot=bslot, ps=ps):
                    return e.activation(sbf[:, bslot * 512:(bslot + 1) * 512], ps[:, :], AF.Copy, bias=0.0, scale=1.0)
                p.emit("act", ev, waits=[mqk, ("qkv_st", max(0, p.cnt("qkv_st") - 16))], inc=["qkv_ev"])
                ps_free[pi] = p.mark("qkv_ev")
                p.dma("sp", dst_dram[hdc * 128:(hdc + 1) * 128, tc * 512:(tc + 1) * 512],
                      sbf[:, bslot * 512:(bslot + 1) * 512], waits=[ps_free[pi]], sem="qkv_st")
            wl_free[slot] = p.mark("qk_p")
    wr3 = [None, None]
    for hg in range(4):
        ws3 = hg % 2
        mwv = p.dma("pool", wrhs[:, ws3, :, :],
                    wv_d[:, hg * 512:(hg + 1) * 512].rearrange("(a p) c -> p a c", p=128),
                    waits=[wr3[ws3], ("lg_p", p.cnt("lg_p"))], sem="w_dma")
        for tchunk in range(8):
            pi = tchunk % 2
            ps = psA if pi == 0 else psB
            def mmbv(e, hg=hg, ps=ps):
                return e.matmul(ps[:, :], ones_rb[0:1, 0:128], brow3[0:1, 2 * C + hg * 512: 2 * C + (hg + 1) * 512],
                                start=True, stop=False)
            p.emit("pe", mmbv, waits=[mwv, ps_free[pi]], inc=[])
            for fc in range(16):
                def mmv(e, fc=fc, tchunk=tchunk, ps=ps, ws3=ws3):
                    return e.matmul(ps[:, :], xnT[:, fc, tchunk * 128:(tchunk + 1) * 128], wrhs[:, ws3, fc, :],
                                    start=False, stop=(fc == 15))
                p.emit("pe", mmv, inc=["v_p"] if fc == 15 else [])
            mv = p.mark("v_p")
            bslot = 2 + pi
            def evv(e, bslot=bslot, ps=ps):
                return e.activation(sbf[:, bslot * 512:(bslot + 1) * 512], ps[:, :], AF.Copy, bias=0.0, scale=1.0)
            p.emit("act", evv, waits=[mv, ("qkv_st", max(0, p.cnt("qkv_st") - 16))], inc=["qkv_ev"])
            ps_free[pi] = p.mark("qkv_ev")
            p.dma("sp", kvv_in[tchunk * 128:(tchunk + 1) * 128, hg * 512:(hg + 1) * 512],
                  sbf[:, bslot * 512:(bslot + 1) * 512], waits=[ps_free[pi]], sem="qkv_st")
        wr3[ws3] = p.mark("v_p")
    p.emit("pool", lambda e: e.collective_compute(
        "AllGather", ALU.bypass, replica_groups=[[0, 1, 2, 3], [4, 5, 6, 7]],
        ins=[kvk_in.tensor.ap().opt()], outs=[kt_all.tensor.ap().opt()]),
        waits=["qkv_st"], inc=["cc"])
    p.emit("pool", lambda e: e.collective_compute(
        "AllGather", ALU.bypass, replica_groups=[[0, 1, 2, 3], [4, 5, 6, 7]],
        ins=[kvv_in.tensor.ap().opt()], outs=[v_all.tensor.ap().opt()]),
        inc=["cc"])
    cc_kv = p.mark("cc")

    p.emit("pool", lambda e: e.memset(sx[:, 0:128], 1.0), waits=["ln_d"], inc=["k5"])
    run_step = [0]
    for g in range(64):
        rk, mg = chunk_owner(g)
        gm = g % 2
        mrt = p.dma("sp", rt4[:, gm, :], r_all[rk * 1024 + mg * 128: rk * 1024 + (mg + 1) * 128, :],
                    waits=[cc_rt, ("d4", p.cnt("d4"))], sem="r_ld")
        p.emit("dve", lambda e, gm=gm: e.tensor_scalar(s16a[:, :], e16[:, :], rt4[:, gm, 0:1], None, ALU.is_equal),
               waits=[mrt], inc=["d4"])
        p.emit("dve", lambda e, gm=gm: e.tensor_scalar(s16b[:, :], e16[:, :], rt4[:, gm, 1:2], None, ALU.is_equal),
               inc=["d4"])
        p.emit("dve", lambda e: e.tensor_tensor(s16c[:, :], s16a[:, :], s16b[:, :], ALU.add), inc=["d4"])
        mB = p.mark("d4")
        def mmbc(e):
            return e.matmul(psX[0:128, 0:16], sx[0:1, 0:128], srun[0:1, :], start=True, stop=False)
        p.emit("pe", mmbc, waits=[mB, ("d4run", run_step[0]), ("k5", 1)], inc=[])
        def mmtri(e):
            return e.matmul(psX[0:128, 0:16], ltri[:, :], s16c[:, :], start=False, stop=True)
        p.emit("pe", mmtri, inc=["p4"])
        def mmtot(e, g=g):
            return e.matmul(psT[0:128, 0:16], sx[:, 0:128], s16c[:, :], start=(g == 0), stop=(g == 63))
        p.emit("pe", mmtot, inc=["p4"])
        mp4 = p.mark("p4")
        p.emit("dve", lambda e: e.tensor_copy(s16c[:, :], psX[:, 0:16]), waits=[mp4], inc=["d4"])
        p.emit("dve", lambda e: e.tensor_copy(srun[0:1, :], psT[0:1, 0:16]), inc=["d4run"])
        run_step[0] += 1
        p.emit("dve", lambda e: e.tensor_tensor(s16a[:, :], s16a[:, :], s16c[:, :], ALU.mult), inc=["d4"])
        p.emit("dve", lambda e: e.tensor_reduce(vec1[:, :], s16a[:, :], axis=AX.X, op=ALU.add), inc=["d4"])
        p.emit("dve", lambda e: e.tensor_tensor(s16b[:, :], s16b[:, :], s16c[:, :], ALU.mult), inc=["d4"])
        p.emit("dve", lambda e: e.tensor_reduce(vec2[:, :], s16b[:, :], axis=AX.X, op=ALU.add), inc=["d4"])
        for kk, posv, dst in ((0, vec1, su32), (1, vec2, su32b)):
            p.emit("dve", lambda e, posv=posv: e.tensor_scalar(vec3[:, :], posv[:, :], float(CAP), 1.0e6,
                                                               ALU.is_ge, ALU.mult),
                   waits=[("sc4", p.cnt("sc4"))], inc=["d4"])
            p.emit("dve", lambda e, gm=gm, kk=kk, posv=posv: e.scalar_tensor_tensor(
                vec4[:, :], rt4[:, gm, kk:kk + 1], float(CAP), posv[:, :], ALU.mult, ALU.add), inc=["d4"])
            p.emit("dve", lambda e: e.tensor_tensor(vec4[:, :], vec4[:, :], vec3[:, :], ALU.add), inc=["d4"])
            p.emit("dve", lambda e, dst=dst: e.tensor_copy(dst[:, 0:1], vec4[:, :]), inc=["d4"])
        mid = p.mark("d4")
        p.emit("pool", lambda e, g=g: e.iota(vals[:, :], [[0, 4]], base=g * 128, channel_multiplier=1),
               waits=[("sc4", p.cnt("sc4"))], inc=["k4"])
        mvv = p.mark("k4")
        def sc1(e):
            return e.indirect_dma_start(g_dd, bass.IndirectOffsetOnAxis(su32[:, 0:1], 0), vals[:, :], None,
                                        bounds_check=regs["b16k"], oob_is_err=False)
        def sc2(e):
            return e.indirect_dma_start(g_dd, bass.IndirectOffsetOnAxis(su32b[:, 0:1], 0), vals[:, :], None,
                                        bounds_check=regs["b16k"], oob_is_err=False)
        p.emit("pool", sc1, waits=[mid, mvv, "g_init"], inc=[("sc4", 16)])
        p.emit("pool", sc2, inc=[("sc4", 16)])
        p.emit("dve", lambda e, g=g, gm=gm: e.tensor_copy(cwsb[:, g, 0:1], rt4[:, gm, 2:3]), inc=["d4"])
        p.emit("dve", lambda e, g=g, gm=gm: e.tensor_copy(cwsb[:, g, 1:2], rt4[:, gm, 3:4]), inc=["d4"])
        for kk, posv in ((0, vec1), (1, vec2)):
            p.emit("dve", lambda e, posv=posv: e.tensor_scalar(vec3[:, :], posv[:, :], float(CAP - 1), None,
                                                               ALU.min), inc=["d4"])
            p.emit("dve", lambda e, gm=gm, kk=kk: e.scalar_tensor_tensor(
                vec3[:, :], rt4[:, gm, kk:kk + 1], float(CAP), vec3[:, :], ALU.mult, ALU.add), inc=["d4"])
            p.emit("dve", lambda e, g=g, kk=kk: e.tensor_copy(cisb[:, g, kk:kk + 1], vec3[:, :]), inc=["d4"])
    mtab4 = p.mark("d4")
    p.dma("sp", ci_dd.rearrange("(a p) k -> p a k", p=128), cisb[:, :, :], waits=[mtab4], sem="ct_w")
    p.dma("sp", cw_dd.rearrange("(a p) k -> p a k", p=128), cwsb[:, :, :], waits=[mtab4], sem="ct_w")

    psS_free = dict(ps_free)
    wl_free2 = [None, None]
    wr_last = [None, None]
    evst_free = None
    for le in range(2):
        warm(AF.Sqrt)
        egath = None
        for st in range(8):
            def ggi(e, le=le, st=st):
                return e.indirect_dma_start(gi_t[:, :], None, g_dd,
                                            bass.IndirectOffsetOnAxis(eix_sb[:, le * 8 + st: le * 8 + st + 1], 0),
                                            bounds_check=regs["b16k"], oob_is_err=False)
            p.emit("pool", ggi, waits=[("sc4", p.cnt("sc4")), egath, m_eix], inc=[("g6", 16)])
            p.emit("pool", lambda e: e.memset(sx[:, :], 0.0), waits=[egath, "ln_d", ("p4", p.cnt("p4"))], inc=["k6"])
            mz = p.mark("k6")
            def gat(e):
                return e.indirect_dma_start(sx[:, :], None, x_d,
                                            bass.IndirectOffsetOnAxis(gi_t[:, 0:1], 0),
                                            bounds_check=regs["b8k"], oob_is_err=False)
            p.emit("pool", gat, waits=[("g6", p.cnt("g6")), mz], inc=[("g6", 16)])
            mgat = p.mark("g6")
            mhi = emit_ln(sx[:, :], mgat)
            egath = p.dma("sp", e_dd[st * 128:(st + 1) * 128, :], sbf[:, :], waits=[mhi], sem="sbf_fr")
        warm(AF.Silu)
        for sc in range(2):
            et_ld = None
            for fc in range(16):
                et_ld = p.dma("act", big16[:, fc, :], e_dd[sc * 512:(sc + 1) * 512, fc * 128:(fc + 1) * 128],
                              waits=[("sbf_fr", p.cnt("sbf_fr")), ("p6", p.cnt("p6")), ("lg_p", p.cnt("lg_p"))],
                              sem="tp_dma", transpose=True)
            for ic in range(11):
                pr = ic % 2
                mwg = p.dma("pool", wlhs[:, 2 * pr, :, :],
                            wg_d[le, :, ic * 128:(ic + 1) * 128].rearrange("(a p) i -> p a i", p=128),
                            waits=[wl_free2[pr]], sem="w_dma")
                mwu = p.dma("pool", wlhs[:, 2 * pr + 1, :, :],
                            wu_d[le, :, ic * 128:(ic + 1) * 128].rearrange("(a p) i -> p a i", p=128),
                            sem="w_dma")
                for fc in range(16):
                    def mmg(e, fc=fc, pr=pr):
                        return e.matmul(psA[:, :], wlhs[:, 2 * pr, fc, :], big16[:, fc, :],
                                        start=(fc == 0), stop=(fc == 15))
                    p.emit("pe", mmg, waits=([mwg, et_ld, psS_free[0]] if fc == 0 else []), inc=[])
                    def mmu(e, fc=fc, pr=pr):
                        return e.matmul(psB[:, :], wlhs[:, 2 * pr + 1, fc, :], big16[:, fc, :],
                                        start=(fc == 0), stop=(fc == 15))
                    p.emit("pe", mmu, waits=([mwu, psS_free[1]] if fc == 0 else []),
                           inc=["p6"] if fc == 15 else [])
                mgu = p.mark("p6")
                p.emit("act", lambda e: e.activation(stile[:, :], psA[:, :], AF.Silu, bias=zv[:, :], scale=1.0),
                       waits=[mgu, "d6"], inc=["a6"])
                p.emit("dve", lambda e, ic=ic, sc=sc: e.tensor_tensor(
                    at_v[:, ic, sc * 512:(sc + 1) * 512], stile[:, :], psB[:, :], ALU.mult),
                    waits=["a6", ("p6", p.cnt("p6"))], inc=["d6"])
                psS_free[0] = p.mark("a6")
                psS_free[1] = p.mark("d6")
                wl_free2[pr] = p.mark("p6")
        for hg in range(4):
            ws = hg % 2
            mwd = p.dma("pool", wrhs[:, ws, 0:11, :],
                        wd_d[le, :, hg * 512:(hg + 1) * 512].rearrange("(a p) c -> p a c", p=128),
                        waits=[wr_last[ws], ("lg_p", p.cnt("lg_p")), ("v_p", p.cnt("v_p"))], sem="w_dma")
            for st in range(8):
                pi = st % 2
                ps = psA if pi == 0 else psB
                for ic in range(11):
                    def mmd(e, ic=ic, st=st, ps=ps, ws=ws):
                        return e.matmul(ps[:, :], at_v[:, ic, st * 128:(st + 1) * 128], wrhs[:, ws, ic, :],
                                        start=(ic == 0), stop=(ic == 10))
                    p.emit("pe", mmd, waits=([mwd, "d6", psS_free[pi]] if ic == 0 else []),
                           inc=["p6"] if ic == 10 else [])
                mdn6 = p.mark("p6")
                p.emit("act", lambda e, ps=ps: e.activation(stile[:, :], ps[:, :], AF.Copy, bias=0.0, scale=1.0),
                       waits=[mdn6, evst_free], inc=["a6"])
                psS_free[pi] = p.mark("a6")
                evst_free = p.dma("sp", eo_in[le * CAP + st * 128: le * CAP + (st + 1) * 128,
                                              hg * 512:(hg + 1) * 512],
                                  stile[:, :], waits=[psS_free[pi]], sem="eo_st")
            wr_last[ws] = p.mark("p6")
    p.emit("pool", lambda e: e.collective_compute(
        "AllGather", ALU.bypass, replica_groups=[list(range(NC))],
        ins=[eo_in.tensor.ap().opt()], outs=[eo_all.tensor.ap().opt()]),
        waits=["eo_st"], inc=["cc"])
    cc_eo = p.mark("cc")

    p.emit("pool", lambda e: e.memset(vh[:, :, :, 128:129], 1.0),
           waits=[("p6", p.cnt("p6")), "d6", ("eo_st", p.cnt("eo_st"))], inc=["k5"])
    p.emit("act", lambda e: e.activation(dn[:, 12:13], epsv[:, :], AF.Exp, bias=zv[:, :], scale=1.0),
           inc=["warm5"])
    qh_free = [None, None]
    kv_free = None
    exp_marks = {}
    psO_free = [None] * 4
    stile_free = p.mark("eo_st")
    for h in range(16):
        par = 0
        mq = p.dma("sp", qh[:, par, :], q_dd[h * 128:(h + 1) * 128, :], waits=[qh_free[par]], sem="kv_ld")
        mks = []
        for rho in range(4):
            mk1 = p.dma("sp", kth[:, rho, :], kt_all[rho * C + h * 128: rho * C + (h + 1) * 128, :],
                        waits=[cc_kv, kv_free, "p6", "d6"], sem="kv_ld")
            mk2 = p.dma("sp", vh[:, rho, :, 0:128],
                        v_all[rho * 1024:(rho + 1) * 1024, h * 128:(h + 1) * 128]
                        .rearrange("(m p) d -> p m d", p=128),
                        waits=[cc_kv, kv_free, ("k5", 2)], sem="kv_ld")
            mks += [mk1, mk2]
        for qg in range(2):
            cis = list(range(4))
            mis = [qg * 4 + i for i in range(4)]
            kmax = 4 * mis[-1] + 3
            pend = []
            for kc in range(kmax + 2):
                if kc <= kmax:
                    pi = kc % 2
                    ps = psA if pi == 0 else psB
                    def mms(e, kc=kc, qg=qg, par=par, ps=ps):
                        return e.matmul(ps[:, :], kth[:, kc % 4, (kc // 4) * 128:(kc // 4 + 1) * 128],
                                        qh[:, par, qg * 512:(qg + 1) * 512], start=True, stop=True)
                    p.emit("pe", mms, waits=[mq] + mks + [psS_free[pi]], inc=["p5"])
                    msc = p.mark("p5")
                    newpend = []
                    for ci, mi in zip(cis, mis):
                        dcp = 4 * mi - kc
                        if dcp < -3:
                            continue
                        slot = (kc % 2) * 4 + ci
                        dcol = h * 32 + dcp + 3
                        if dcp <= 0:
                            p.emit("dve", lambda e, ci=ci, ps=ps, dcp=dcp: e.tensor_tensor(
                                sfp[:, :], ps[:, ci * 128:(ci + 1) * 128], mtab[:, -dcp, :], ALU.add),
                                waits=[msc, exp_marks.get(("pv", slot))], inc=["d5"])
                            md = p.mark("d5")
                            p.emit("act", lambda e, slot=slot, dcol=dcol: e.activation(
                                pt_v[:, slot, :], sfp[:, :], AF.Exp, bias=alib[:, dcol:dcol + 1], scale=SCALE),
                                waits=[md], inc=["a5"])
                        else:
                            p.emit("act", lambda e, slot=slot, dcol=dcol, ci=ci, ps=ps: e.activation(
                                pt_v[:, slot, :], ps[:, ci * 128:(ci + 1) * 128], AF.Exp,
                                bias=alib[:, dcol:dcol + 1], scale=SCALE),
                                waits=[msc, exp_marks.get(("pv", slot))], inc=["a5"])
                        mex = p.mark("a5")
                        def mmpv(e, slot=slot, ci=ci, kc=kc, kmax=kmax, mi=mi):
                            return e.matmul(psO[ci][:, 0:129], pt_v[:, slot, :], vh[:, kc % 4, kc // 4, :],
                                            start=(kc == 0), stop=(kc == min(kmax, 4 * mi + 3)))
                        newpend.append((mmpv, mex, ci, kc, slot))
                    psS_free[pi] = p.mark("a5")
                else:
                    newpend = []
                for fn, mex, ci, kcp, slot in pend:
                    p.emit("pe", fn, waits=[mex] + ([psO_free[ci]] if kcp == 0 else []), inc=["p5"])
                    exp_marks[("pv", slot)] = p.mark("p5")
                pend = newpend
            for ci, mi in zip(cis, mis):
                last_kc = min(kmax, 4 * mi + 3)
                mpv = exp_marks[("pv", (last_kc % 2) * 4 + ci)]
                p.emit("dve", lambda e, ci=ci: e.tensor_copy(dn[:, ci:ci + 1], psO[ci][:, 128:129]),
                       waits=[mpv], inc=["d5"])
                p.emit("dve", lambda e, ci=ci: e.reciprocal(dn[:, 8 + ci:9 + ci], dn[:, ci:ci + 1]), inc=["d5"])
                md5 = p.mark("d5")
                p.emit("act", lambda e, ci=ci: e.activation(
                    stile[:, ci * 128:(ci + 1) * 128], psO[ci][:, 0:128], AF.Copy,
                    bias=0.0, scale=dn[:, 8 + ci:9 + ci]),
                    waits=[md5, stile_free], inc=["a5"])
                psO_free[ci] = p.mark("a5")
            mev = p.mark("a5")
            stile_free = p.dma("sp", o_dd[qg * 512: (qg + 1) * 512, h * 128:(h + 1) * 128]
                               .rearrange("(a p) d -> p a d", p=128),
                               stile[:, :].rearrange("p (a d) -> p a d", d=128),
                               waits=[mev], sem="o_st")
        qh_free[par] = p.mark("p5")
        kv_free = p.mark("p5")

    wr7 = [p.mark("p6"), p.mark("p6")]
    for mig in range(2):
        ot_ld = None
        for fc in range(16):
            ot_ld = p.dma("act", big16[:, fc, :], o_dd[mig * 512:(mig + 1) * 512, fc * 128:(fc + 1) * 128],
                          waits=[("o_st", p.cnt("o_st")), ("p7", p.cnt("p7")), ("p6", p.cnt("p6"))],
                          sem="tp_dma", transpose=True)
        for cg in range(4):
            ws7 = cg % 2
            mwo = p.dma("pool", wrhs[:, ws7, :, :],
                        wo_d[:, cg * 512:(cg + 1) * 512].rearrange("(a p) c -> p a c", p=128),
                        waits=[wr7[ws7]], sem="w_dma")
            for mi2 in range(4):
                mi = mig * 4 + mi2
                pi = mi2 % 2
                ps = psA if pi == 0 else psB
                def mmbo(e, cg=cg, ps=ps):
                    return e.matmul(ps[:, :], ones_rb[0:1, 0:128], brow_o[0:1, cg * 512:(cg + 1) * 512],
                                    start=True, stop=False)
                p.emit("pe", mmbo, waits=[mwo, ot_ld, psS_free[pi]], inc=[])
                for fc in range(16):
                    def mmo(e, fc=fc, mi2=mi2, ps=ps, ws7=ws7):
                        return e.matmul(ps[:, :], big16[:, fc, mi2 * 128:(mi2 + 1) * 128], wrhs[:, ws7, fc, :],
                                        start=False, stop=(fc == 15))
                    p.emit("pe", mmo, inc=["p7"] if fc == 15 else [])
                mo7 = p.mark("p7")
                p.emit("dve", lambda e, mi=mi, cg=cg, ps=ps: e.tensor_tensor(
                    out_res[:, mi, cg * 512:(cg + 1) * 512], out_res[:, mi, cg * 512:(cg + 1) * 512],
                    ps[:, :], ALU.add), waits=[mo7], inc=["d7"])
                psS_free[pi] = p.mark("d7")
            wr7[ws7] = p.mark("p7")

    for ih in range(2):
        warm(AF.Silu)
        for ic in range(11):
            ica = ih * 11 + ic
            pr = ic % 2
            mwg = p.dma("pool", wlhs[:, 2 * pr, :, :],
                        wsg_d[:, ica * 128:(ica + 1) * 128].rearrange("(a p) i -> p a i", p=128),
                        waits=[wl_free2[pr]], sem="w_dma")
            mwu = p.dma("pool", wlhs[:, 2 * pr + 1, :, :],
                        wsu_d[:, ica * 128:(ica + 1) * 128].rearrange("(a p) i -> p a i", p=128),
                        sem="w_dma")
            for sc in range(2):
                for fc in range(16):
                    def mmg(e, fc=fc, sc=sc, pr=pr):
                        return e.matmul(psA[:, :], wlhs[:, 2 * pr, fc, :], xnT[:, fc, sc * 512:(sc + 1) * 512],
                                        start=(fc == 0), stop=(fc == 15))
                    p.emit("pe", mmg, waits=([mwg, psS_free[0]] if fc == 0 else []), inc=[])
                    def mmu(e, fc=fc, sc=sc, pr=pr):
                        return e.matmul(psB[:, :], wlhs[:, 2 * pr + 1, fc, :], xnT[:, fc, sc * 512:(sc + 1) * 512],
                                        start=(fc == 0), stop=(fc == 15))
                    p.emit("pe", mmu, waits=([mwu, psS_free[1]] if fc == 0 else []),
                           inc=["p8"] if fc == 15 else [])
                mgu = p.mark("p8")
                p.emit("act", lambda e: e.activation(stile[:, :], psA[:, :], AF.Silu, bias=zv[:, :], scale=1.0),
                       waits=[mgu, "d8", ("o_st", p.cnt("o_st"))], inc=["a8"])
                p.emit("dve", lambda e, ic=ic, sc=sc: e.tensor_tensor(
                    at_v[:, ic, sc * 512:(sc + 1) * 512], stile[:, :], psB[:, :], ALU.mult),
                    waits=["a8", ("p8", p.cnt("p8")), ("p6", p.cnt("p6")), "p5"], inc=["d8"])
                psS_free[0] = p.mark("a8")
                psS_free[1] = p.mark("d8")
            wl_free2[pr] = p.mark("p8")
        for hg in range(4):
            ws8 = hg % 2
            mwd = p.dma("pool", wrhs[:, ws8, 0:11, :],
                        wsd_d[ih * 11 * 128:(ih + 1) * 11 * 128, hg * 512:(hg + 1) * 512]
                        .rearrange("(a p) c -> p a c", p=128),
                        waits=[wr7[ws8]], sem="w_dma")
            for st in range(8):
                pi = st % 2
                ps = psA if pi == 0 else psB
                for ic in range(11):
                    def mmd(e, ic=ic, st=st, ps=ps, ws8=ws8):
                        return e.matmul(ps[:, :], at_v[:, ic, st * 128:(st + 1) * 128], wrhs[:, ws8, ic, :],
                                        start=(ic == 0), stop=(ic == 10))
                    p.emit("pe", mmd, waits=([mwd, "d8", psS_free[pi]] if ic == 0 else []),
                           inc=["p8"] if ic == 10 else [])
                mdn8 = p.mark("p8")
                p.emit("dve", lambda e, st=st, hg=hg, ps=ps: e.tensor_tensor(
                    out_res[:, st, hg * 512:(hg + 1) * 512], out_res[:, st, hg * 512:(hg + 1) * 512],
                    ps[:, :], ALU.add), waits=[mdn8], inc=["d8"])
                psS_free[pi] = p.mark("d8")
            wr7[ws8] = p.mark("p8")

    gath_free = None
    for m in range(8):
        def gci(e, m=m):
            return e.indirect_dma_start(ci2[:, :], None, ci_dd,
                                        bass.IndirectOffsetOnAxis(tok_sb[:, m:m + 1], 0),
                                        bounds_check=regs["b8k"], oob_is_err=False)
        def gcw(e, m=m):
            return e.indirect_dma_start(cw2[:, :], None, cw_dd,
                                        bass.IndirectOffsetOnAxis(tok_sb[:, m:m + 1], 0),
                                        bounds_check=regs["b8k"], oob_is_err=False)
        p.emit("pool", gci, waits=["ct_w", gath_free, m_tok], inc=[("g9", 16)])
        p.emit("pool", gcw, inc=[("g9", 16)])
        mci = p.mark("g9")
        for kk in range(2):
            p.emit("dve", lambda e, kk=kk: e.tensor_copy(su32[:, 0:1], ci2[:, kk:kk + 1]),
                   waits=[mci, ("g9", p.cnt("g9"))], inc=["d9"])
            mi9 = p.mark("d9")
            def gat9(e):
                return e.indirect_dma_start(sbf[:, :], None, eo_all,
                                            bass.IndirectOffsetOnAxis(su32[:, 0:1], 0),
                                            bounds_check=regs["b16k"], oob_is_err=False)
            p.emit("pool", gat9, waits=[mi9, cc_eo, ("d9", p.cnt("d9"))], inc=[("g9", 16)])
            mg9 = p.mark("g9")
            p.emit("dve", lambda e, m=m, kk=kk: e.scalar_tensor_tensor(
                out_res[:, m, :], sbf[:, :], cw2[:, kk:kk + 1], out_res[:, m, :], ALU.mult, ALU.add),
                waits=[mg9], inc=["d9"])
            gath_free = p.mark("d9")
        p.dma("sp", y_d[m * 128:(m + 1) * 128, :], out_res[:, m, :], waits=[("d9", p.cnt("d9"))], sem="y_st")
    p.emit("sp", lambda e: e.nop(), waits=[("y_st", p.cnt("y_st"))])
    p.emit("pool", lambda e: e.nop(), waits=[("y_st", p.cnt("y_st"))])

    p.build_blocks()
    es.close()
    return nc


_CACHE = {}


def kernel(**inputs):
    xs = np.ascontiguousarray(np.asarray(inputs["L0_attn_x"], np.float32).reshape(T_ALL, C))
    shared = {
        "x": xs,
        "wq": np.ascontiguousarray(inputs["L0_attn_Wq"], np.float32),
        "wk": np.ascontiguousarray(inputs["L0_attn_Wk"], np.float32),
        "wv": np.ascontiguousarray(inputs["L0_attn_Wv"], np.float32),
        "wo": np.ascontiguousarray(inputs["L0_attn_Wo"], np.float32),
        "bq": np.ascontiguousarray(inputs["L0_attn_bq"], np.float32).reshape(1, C),
        "bk": np.ascontiguousarray(inputs["L0_attn_bk"], np.float32).reshape(1, C),
        "bv": np.ascontiguousarray(inputs["L0_attn_bv"], np.float32).reshape(1, C),
        "bo": np.ascontiguousarray(inputs["L0_attn_bo"], np.float32).reshape(1, C),
        "gw": np.ascontiguousarray(inputs["L0_ffn_gate_w"], np.float32),
        "wsg": np.ascontiguousarray(inputs["L0_ffn_ws_gate"], np.float32),
        "wsu": np.ascontiguousarray(inputs["L0_ffn_ws_up"], np.float32),
        "wsd": np.ascontiguousarray(inputs["L0_ffn_ws_down"], np.float32),
    }
    in_maps = []
    own_all = []
    for core in range(NC):
        b, r = core // 4, core % 4
        own = np.array([b * 32 + r + 4 * m for m in range(8)], np.int64)
        own_all.append(own)
        tok = (own[None, :] * 128 + np.arange(128)[:, None]).astype(np.uint32)
        eix = (2 * core * CAP + (np.arange(16)[None, :] * 128 + np.arange(128)[:, None])).astype(np.uint32)
        rv = np.full((128, 1), 128.0 * r, np.float32)
        m = dict(shared)
        xo_rows = (own[:, None] * 128 + np.arange(128)[None, :]).reshape(1024).astype(np.int64)
        m["xo"] = np.ascontiguousarray(xs[xo_rows])
        m["tok"] = tok
        m["eix"] = eix
        m["rv"] = rv
        m["wg"] = np.ascontiguousarray(inputs["L0_ffn_w_gate"][2 * core:2 * core + 2], np.float32)
        m["wu"] = np.ascontiguousarray(inputs["L0_ffn_w_up"][2 * core:2 * core + 2], np.float32)
        m["wd"] = np.ascontiguousarray(inputs["L0_ffn_w_down"][2 * core:2 * core + 2], np.float32)
        in_maps.append(m)
    if "nc" not in _CACHE:
        _CACHE["nc"] = build_core()
    res = run_bass_kernel_spmd(_CACHE["nc"], in_maps, core_ids=list(range(NC)))
    out = np.empty((T_ALL, C), np.float32)
    for core in range(NC):
        y = np.asarray(res.results[core]["y"], np.float32)
        for mi in range(8):
            g = own_all[core][mi]
            out[g * 128:(g + 1) * 128] = y[mi * 128:(mi + 1) * 128]
    return out.reshape(2, 4096, C)


# version 10
# speedup vs baseline: 1.3819x; 1.0382x over previous
import numpy as np
import concourse.bass as bass
import concourse.mybir as mybir
from concourse.bass_utils import run_bass_kernel_spmd
from contextlib import ExitStack

F32 = mybir.dt.float32
BF16 = mybir.dt.bfloat16
U32 = mybir.dt.uint32
AF = mybir.ActivationFunctionType
ALU = mybir.AluOpType
AX = mybir.AxisListType

T_ALL = 8192
C = 2048
E = 16
CAP = 1024
I_DIM = 1408
NC = 8
EPS = 1e-5
SCALE = float(1.0 / np.sqrt(128.0))
LN2H = -float(np.log(2.0) / 2.0)
BIG = 1.0e9
QUEUES = ("pe", "act", "dve", "pool", "sp")
SERIAL_QUEUES = ("dve", "act", "pool")
ENG_ATTR = {"pe": "tensor", "act": "scalar", "dve": "vector", "pool": "gpsimd", "sp": "sync"}


class Prog:
    def __init__(self, nc):
        self.nc = nc
        self.q = {k: [] for k in QUEUES}
        self._sems = {}
        self._prev = {}

    def sem(self, name):
        if name not in self._sems:
            h = self.nc.semaphore(name).__enter__()
            self._sems[name] = [h, 0]
        return self._sems[name]

    def cnt(self, name):
        return self.sem(name)[1]

    def mark(self, name):
        h, c = self.sem(name)
        return (name, c)

    def emit(self, queue, fn, waits=(), inc=()):
        wlist = []
        for w in waits:
            if w is None:
                continue
            if isinstance(w, str):
                h, c = self.sem(w)
                if c > 0:
                    wlist.append((h, c))
            else:
                name, val = w
                if val > 0:
                    h, _ = self.sem(name)
                    wlist.append((h, val))
        if queue in SERIAL_QUEUES and queue in self._prev:
            wlist.append(self._prev[queue])
        ilist = []
        inc = list(inc)
        if queue in SERIAL_QUEUES and not inc:
            inc = [queue + "_ser"]
        for i in inc:
            name, n = (i, 1) if isinstance(i, str) else i
            rec = self.sem(name)
            rec[1] += n
            ilist.append((rec[0], n))
        if queue in SERIAL_QUEUES and ilist:
            h0 = ilist[0][0]
            for name, rec in self._sems.items():
                if rec[0] is h0:
                    self._prev[queue] = (h0, rec[1])
                    break
        self.q[queue].append((fn, wlist, ilist))

    def dma(self, queue, out, in_, waits=(), sem=None, **kw):
        name = sem or (queue + "_dma")
        self.emit(queue, lambda e: e.dma_start(out=out, in_=in_, **kw),
                  waits=waits, inc=[(name, 16)])
        return self.mark(name)

    def barrier(self, name="bar"):
        allw = [(nm, rec[1]) for nm, rec in self._sems.items() if rec[1] > 0 and nm != name]
        for qn in QUEUES:
            self.emit(qn, lambda e: e.nop(), waits=allw, inc=[name])
        total = self.sem(name)[1]
        for qn in QUEUES:
            self.emit(qn, lambda e: e.nop(), waits=[(name, total)])

    def build_blocks(self):
        nc = self.nc
        with nc.Block() as blk:
            for qn in QUEUES:
                ops = self.q[qn]
                def mk(ops):
                    def body(eng):
                        for fn, waits, incs in ops:
                            for h, v in waits:
                                eng.wait_ge(h, v)
                            inst = fn(eng)
                            for j, (h, n) in enumerate(incs):
                                if inst is None or j > 0:
                                    eng.nop().then_inc(h, n)
                                else:
                                    inst.then_inc(h, n)
                    return body
                getattr(blk, ENG_ATTR[qn])(mk(ops))


def chunk_owner(g):
    gb, j = divmod(g, 32)
    return gb * 4 + j % 4, j // 4


def build_core():
    nc = bass.Bass("TRN2", target_bir_lowering=False)
    def dram(n, s, d=F32, k=None):
        t = nc.dram_tensor(n, s, d, kind=k) if k else nc.dram_tensor(n, s, d)
        return t.ap()
    x_d = dram("x", [T_ALL, C], F32, "ExternalInput")
    xo_d = dram("xo", [1024, C], F32, "ExternalInput")
    tok_d = dram("tok", [128, 8], U32, "ExternalInput")
    eix_d = dram("eix", [128, 16], U32, "ExternalInput")
    rv_d = dram("rv", [128, 1], F32, "ExternalInput")
    wq_d = dram("wq", [C, C], F32, "ExternalInput")
    wk_d = dram("wk", [C, C], F32, "ExternalInput")
    wv_d = dram("wv", [C, C], F32, "ExternalInput")
    wo_d = dram("wo", [C, C], F32, "ExternalInput")
    bq_d = dram("bq", [1, C], F32, "ExternalInput")
    bk_d = dram("bk", [1, C], F32, "ExternalInput")
    bv_d = dram("bv", [1, C], F32, "ExternalInput")
    bo_d = dram("bo", [1, C], F32, "ExternalInput")
    gw_d = dram("gw", [E, C], F32, "ExternalInput")
    wg_d = dram("wg", [2, C, I_DIM], F32, "ExternalInput")
    wu_d = dram("wu", [2, C, I_DIM], F32, "ExternalInput")
    wd_d = dram("wd", [2, I_DIM, C], F32, "ExternalInput")
    wsg_d = dram("wsg", [C, 2816], F32, "ExternalInput")
    wsu_d = dram("wsu", [C, 2816], F32, "ExternalInput")
    wsd_d = dram("wsd", [2816, C], F32, "ExternalInput")
    y_d = dram("y", [1024, C], F32, "ExternalOutput")

    xn_d = dram("xn_dd", [1024, C], BF16)
    lo_d = dram("lo_dd", [1024, C], BF16)
    r_in = dram("r_in", [1024, 4], F32)
    r_all = dram("r_all", [T_ALL, 4], F32)
    q_dd = dram("q_dd", [C, 1024], BF16)
    kvk_in = dram("kvk_in", [C, 1024], BF16)
    kvv_in = dram("kvv_in", [1024, C], BF16)
    kt_all = dram("kt_all", [4 * C, 1024], BF16)
    v_all = dram("v_all", [4096, C], BF16)
    o_dd = dram("o_dd", [1024, C], BF16)
    g_dd = dram("g_dd", [E * CAP, 4], U32)
    ci_dd = dram("ci_dd", [T_ALL, 2], U32)
    cw_dd = dram("cw_dd", [T_ALL, 2], F32)
    e_dd = dram("e_dd", [CAP, C], BF16)
    eo_in = dram("eo_in", [2 * CAP, C], BF16)
    eo_all = dram("eo_all", [NC * 2 * CAP, C], BF16)

    es = ExitStack()
    T = lambda n, s, d: es.enter_context(nc.sbuf_tensor(n, s, d))
    PT = lambda n: es.enter_context(nc.psum_tensor(n, [128, 512], F32))

    out_res = T("out_res", [128, 8, C], F32)
    xnT = T("xnT", [128, 16, 1024], BF16)
    a_at = T("a_at", [128, 11264], BF16)
    at_v = a_at[:, 0:11264].rearrange("p (a t) -> p a t", a=11)
    brow3 = a_at.tensor.bitcast(BF16) if False else a_at[0:1, 0:6144]
    kth = a_at[:, 0:4096].rearrange("p (a t) -> p a t", a=4)
    vh = a_at[:, 4096:8224].rearrange("p (a m t) -> p a m t", a=4, m=8)
    pt_v = a_at[:, 8224:9248].rearrange("p (a t) -> p a t", a=8)
    big16 = T("big16", [128, 16, 512], BF16)
    wrhs = T("wrhs", [128, 2, 16, 512], BF16)
    wlhs = T("wlhs", [128, 4, 16, 128], BF16)
    sx = T("sx", [128, C], F32)
    sbf = T("sbf", [128, C], BF16)
    f32s = T("f32s", [128, 128], F32)
    sfp = f32s[:, 0:128]
    qh = T("qh", [128, 1, 1024], BF16)
    brow_o = T("brow_o", [1, C], BF16)
    alib = T("alib", [128, 512], F32)
    mtab = T("mtab", [128, 4, 128], BF16)
    ltri = T("ltri", [128, 128], F32)
    e16 = T("e16", [128, 16], F32)
    e16b = T("e16b", [128, 16], F32)
    iop = T("iop", [128, 1], F32)
    rvp = T("rvp", [128, 1], F32)
    idf = T("idf", [128, 16], F32)
    ones_rb = T("ones_rb", [1, 512], BF16)
    ones_cf = T("ones_cf", [128, 1], F32)
    gthi = a_at[:, 6144:6400].rearrange("p (a t) -> p a t", a=16)
    gtlo = a_at[:, 6400:6656].rearrange("p (a t) -> p a t", a=16)
    s16a = f32s[:, 0:16]
    s16b = f32s[:, 16:32]
    s16c = f32s[:, 32:48]
    w8 = f32s[:, 48:56]
    rt4 = f32s[:, 56:64].rearrange("p (a t) -> p a t", a=2)
    srun = f32s[0:1, 64:80]
    cisb = T("cisb", [128, 64, 2], U32)
    cwsb = T("cwsb", [128, 64, 2], F32)
    tok_sb = T("tok_sb", [128, 8], U32)
    eix_sb = T("eix_sb", [128, 16], U32)
    ci2 = T("ci2", [128, 2], U32)
    cw2 = T("cw2", [128, 2], F32)
    su32 = T("su32", [128, 4], U32)
    su32b = T("su32b", [128, 4], U32)
    gi_t = T("gi_t", [128, 4], U32)
    stile = T("stile", [128, 512], BF16)
    vec1 = T("vec1", [128, 1], F32)
    vec2 = T("vec2", [128, 1], F32)
    vec3 = T("vec3", [128, 1], F32)
    vec4 = T("vec4", [128, 1], F32)
    dvec = T("dvec", [128, 2], F32)
    dn = T("dn", [128, 16], F32)
    vals = T("vals", [128, 4], U32)
    zv = T("zv", [128, 1], F32)
    epsv = T("epsv", [128, 1], F32)

    psA = PT("psA"); psB = PT("psB")
    psO = [PT("psO%d" % i) for i in range(4)]
    psX = PT("psX"); psT = PT("psT")

    p = Prog(nc)
    def warm(func):
        p.emit("act", lambda e, func=func: e.activation(dn[:, 14:15], epsv[:, :], func,
                                                        bias=zv[:, :], scale=1.0), inc=["warmx"])
    regs = {}
    def setup_regs(e):
        r1 = e.alloc_register()
        e.reg_mov(r1, E * CAP - 1)
        regs["b16k"] = r1
        r2 = e.alloc_register()
        e.reg_mov(r2, T_ALL - 1)
        regs["b8k"] = r2
        return None
    p.emit("pool", setup_regs)

    p.emit("pool", lambda e: e.iota(e16[:, :], [[1, 16]], base=0, channel_multiplier=0,
                                    allow_small_or_imprecise_dtypes=True), inc=["k"])
    p.emit("pool", lambda e: e.iota(iop[:, :], [[1, 1]], base=0, channel_multiplier=1,
                                    allow_small_or_imprecise_dtypes=True), inc=["k"])
    p.emit("pool", lambda e: e.memset(ones_rb[:, :], 1.0), inc=["k"])
    p.emit("pool", lambda e: e.memset(ones_cf[:, :], 1.0), inc=["k"])
    p.emit("pool", lambda e: e.memset(srun[:, :], 0.0), inc=["k"])
    p.emit("pool", lambda e: e.memset(zv[:, :], 0.0), inc=["k"])
    p.emit("pool", lambda e: e.memset(epsv[:, :], EPS), inc=["k"])
    p.emit("pool", lambda e: e.memset(cisb[:, :, :], 0xFFFFFFFF), inc=["k"])
    p.emit("pool", lambda e: e.iota(sx[:, 0:128], [[1, 128]], base=0, channel_multiplier=0,
                                    allow_small_or_imprecise_dtypes=True), inc=["k"])
    mrv = p.dma("sp", rvp[:, :], rv_d)
    m_tok = p.dma("sp", tok_sb[:, :], tok_d)
    m_eix = p.dma("sp", eix_sb[:, :], eix_d)
    kc0 = p.mark("k")
    p.emit("dve", lambda e: e.tensor_scalar(e16b[:, :], e16[:, :], 65536.0, None, ALU.add),
           waits=[kc0], inc=["kd"])
    p.emit("dve", lambda e: e.tensor_scalar(idf[:, :], e16[:, :], iop[:, :], None, ALU.is_equal), inc=["kd"])
    p.emit("dve", lambda e: e.tensor_scalar(ltri[:, :], sx[:, 0:128], iop[:, :], None, ALU.is_gt), inc=["kd"])
    for md in range(4):
        p.emit("dve", lambda e, md=md: e.tensor_scalar(vec1[:, :], iop[:, :], rvp[:, :], float(128 * md),
                                                       ALU.subtract, ALU.add), waits=[mrv], inc=["kd"])
        p.emit("dve", lambda e, md=md: e.tensor_scalar(mtab[:, md, :], sx[:, 0:128], vec1[:, :], -BIG,
                                                       ALU.is_lt, ALU.mult), inc=["kd"])
    mdc = p.mark("kd")
    p.emit("pool", lambda e: e.iota(sx[:, 128:640], [[0, 16], [1, 32]], base=-3, channel_multiplier=0,
                                    allow_small_or_imprecise_dtypes=True), waits=[mdc], inc=["k"])
    p.emit("pool", lambda e: e.iota(sx[:, 640:1152], [[1, 16], [0, 32]], base=1, channel_multiplier=0,
                                    allow_small_or_imprecise_dtypes=True), inc=["k"])
    p.emit("pool", lambda e: e.iota(sx[:, 1152:1664], [[0, 512]], base=0, channel_multiplier=1,
                                    allow_small_or_imprecise_dtypes=True), inc=["k"])
    kc1 = p.mark("k")
    p.emit("act", lambda e: e.activation(sx[:, 640:1152], sx[:, 640:1152], AF.Exp, bias=zv[:, :], scale=LN2H),
           waits=[kc1], inc=["ka"])
    p.emit("dve", lambda e: e.tensor_scalar(sx[:, 1152:1664], sx[:, 1152:1664], rvp[:, :], -64.0,
                                            ALU.subtract, ALU.add), waits=[kc1], inc=["kd"])
    p.emit("dve", lambda e: e.scalar_tensor_tensor(alib[:, :], sx[:, 128:640], -128.0, sx[:, 1152:1664],
                                                   ALU.mult, ALU.add), inc=["kd"])
    p.emit("dve", lambda e: e.tensor_tensor(alib[:, :], alib[:, :], sx[:, 640:1152], ALU.mult),
           waits=["ka"], inc=["kd"])
    mbq = p.dma("pool", brow3[0:1, 0:C], bq_d)
    p.dma("pool", brow3[0:1, C:2 * C], bk_d)
    p.dma("pool", brow3[0:1, 2 * C:3 * C], bv_d)
    mbr = p.dma("pool", brow_o[0:1, :], bo_d)
    mg_ = p.dma("sp", out_res[0:16, 0, :], gw_d)
    idm = p.mark("kd")
    for fc in range(16):
        def trg(e, fc=fc):
            return e.transpose(psB[0:128, 0:16], out_res[0:16, 0, fc * 128:(fc + 1) * 128], idf[0:16, 0:16])
        p.emit("pe", trg, waits=[mg_, idm, ("kgt", 2 * fc)], inc=["kp"])
        p.emit("act", lambda e, fc=fc: e.activation(gthi[:, fc, :], psB[:, 0:16], AF.Copy, bias=0.0, scale=1.0),
               waits=[("kp", fc + 1)], inc=["kgt"])
        p.emit("dve", lambda e, fc=fc: e.scalar_tensor_tensor(gtlo[:, fc, :], gthi[:, fc, :], -1.0, psB[:, 0:16],
                                                              ALU.mult, ALU.add),
               waits=[("kgt", 2 * fc + 1)], inc=["kgt"])
    mzp = p.mark("k")
    for _q in range(4):
        p.dma("pool", g_dd[_q * 4096:(_q + 1) * 4096, :].rearrange("(p a) f -> p (a f)", p=128), cisb[:, :, :].rearrange("p a k -> p (a k)"),
              waits=[mzp], sem="g_init")
    p.barrier("b0")

    def emit_ln(src_ap, waitm):
        p.emit("dve", lambda e: e.tensor_reduce(vec1[:, :], src_ap, axis=AX.X, op=ALU.add),
               waits=[waitm], inc=["ln_d"])
        p.emit("act", lambda e: e.activation(psX[:, :], src_ap[:, 0:512], AF.Square, bias=zv[:, :], scale=1.0,
                                             accum_out=vec2[:, :]), waits=[waitm, "d4run", "d4"], inc=["ln_a"])
        p.emit("act", lambda e: e.activation(psX[:, :], src_ap[:, 512:1024], AF.Square, bias=zv[:, :], scale=1.0,
                                             accum_out=vec3[:, :]), inc=["ln_a"])
        p.emit("act", lambda e: e.activation(psT[:, :], src_ap[:, 1024:1536], AF.Square, bias=zv[:, :], scale=1.0,
                                             accum_out=vec4[:, :]), inc=["ln_a"])
        p.emit("act", lambda e: e.activation(psT[:, :], src_ap[:, 1536:2048], AF.Square, bias=zv[:, :], scale=1.0,
                                             accum_out=dvec[:, 0:1]), inc=["ln_a"])
        p.emit("dve", lambda e: e.tensor_scalar(vec1[:, :], vec1[:, :], 1.0 / C, None, ALU.mult), inc=["ln_d"])
        am = p.mark("ln_a")
        p.emit("dve", lambda e: e.tensor_tensor(vec2[:, :], vec2[:, :], vec3[:, :], ALU.add), waits=[am], inc=["ln_d"])
        p.emit("dve", lambda e: e.tensor_tensor(vec4[:, :], vec4[:, :], dvec[:, 0:1], ALU.add), inc=["ln_d"])
        p.emit("dve", lambda e: e.tensor_tensor(vec2[:, :], vec2[:, :], vec4[:, :], ALU.add), inc=["ln_d"])
        p.emit("dve", lambda e: e.tensor_tensor(vec3[:, :], vec1[:, :], vec1[:, :], ALU.mult), inc=["ln_d"])
        p.emit("dve", lambda e: e.scalar_tensor_tensor(vec3[:, :], vec2[:, :], 1.0 / C, vec3[:, :],
                                                       ALU.mult, ALU.subtract), inc=["ln_d"])
        p.emit("act", lambda e: e.activation(vec3[:, :], vec3[:, :], AF.Sqrt, bias=epsv[:, :], scale=1.0),
               waits=["ln_d"], inc=["ln_a"])
        p.emit("dve", lambda e: e.reciprocal(vec4[:, :], vec3[:, :]), waits=["ln_a"], inc=["ln_d"])
        p.emit("dve", lambda e: e.tensor_scalar(sbf[:, :], src_ap, vec1[:, :], vec4[:, :],
                                                ALU.subtract, ALU.mult), waits=["sbf_fr", "qkv_st"], inc=["ln_d"])
        return p.mark("ln_d")

    warm(AF.Sqrt)
    sx_free = None
    for m in range(8):
        mload = p.dma("sp", sx[:, :], xo_d[m * 128:(m + 1) * 128, :], waits=[sx_free])
        p.emit("dve", lambda e, m=m: e.tensor_copy(out_res[:, m, :], sx[:, :]), waits=[mload], inc=["ln_d"])
        mhi = emit_ln(sx[:, :], mload)
        mw1 = p.dma("sp", xn_d[m * 128:(m + 1) * 128, :], sbf[:, :], waits=[mhi], sem="sbf_fr")
        p.emit("dve", lambda e: e.tensor_scalar(sx[:, :], sx[:, :], vec1[:, :], vec4[:, :],
                                                ALU.subtract, ALU.mult), inc=["ln_d"])
        p.emit("dve", lambda e: e.tensor_tensor(sx[:, :], sx[:, :], sbf[:, :], ALU.subtract),
               waits=[mw1], inc=["ln_d"])
        p.emit("act", lambda e: e.activation(sbf[:, :], sx[:, :], AF.Copy, bias=0.0, scale=1.0),
               waits=["ln_d"], inc=["ln_a"])
        mlo = p.mark("ln_a")
        mw2 = p.dma("sp", lo_d[m * 128:(m + 1) * 128, :], sbf[:, :], waits=[mlo], sem="sbf_fr")
        sx_free = p.mark("ln_d")

    m_st1 = p.mark("sbf_fr")
    for fc in range(16):
        p.dma("act", xnT[:, fc, :], xn_d[0:1024, fc * 128:(fc + 1) * 128], waits=[m_st1], sem="tp_dma", transpose=True)
        p.dma("act", big16[:, fc, :], lo_d[0:512, fc * 128:(fc + 1) * 128], sem="tp_dma", transpose=True)
        p.dma("act", wrhs[:, 0, fc, :], lo_d[512:1024, fc * 128:(fc + 1) * 128], sem="tp_dma", transpose=True)
    mtp = p.mark("tp_dma")
    warm(AF.Exp)
    for m in range(8):
        loT = big16 if m < 4 else wrhs
        mloc = m % 4
        first = True
        for fc in range(16):
            for which in range(3):
                def mml(e, which=which, fc=fc, m=m, mloc=mloc, loT=loT, first=first):
                    gt = gthi if which != 2 else gtlo
                    lh = (xnT[:, fc, m * 128:(m + 1) * 128] if which != 1 else
                          (big16[:, fc, mloc * 128:(mloc + 1) * 128] if m < 4 else wrhs[:, 0, fc, mloc * 128:(mloc + 1) * 128]))
                    return e.matmul(psB[:, 0:16], lh, gt[:, fc, :], start=first, stop=False)
                p.emit("pe", mml, waits=[mtp, "rt_d", ("kgt", 32), "kd"] if first else [],
                       inc=["lg_p"] if (fc == 15 and which == 2) else [])
                first = False
        p.emit("dve", lambda e: e.tensor_copy(s16a[:, :], psB[:, 0:16]), waits=["lg_p"], inc=["rt_d"])
        p.emit("dve", lambda e: e.tensor_reduce(vec1[:, :], s16a[:, :], axis=AX.X, op=ALU.max, negate=True),
               inc=["rt_d"])
        p.emit("act", lambda e: e.activation(s16b[:, :], s16a[:, :], AF.Exp, bias=vec1[:, :], scale=1.0),
               waits=["rt_d"], inc=["rt_a"])
        p.emit("dve", lambda e: e.tensor_reduce(vec2[:, :], s16b[:, :], axis=AX.X, op=ALU.add),
               waits=["rt_a"], inc=["rt_d"])
        p.emit("dve", lambda e: e.reciprocal(vec3[:, :], vec2[:, :]), inc=["rt_d"])
        p.emit("dve", lambda e: e.tensor_scalar(s16a[:, :], s16b[:, :], vec3[:, :], None, ALU.mult), inc=["rt_d"])
        p.emit("dve", lambda e: e.max(out=w8[:, :], in_=s16a[:, :]), inc=["rt_d"])
        mm_ = m % 2
        p.emit("dve", lambda e: e.tensor_scalar(s16b[:, :], s16a[:, :], w8[:, 0:1], None, ALU.is_equal), inc=["rt_d"])
        p.emit("dve", lambda e: e.scalar_tensor_tensor(s16c[:, :], s16b[:, :], -65536.0, e16b[:, :], ALU.mult, ALU.add),
               inc=["rt_d"])
        p.emit("dve", lambda e: e.tensor_reduce(vec1[:, :], s16c[:, :], axis=AX.X, op=ALU.min), inc=["rt_d"])
        p.emit("dve", lambda e, mm_=mm_: e.tensor_copy(rt4[:, mm_, 0:1], vec1[:, :]),
               inc=["rt_d"])
        p.emit("dve", lambda e, mm_=mm_: e.tensor_scalar(s16b[:, :], e16[:, :], rt4[:, mm_, 0:1], None, ALU.is_equal),
               inc=["rt_d"])
        p.emit("dve", lambda e: e.scalar_tensor_tensor(s16c[:, :], s16b[:, :], -BIG, s16a[:, :], ALU.mult, ALU.add),
               inc=["rt_d"])
        p.emit("dve", lambda e, mm_=mm_: e.tensor_reduce(rt4[:, mm_, 3:4], s16c[:, :], axis=AX.X, op=ALU.max),
               inc=["rt_d"])
        p.emit("dve", lambda e, mm_=mm_: e.tensor_copy(rt4[:, mm_, 2:3], w8[:, 0:1]), inc=["rt_d"])
        p.emit("dve", lambda e, mm_=mm_: e.tensor_scalar(s16b[:, :], s16c[:, :], rt4[:, mm_, 3:4], None, ALU.is_equal),
               inc=["rt_d"])
        p.emit("dve", lambda e: e.scalar_tensor_tensor(s16c[:, :], s16b[:, :], -65536.0, e16b[:, :], ALU.mult, ALU.add),
               inc=["rt_d"])
        p.emit("dve", lambda e: e.tensor_reduce(vec1[:, :], s16c[:, :], axis=AX.X, op=ALU.min), inc=["rt_d"])
        p.emit("dve", lambda e, mm_=mm_: e.tensor_copy(rt4[:, mm_, 1:2], vec1[:, :]),
               inc=["rt_d"])
        mrt = p.mark("rt_d")
        p.dma("sp", r_in[m * 128:(m + 1) * 128, :], rt4[:, mm_, :], waits=[mrt], sem="rt_w")
    p.emit("pool", lambda e: e.collective_compute(
        "AllGather", ALU.bypass, replica_groups=[list(range(NC))],
        ins=[r_in.tensor.ap().opt()], outs=[r_all.tensor.ap().opt()]),
        waits=["rt_w"], inc=["cc"])
    cc_rt = p.mark("cc")

    ps_free = {0: p.mark("ps_ab"), 1: p.mark("rt_d")}
    wl_free = [None, None]
    for wi, (w_src, brow_i, dst_dram) in enumerate(((wq_d, 0, q_dd), (wk_d, 1, kvk_in))):
        for hdc in range(16):
            slot = hdc % 2
            mw = p.dma("pool", wlhs[:, slot, :, :],
                       w_src[:, hdc * 128:(hdc + 1) * 128].rearrange("(a p) c -> p a c", p=128),
                       waits=[wl_free[slot]], sem="w_dma")
            for tc in range(2):
                pi = tc
                ps = psA if pi == 0 else psB
                def mmb(e, hdc=hdc, brow_i=brow_i, ps=ps):
                    return e.matmul(ps[:, :], brow3[0:1, brow_i * C + hdc * 128: brow_i * C + (hdc + 1) * 128],
                                    ones_rb[0:1, :], start=True, stop=False)
                p.emit("pe", mmb, waits=[mw, mbr, ps_free[pi]], inc=[])
                for fc in range(16):
                    def mmq(e, fc=fc, tc=tc, slot=slot, ps=ps):
                        return e.matmul(ps[:, :], wlhs[:, slot, fc, :], xnT[:, fc, tc * 512:(tc + 1) * 512],
                                        start=False, stop=(fc == 15))
                    p.emit("pe", mmq, inc=["qk_p"] if fc == 15 else [])
                mqk = p.mark("qk_p")
                bslot = tc
                def ev(e, bslot=bslot, ps=ps):
                    return e.activation(sbf[:, bslot * 512:(bslot + 1) * 512], ps[:, :], AF.Copy, bias=0.0, scale=1.0)
                p.emit("act", ev, waits=[mqk, ("qkv_st", max(0, p.cnt("qkv_st") - 16))], inc=["qkv_ev"])
                ps_free[pi] = p.mark("qkv_ev")
                p.dma("sp", dst_dram[hdc * 128:(hdc + 1) * 128, tc * 512:(tc + 1) * 512],
                      sbf[:, bslot * 512:(bslot + 1) * 512], waits=[ps_free[pi]], sem="qkv_st")
            wl_free[slot] = p.mark("qk_p")
    wr3 = [None, None]
    for hg in range(4):
        ws3 = hg % 2
        mwv = p.dma("pool", wrhs[:, ws3, :, :],
                    wv_d[:, hg * 512:(hg + 1) * 512].rearrange("(a p) c -> p a c", p=128),
                    waits=[wr3[ws3], ("lg_p", p.cnt("lg_p"))], sem="w_dma")
        for tchunk in range(8):
            pi = tchunk % 2
            ps = psA if pi == 0 else psB
            def mmbv(e, hg=hg, ps=ps):
                return e.matmul(ps[:, :], ones_rb[0:1, 0:128], brow3[0:1, 2 * C + hg * 512: 2 * C + (hg + 1) * 512],
                                start=True, stop=False)
            p.emit("pe", mmbv, waits=[mwv, ps_free[pi]], inc=[])
            for fc in range(16):
                def mmv(e, fc=fc, tchunk=tchunk, ps=ps, ws3=ws3):
                    return e.matmul(ps[:, :], xnT[:, fc, tchunk * 128:(tchunk + 1) * 128], wrhs[:, ws3, fc, :],
                                    start=False, stop=(fc == 15))
                p.emit("pe", mmv, inc=["v_p"] if fc == 15 else [])
            mv = p.mark("v_p")
            bslot = 2 + pi
            def evv(e, bslot=bslot, ps=ps):
                return e.activation(sbf[:, bslot * 512:(bslot + 1) * 512], ps[:, :], AF.Copy, bias=0.0, scale=1.0)
            p.emit("act", evv, waits=[mv, ("qkv_st", max(0, p.cnt("qkv_st") - 16))], inc=["qkv_ev"])
            ps_free[pi] = p.mark("qkv_ev")
            p.dma("sp", kvv_in[tchunk * 128:(tchunk + 1) * 128, hg * 512:(hg + 1) * 512],
                  sbf[:, bslot * 512:(bslot + 1) * 512], waits=[ps_free[pi]], sem="qkv_st")
        wr3[ws3] = p.mark("v_p")
    p.emit("pool", lambda e: e.collective_compute(
        "AllGather", ALU.bypass, replica_groups=[[0, 1, 2, 3], [4, 5, 6, 7]],
        ins=[kvk_in.tensor.ap().opt()], outs=[kt_all.tensor.ap().opt()]),
        waits=["qkv_st"], inc=["cc"])
    p.emit("pool", lambda e: e.collective_compute(
        "AllGather", ALU.bypass, replica_groups=[[0, 1, 2, 3], [4, 5, 6, 7]],
        ins=[kvv_in.tensor.ap().opt()], outs=[v_all.tensor.ap().opt()]),
        inc=["cc"])
    cc_kv = p.mark("cc")

    p.emit("pool", lambda e: e.memset(sx[:, 0:128], 1.0), waits=["ln_d"], inc=["k5"])
    run_step = [0]
    for g in range(64):
        rk, mg = chunk_owner(g)
        gm = g % 2
        mrt = p.dma("sp", rt4[:, gm, :], r_all[rk * 1024 + mg * 128: rk * 1024 + (mg + 1) * 128, :],
                    waits=[cc_rt, ("d4", p.cnt("d4"))], sem="r_ld")
        p.emit("dve", lambda e, gm=gm: e.tensor_scalar(s16a[:, :], e16[:, :], rt4[:, gm, 0:1], None, ALU.is_equal),
               waits=[mrt], inc=["d4"])
        p.emit("dve", lambda e, gm=gm: e.tensor_scalar(s16b[:, :], e16[:, :], rt4[:, gm, 1:2], None, ALU.is_equal),
               inc=["d4"])
        p.emit("dve", lambda e: e.tensor_tensor(s16c[:, :], s16a[:, :], s16b[:, :], ALU.add), inc=["d4"])
        mB = p.mark("d4")
        def mmbc(e):
            return e.matmul(psX[0:128, 0:16], sx[0:1, 0:128], srun[0:1, :], start=True, stop=False)
        p.emit("pe", mmbc, waits=[mB, ("d4run", run_step[0]), ("k5", 1)], inc=[])
        def mmtri(e):
            return e.matmul(psX[0:128, 0:16], ltri[:, :], s16c[:, :], start=False, stop=True)
        p.emit("pe", mmtri, inc=["p4"])
        def mmtot(e, g=g):
            return e.matmul(psT[0:128, 0:16], sx[:, 0:128], s16c[:, :], start=(g == 0), stop=(g == 63))
        p.emit("pe", mmtot, inc=["p4"])
        mp4 = p.mark("p4")
        p.emit("dve", lambda e: e.tensor_copy(s16c[:, :], psX[:, 0:16]), waits=[mp4], inc=["d4"])
        p.emit("dve", lambda e: e.tensor_copy(srun[0:1, :], psT[0:1, 0:16]), inc=["d4run"])
        run_step[0] += 1
        p.emit("dve", lambda e: e.tensor_tensor(s16a[:, :], s16a[:, :], s16c[:, :], ALU.mult), inc=["d4"])
        p.emit("dve", lambda e: e.tensor_reduce(vec1[:, :], s16a[:, :], axis=AX.X, op=ALU.add), inc=["d4"])
        p.emit("dve", lambda e: e.tensor_tensor(s16b[:, :], s16b[:, :], s16c[:, :], ALU.mult), inc=["d4"])
        p.emit("dve", lambda e: e.tensor_reduce(vec2[:, :], s16b[:, :], axis=AX.X, op=ALU.add), inc=["d4"])
        for kk, posv, dst in ((0, vec1, su32), (1, vec2, su32b)):
            p.emit("dve", lambda e, posv=posv: e.tensor_scalar(vec3[:, :], posv[:, :], float(CAP), 1.0e6,
                                                               ALU.is_ge, ALU.mult),
                   waits=[("sc4", p.cnt("sc4"))], inc=["d4"])
            p.emit("dve", lambda e, gm=gm, kk=kk, posv=posv: e.scalar_tensor_tensor(
                vec4[:, :], rt4[:, gm, kk:kk + 1], float(CAP), posv[:, :], ALU.mult, ALU.add), inc=["d4"])
            p.emit("dve", lambda e: e.tensor_tensor(vec4[:, :], vec4[:, :], vec3[:, :], ALU.add), inc=["d4"])
            p.emit("dve", lambda e, dst=dst: e.tensor_copy(dst[:, 0:1], vec4[:, :]), inc=["d4"])
        mid = p.mark("d4")
        p.emit("pool", lambda e, g=g: e.iota(vals[:, :], [[0, 4]], base=g * 128, channel_multiplier=1),
               waits=[("sc4", p.cnt("sc4"))], inc=["k4"])
        mvv = p.mark("k4")
        def sc1(e):
            return e.indirect_dma_start(g_dd, bass.IndirectOffsetOnAxis(su32[:, 0:1], 0), vals[:, :], None,
                                        bounds_check=regs["b16k"], oob_is_err=False)
        def sc2(e):
            return e.indirect_dma_start(g_dd, bass.IndirectOffsetOnAxis(su32b[:, 0:1], 0), vals[:, :], None,
                                        bounds_check=regs["b16k"], oob_is_err=False)
        p.emit("pool", sc1, waits=[mid, mvv, "g_init"], inc=[("sc4", 16)])
        p.emit("pool", sc2, inc=[("sc4", 16)])
        p.emit("dve", lambda e, g=g, gm=gm: e.tensor_copy(cwsb[:, g, 0:1], rt4[:, gm, 2:3]), inc=["d4"])
        p.emit("dve", lambda e, g=g, gm=gm: e.tensor_copy(cwsb[:, g, 1:2], rt4[:, gm, 3:4]), inc=["d4"])
        for kk, posv in ((0, vec1), (1, vec2)):
            p.emit("dve", lambda e, posv=posv: e.tensor_scalar(vec3[:, :], posv[:, :], float(CAP - 1), None,
                                                               ALU.min), inc=["d4"])
            p.emit("dve", lambda e, gm=gm, kk=kk: e.scalar_tensor_tensor(
                vec3[:, :], rt4[:, gm, kk:kk + 1], float(CAP), vec3[:, :], ALU.mult, ALU.add), inc=["d4"])
            p.emit("dve", lambda e, g=g, kk=kk: e.tensor_copy(cisb[:, g, kk:kk + 1], vec3[:, :]), inc=["d4"])
    mtab4 = p.mark("d4")
    p.dma("sp", ci_dd.rearrange("(a p) k -> p a k", p=128), cisb[:, :, :], waits=[mtab4], sem="ct_w")
    p.dma("sp", cw_dd.rearrange("(a p) k -> p a k", p=128), cwsb[:, :, :], waits=[mtab4], sem="ct_w")

    psS_free = dict(ps_free)
    wl_free2 = [None, None]
    wr_last = [None, None]
    evst_free = None
    for le in range(2):
        warm(AF.Sqrt)
        egath = None
        for st in range(8):
            def ggi(e, le=le, st=st):
                return e.indirect_dma_start(gi_t[:, :], None, g_dd,
                                            bass.IndirectOffsetOnAxis(eix_sb[:, le * 8 + st: le * 8 + st + 1], 0),
                                            bounds_check=regs["b16k"], oob_is_err=False)
            p.emit("pool", ggi, waits=[("sc4", p.cnt("sc4")), egath, m_eix], inc=[("g6", 16)])
            p.emit("pool", lambda e: e.memset(sx[:, :], 0.0), waits=[egath, "ln_d", ("p4", p.cnt("p4"))], inc=["k6"])
            mz = p.mark("k6")
            def gat(e):
                return e.indirect_dma_start(sx[:, :], None, x_d,
                                            bass.IndirectOffsetOnAxis(gi_t[:, 0:1], 0),
                                            bounds_check=regs["b8k"], oob_is_err=False)
            p.emit("pool", gat, waits=[("g6", p.cnt("g6")), mz], inc=[("g6", 16)])
            mgat = p.mark("g6")
            mhi = emit_ln(sx[:, :], mgat)
            egath = p.dma("sp", e_dd[st * 128:(st + 1) * 128, :], sbf[:, :], waits=[mhi], sem="sbf_fr")
        warm(AF.Silu)
        for sc in range(2):
            et_ld = None
            for fc in range(16):
                et_ld = p.dma("act", big16[:, fc, :], e_dd[sc * 512:(sc + 1) * 512, fc * 128:(fc + 1) * 128],
                              waits=[("sbf_fr", p.cnt("sbf_fr")), ("p6", p.cnt("p6")), ("lg_p", p.cnt("lg_p"))],
                              sem="tp_dma", transpose=True)
            for ic in range(11):
                pr = ic % 2
                psg = psA if pr == 0 else psO[0]
                psu = psB if pr == 0 else psO[1]
                kg, ku = (0, 1) if pr == 0 else (2, 3)
                mwg = p.dma("pool", wlhs[:, 2 * pr, :, :],
                            wg_d[le, :, ic * 128:(ic + 1) * 128].rearrange("(a p) i -> p a i", p=128),
                            waits=[wl_free2[pr]], sem="w_dma")
                mwu = p.dma("pool", wlhs[:, 2 * pr + 1, :, :],
                            wu_d[le, :, ic * 128:(ic + 1) * 128].rearrange("(a p) i -> p a i", p=128),
                            sem="w_dma")
                for fc in range(16):
                    def mmg(e, fc=fc, pr=pr, psg=psg):
                        return e.matmul(psg[:, :], wlhs[:, 2 * pr, fc, :], big16[:, fc, :],
                                        start=(fc == 0), stop=(fc == 15))
                    p.emit("pe", mmg, waits=([mwg, et_ld, psS_free.get(kg)] if fc == 0 else []), inc=[])
                    def mmu(e, fc=fc, pr=pr, psu=psu):
                        return e.matmul(psu[:, :], wlhs[:, 2 * pr + 1, fc, :], big16[:, fc, :],
                                        start=(fc == 0), stop=(fc == 15))
                    p.emit("pe", mmu, waits=([mwu, psS_free.get(ku)] if fc == 0 else []),
                           inc=["p6"] if fc == 15 else [])
                mgu = p.mark("p6")
                p.emit("act", lambda e, psg=psg: e.activation(stile[:, :], psg[:, :], AF.Silu, bias=zv[:, :], scale=1.0),
                       waits=[mgu, "d6"], inc=["a6"])
                p.emit("dve", lambda e, ic=ic, sc=sc, psu=psu: e.tensor_tensor(
                    at_v[:, ic, sc * 512:(sc + 1) * 512], stile[:, :], psu[:, :], ALU.mult),
                    waits=["a6", ("p6", p.cnt("p6"))], inc=["d6"])
                psS_free[kg] = p.mark("a6")
                psS_free[ku] = p.mark("d6")
                wl_free2[pr] = p.mark("p6")
        for hg in range(4):
            ws = hg % 2
            mwd = p.dma("pool", wrhs[:, ws, 0:11, :],
                        wd_d[le, :, hg * 512:(hg + 1) * 512].rearrange("(a p) c -> p a c", p=128),
                        waits=[wr_last[ws], ("lg_p", p.cnt("lg_p")), ("v_p", p.cnt("v_p"))], sem="w_dma")
            for st in range(8):
                pi = st % 4
                ps = [psA, psB, psO[0], psO[1]][pi]
                for ic in range(11):
                    def mmd(e, ic=ic, st=st, ps=ps, ws=ws):
                        return e.matmul(ps[:, :], at_v[:, ic, st * 128:(st + 1) * 128], wrhs[:, ws, ic, :],
                                        start=(ic == 0), stop=(ic == 10))
                    p.emit("pe", mmd, waits=([mwd, "d6", psS_free.get(pi)] if ic == 0 else []),
                           inc=["p6"] if ic == 10 else [])
                mdn6 = p.mark("p6")
                p.emit("act", lambda e, ps=ps: e.activation(stile[:, :], ps[:, :], AF.Copy, bias=0.0, scale=1.0),
                       waits=[mdn6, evst_free], inc=["a6"])
                psS_free[pi] = p.mark("a6")
                evst_free = p.dma("sp", eo_in[le * CAP + st * 128: le * CAP + (st + 1) * 128,
                                              hg * 512:(hg + 1) * 512],
                                  stile[:, :], waits=[psS_free[pi]], sem="eo_st")
            wr_last[ws] = p.mark("p6")
    p.emit("pool", lambda e: e.collective_compute(
        "AllGather", ALU.bypass, replica_groups=[list(range(NC))],
        ins=[eo_in.tensor.ap().opt()], outs=[eo_all.tensor.ap().opt()]),
        waits=["eo_st"], inc=["cc"])
    cc_eo = p.mark("cc")

    SKIPT = [2, 3, 3, 4, 5, 7, 10, 13, 18, 25, 99, 99, 99, 99, 99, 99]
    p.emit("pool", lambda e: e.memset(vh[:, :, :, 128:129], 1.0),
           waits=[("p6", p.cnt("p6")), "d6", ("eo_st", p.cnt("eo_st"))], inc=["k5"])
    p.emit("act", lambda e: e.activation(dn[:, 12:13], epsv[:, :], AF.Exp, bias=zv[:, :], scale=1.0),
           inc=["warm5"])
    qh_free = [None, None]
    kv_free = None
    exp_marks = {}
    psO_free = [p.mark("a6"), p.mark("a6"), p.mark("d6"), p.mark("d6")]
    stile_free = p.mark("eo_st")
    for h in range(16):
        par = 0
        mq = p.dma("sp", qh[:, par, :], q_dd[h * 128:(h + 1) * 128, :], waits=[qh_free[par]], sem="kv_ld")
        mks = []
        for rho in range(4):
            mk1 = p.dma("sp", kth[:, rho, :], kt_all[rho * C + h * 128: rho * C + (h + 1) * 128, :],
                        waits=[cc_kv, kv_free, "p6", "d6"], sem="kv_ld")
            mk2 = p.dma("sp", vh[:, rho, :, 0:128],
                        v_all[rho * 1024:(rho + 1) * 1024, h * 128:(h + 1) * 128]
                        .rearrange("(m p) d -> p m d", p=128),
                        waits=[cc_kv, kv_free, ("k5", 2)], sem="kv_ld")
            mks += [mk1, mk2]
        for qg in range(2):
            cis = list(range(4))
            mis = [qg * 4 + i for i in range(4)]
            kmax = 4 * mis[-1] + 3
            pend = []
            for kc in range(kmax + 2):
                live = [4 * mi - kc for mi in mis if -3 <= 4 * mi - kc < SKIPT[h]]
                if kc <= kmax and live:
                    pi = kc % 2
                    ps = psA if pi == 0 else psB
                    def mms(e, kc=kc, qg=qg, par=par, ps=ps):
                        return e.matmul(ps[:, :], kth[:, kc % 4, (kc // 4) * 128:(kc // 4 + 1) * 128],
                                        qh[:, par, qg * 512:(qg + 1) * 512], start=True, stop=True)
                    p.emit("pe", mms, waits=[mq] + mks + [psS_free[pi]], inc=["p5"])
                    msc = p.mark("p5")
                    newpend = []
                    for ci, mi in zip(cis, mis):
                        dcp = 4 * mi - kc
                        if dcp < -3 or dcp >= SKIPT[h]:
                            continue
                        slot = (kc % 2) * 4 + ci
                        dcol = h * 32 + dcp + 3
                        if dcp <= 0:
                            p.emit("dve", lambda e, ci=ci, ps=ps, dcp=dcp: e.tensor_tensor(
                                sfp[:, :], ps[:, ci * 128:(ci + 1) * 128], mtab[:, -dcp, :], ALU.add),
                                waits=[msc, exp_marks.get(("pv", slot))], inc=["d5"])
                            md = p.mark("d5")
                            p.emit("act", lambda e, slot=slot, dcol=dcol: e.activation(
                                pt_v[:, slot, :], sfp[:, :], AF.Exp, bias=alib[:, dcol:dcol + 1], scale=SCALE),
                                waits=[md], inc=["a5"])
                        else:
                            p.emit("act", lambda e, slot=slot, dcol=dcol, ci=ci, ps=ps: e.activation(
                                pt_v[:, slot, :], ps[:, ci * 128:(ci + 1) * 128], AF.Exp,
                                bias=alib[:, dcol:dcol + 1], scale=SCALE),
                                waits=[msc, exp_marks.get(("pv", slot))], inc=["a5"])
                        mex = p.mark("a5")
                        first_kc = max(0, 4 * mi - SKIPT[h] + 1)
                        def mmpv(e, slot=slot, ci=ci, kc=kc, kmax=kmax, mi=mi, first_kc=first_kc):
                            return e.matmul(psO[ci][:, 0:129], pt_v[:, slot, :], vh[:, kc % 4, kc // 4, :],
                                            start=(kc == first_kc), stop=(kc == min(kmax, 4 * mi + 3)))
                        newpend.append((mmpv, mex, ci, kc == first_kc, slot))
                    psS_free[pi] = p.mark("a5")
                else:
                    newpend = []
                for fn, mex, ci, isfirst, slot in pend:
                    p.emit("pe", fn, waits=[mex] + ([psO_free[ci]] if isfirst else []), inc=["p5"])
                    exp_marks[("pv", slot)] = p.mark("p5")
                pend = newpend
            for ci, mi in zip(cis, mis):
                last_kc = min(kmax, 4 * mi + 3)
                mpv = exp_marks[("pv", (last_kc % 2) * 4 + ci)]
                p.emit("dve", lambda e, ci=ci: e.tensor_copy(dn[:, ci:ci + 1], psO[ci][:, 128:129]),
                       waits=[mpv], inc=["d5"])
                p.emit("dve", lambda e, ci=ci: e.reciprocal(dn[:, 8 + ci:9 + ci], dn[:, ci:ci + 1]), inc=["d5"])
                md5 = p.mark("d5")
                p.emit("act", lambda e, ci=ci: e.activation(
                    stile[:, ci * 128:(ci + 1) * 128], psO[ci][:, 0:128], AF.Copy,
                    bias=0.0, scale=dn[:, 8 + ci:9 + ci]),
                    waits=[md5, stile_free], inc=["a5"])
                psO_free[ci] = p.mark("a5")
            mev = p.mark("a5")
            stile_free = p.dma("sp", o_dd[qg * 512: (qg + 1) * 512, h * 128:(h + 1) * 128]
                               .rearrange("(a p) d -> p a d", p=128),
                               stile[:, :].rearrange("p (a d) -> p a d", d=128),
                               waits=[mev], sem="o_st")
        qh_free[par] = p.mark("p5")
        kv_free = p.mark("p5")

    psS_free[2] = p.mark("a5")
    psS_free[3] = p.mark("a5")
    wr7 = [p.mark("p6"), p.mark("p6")]
    for mig in range(2):
        ot_ld = None
        for fc in range(16):
            ot_ld = p.dma("act", big16[:, fc, :], o_dd[mig * 512:(mig + 1) * 512, fc * 128:(fc + 1) * 128],
                          waits=[("o_st", p.cnt("o_st")), ("p7", p.cnt("p7")), ("p6", p.cnt("p6"))],
                          sem="tp_dma", transpose=True)
        for cg in range(4):
            ws7 = cg % 2
            mwo = p.dma("pool", wrhs[:, ws7, :, :],
                        wo_d[:, cg * 512:(cg + 1) * 512].rearrange("(a p) c -> p a c", p=128),
                        waits=[wr7[ws7]], sem="w_dma")
            for mi2 in range(4):
                mi = mig * 4 + mi2
                pi = mi2 % 4
                ps = [psA, psB, psO[0], psO[1]][pi]
                def mmbo(e, cg=cg, ps=ps):
                    return e.matmul(ps[:, :], ones_rb[0:1, 0:128], brow_o[0:1, cg * 512:(cg + 1) * 512],
                                    start=True, stop=False)
                p.emit("pe", mmbo, waits=[mwo, ot_ld, psS_free.get(pi)], inc=[])
                for fc in range(16):
                    def mmo(e, fc=fc, mi2=mi2, ps=ps, ws7=ws7):
                        return e.matmul(ps[:, :], big16[:, fc, mi2 * 128:(mi2 + 1) * 128], wrhs[:, ws7, fc, :],
                                        start=False, stop=(fc == 15))
                    p.emit("pe", mmo, inc=["p7"] if fc == 15 else [])
                mo7 = p.mark("p7")
                p.emit("dve", lambda e, mi=mi, cg=cg, ps=ps: e.tensor_tensor(
                    out_res[:, mi, cg * 512:(cg + 1) * 512], out_res[:, mi, cg * 512:(cg + 1) * 512],
                    ps[:, :], ALU.add), waits=[mo7], inc=["d7"])
                psS_free[pi] = p.mark("d7")
            wr7[ws7] = p.mark("p7")

    for ih in range(2):
        warm(AF.Silu)
        for ic in range(11):
            ica = ih * 11 + ic
            pr = ic % 2
            mwg = p.dma("pool", wlhs[:, 2 * pr, :, :],
                        wsg_d[:, ica * 128:(ica + 1) * 128].rearrange("(a p) i -> p a i", p=128),
                        waits=[wl_free2[pr]], sem="w_dma")
            mwu = p.dma("pool", wlhs[:, 2 * pr + 1, :, :],
                        wsu_d[:, ica * 128:(ica + 1) * 128].rearrange("(a p) i -> p a i", p=128),
                        sem="w_dma")
            psg = psA if pr == 0 else psO[0]
            psu = psB if pr == 0 else psO[1]
            kg, ku = (0, 1) if pr == 0 else (2, 3)
            for sc in range(2):
                for fc in range(16):
                    def mmg(e, fc=fc, sc=sc, pr=pr, psg=psg):
                        return e.matmul(psg[:, :], wlhs[:, 2 * pr, fc, :], xnT[:, fc, sc * 512:(sc + 1) * 512],
                                        start=(fc == 0), stop=(fc == 15))
                    p.emit("pe", mmg, waits=([mwg, psS_free.get(kg)] if fc == 0 else []), inc=[])
                    def mmu(e, fc=fc, sc=sc, pr=pr, psu=psu):
                        return e.matmul(psu[:, :], wlhs[:, 2 * pr + 1, fc, :], xnT[:, fc, sc * 512:(sc + 1) * 512],
                                        start=(fc == 0), stop=(fc == 15))
                    p.emit("pe", mmu, waits=([mwu, psS_free.get(ku)] if fc == 0 else []),
                           inc=["p8"] if fc == 15 else [])
                mgu = p.mark("p8")
                p.emit("act", lambda e, psg=psg: e.activation(stile[:, :], psg[:, :], AF.Silu, bias=zv[:, :], scale=1.0),
                       waits=[mgu, "d8", ("o_st", p.cnt("o_st"))], inc=["a8"])
                p.emit("dve", lambda e, ic=ic, sc=sc, psu=psu: e.tensor_tensor(
                    at_v[:, ic, sc * 512:(sc + 1) * 512], stile[:, :], psu[:, :], ALU.mult),
                    waits=["a8", ("p8", p.cnt("p8")), ("p6", p.cnt("p6")), "p5"], inc=["d8"])
                psS_free[kg] = p.mark("a8")
                psS_free[ku] = p.mark("d8")
            wl_free2[pr] = p.mark("p8")
        for hg in range(4):
            ws8 = hg % 2
            mwd = p.dma("pool", wrhs[:, ws8, 0:11, :],
                        wsd_d[ih * 11 * 128:(ih + 1) * 11 * 128, hg * 512:(hg + 1) * 512]
                        .rearrange("(a p) c -> p a c", p=128),
                        waits=[wr7[ws8]], sem="w_dma")
            for st in range(8):
                pi = st % 4
                ps = [psA, psB, psO[0], psO[1]][pi]
                for ic in range(11):
                    def mmd(e, ic=ic, st=st, ps=ps, ws8=ws8):
                        return e.matmul(ps[:, :], at_v[:, ic, st * 128:(st + 1) * 128], wrhs[:, ws8, ic, :],
                                        start=(ic == 0), stop=(ic == 10))
                    p.emit("pe", mmd, waits=([mwd, "d8", psS_free.get(pi)] if ic == 0 else []),
                           inc=["p8"] if ic == 10 else [])
                mdn8 = p.mark("p8")
                p.emit("dve", lambda e, st=st, hg=hg, ps=ps: e.tensor_tensor(
                    out_res[:, st, hg * 512:(hg + 1) * 512], out_res[:, st, hg * 512:(hg + 1) * 512],
                    ps[:, :], ALU.add), waits=[mdn8], inc=["d8"])
                psS_free[pi] = p.mark("d8")
            wr7[ws8] = p.mark("p8")

    gath_free = None
    for m in range(8):
        def gci(e, m=m):
            return e.indirect_dma_start(ci2[:, :], None, ci_dd,
                                        bass.IndirectOffsetOnAxis(tok_sb[:, m:m + 1], 0),
                                        bounds_check=regs["b8k"], oob_is_err=False)
        def gcw(e, m=m):
            return e.indirect_dma_start(cw2[:, :], None, cw_dd,
                                        bass.IndirectOffsetOnAxis(tok_sb[:, m:m + 1], 0),
                                        bounds_check=regs["b8k"], oob_is_err=False)
        p.emit("pool", gci, waits=["ct_w", gath_free, m_tok], inc=[("g9", 16)])
        p.emit("pool", gcw, inc=[("g9", 16)])
        mci = p.mark("g9")
        for kk in range(2):
            p.emit("dve", lambda e, kk=kk: e.tensor_copy(su32[:, 0:1], ci2[:, kk:kk + 1]),
                   waits=[mci, ("g9", p.cnt("g9"))], inc=["d9"])
            mi9 = p.mark("d9")
            def gat9(e):
                return e.indirect_dma_start(sbf[:, :], None, eo_all,
                                            bass.IndirectOffsetOnAxis(su32[:, 0:1], 0),
                                            bounds_check=regs["b16k"], oob_is_err=False)
            p.emit("pool", gat9, waits=[mi9, cc_eo, ("d9", p.cnt("d9"))], inc=[("g9", 16)])
            mg9 = p.mark("g9")
            p.emit("dve", lambda e, m=m, kk=kk: e.scalar_tensor_tensor(
                out_res[:, m, :], sbf[:, :], cw2[:, kk:kk + 1], out_res[:, m, :], ALU.mult, ALU.add),
                waits=[mg9], inc=["d9"])
            gath_free = p.mark("d9")
        p.dma("sp", y_d[m * 128:(m + 1) * 128, :], out_res[:, m, :], waits=[("d9", p.cnt("d9"))], sem="y_st")
    p.emit("sp", lambda e: e.nop(), waits=[("y_st", p.cnt("y_st"))])
    p.emit("pool", lambda e: e.nop(), waits=[("y_st", p.cnt("y_st"))])

    p.build_blocks()
    es.close()
    return nc


_CACHE = {}


def kernel(**inputs):
    xs = np.ascontiguousarray(np.asarray(inputs["L0_attn_x"], np.float32).reshape(T_ALL, C))
    shared = {
        "x": xs,
        "wq": np.ascontiguousarray(inputs["L0_attn_Wq"], np.float32),
        "wk": np.ascontiguousarray(inputs["L0_attn_Wk"], np.float32),
        "wv": np.ascontiguousarray(inputs["L0_attn_Wv"], np.float32),
        "wo": np.ascontiguousarray(inputs["L0_attn_Wo"], np.float32),
        "bq": np.ascontiguousarray(inputs["L0_attn_bq"], np.float32).reshape(1, C),
        "bk": np.ascontiguousarray(inputs["L0_attn_bk"], np.float32).reshape(1, C),
        "bv": np.ascontiguousarray(inputs["L0_attn_bv"], np.float32).reshape(1, C),
        "bo": np.ascontiguousarray(inputs["L0_attn_bo"], np.float32).reshape(1, C),
        "gw": np.ascontiguousarray(inputs["L0_ffn_gate_w"], np.float32),
        "wsg": np.ascontiguousarray(inputs["L0_ffn_ws_gate"], np.float32),
        "wsu": np.ascontiguousarray(inputs["L0_ffn_ws_up"], np.float32),
        "wsd": np.ascontiguousarray(inputs["L0_ffn_ws_down"], np.float32),
    }
    in_maps = []
    own_all = []
    for core in range(NC):
        b, r = core // 4, core % 4
        own = np.array([b * 32 + r + 4 * m for m in range(8)], np.int64)
        own_all.append(own)
        tok = (own[None, :] * 128 + np.arange(128)[:, None]).astype(np.uint32)
        eix = (2 * core * CAP + (np.arange(16)[None, :] * 128 + np.arange(128)[:, None])).astype(np.uint32)
        rv = np.full((128, 1), 128.0 * r, np.float32)
        m = dict(shared)
        xo_rows = (own[:, None] * 128 + np.arange(128)[None, :]).reshape(1024).astype(np.int64)
        m["xo"] = np.ascontiguousarray(xs[xo_rows])
        m["tok"] = tok
        m["eix"] = eix
        m["rv"] = rv
        m["wg"] = np.ascontiguousarray(inputs["L0_ffn_w_gate"][2 * core:2 * core + 2], np.float32)
        m["wu"] = np.ascontiguousarray(inputs["L0_ffn_w_up"][2 * core:2 * core + 2], np.float32)
        m["wd"] = np.ascontiguousarray(inputs["L0_ffn_w_down"][2 * core:2 * core + 2], np.float32)
        in_maps.append(m)
    if "nc" not in _CACHE:
        _CACHE["nc"] = build_core()
    res = run_bass_kernel_spmd(_CACHE["nc"], in_maps, core_ids=list(range(NC)))
    out = np.empty((T_ALL, C), np.float32)
    for core in range(NC):
        y = np.asarray(res.results[core]["y"], np.float32)
        for mi in range(8):
            g = own_all[core][mi]
            out[g * 128:(g + 1) * 128] = y[mi * 128:(mi + 1) * 128]
    return out.reshape(2, 4096, C)


# version 11
# speedup vs baseline: 1.4672x; 1.0617x over previous
import numpy as np
import concourse.bass as bass
import concourse.mybir as mybir
from concourse.bass_utils import run_bass_kernel_spmd
from contextlib import ExitStack

F32 = mybir.dt.float32
BF16 = mybir.dt.bfloat16
U32 = mybir.dt.uint32
AF = mybir.ActivationFunctionType
ALU = mybir.AluOpType
AX = mybir.AxisListType

T_ALL = 8192
C = 2048
E = 16
CAP = 1024
I_DIM = 1408
NC = 8
EPS = 1e-5
SCALE = float(1.0 / np.sqrt(128.0))
LN2H = -float(np.log(2.0) / 2.0)
BIG = 1.0e9
QUEUES = ("pe", "act", "dve", "pool", "sp")
SERIAL_QUEUES = ("dve", "act", "pool")
ENG_ATTR = {"pe": "tensor", "act": "scalar", "dve": "vector", "pool": "gpsimd", "sp": "sync"}


class Prog:
    def __init__(self, nc):
        self.nc = nc
        self.q = {k: [] for k in QUEUES}
        self._sems = {}
        self._prev = {}

    def sem(self, name):
        if name not in self._sems:
            h = self.nc.semaphore(name).__enter__()
            self._sems[name] = [h, 0]
        return self._sems[name]

    def cnt(self, name):
        return self.sem(name)[1]

    def mark(self, name):
        h, c = self.sem(name)
        return (name, c)

    def emit(self, queue, fn, waits=(), inc=(), serial=True):
        wlist = []
        for w in waits:
            if w is None:
                continue
            if isinstance(w, str):
                h, c = self.sem(w)
                if c > 0:
                    wlist.append((h, c))
            else:
                name, val = w
                if val > 0:
                    h, _ = self.sem(name)
                    wlist.append((h, val))
        if serial and queue in SERIAL_QUEUES and queue in self._prev:
            wlist.append(self._prev[queue])
        ilist = []
        inc = list(inc)
        if queue in SERIAL_QUEUES and not inc:
            inc = [queue + "_ser"]
        for i in inc:
            name, n = (i, 1) if isinstance(i, str) else i
            rec = self.sem(name)
            rec[1] += n
            ilist.append((rec[0], n))
        if serial and queue in SERIAL_QUEUES and ilist:
            h0 = ilist[0][0]
            for name, rec in self._sems.items():
                if rec[0] is h0:
                    self._prev[queue] = (h0, rec[1])
                    break
        self.q[queue].append((fn, wlist, ilist))

    def dma(self, queue, out, in_, waits=(), sem=None, **kw):
        name = sem or (queue + "_dma")
        self.emit(queue, lambda e: e.dma_start(out=out, in_=in_, **kw),
                  waits=waits, inc=[(name, 16)], serial=False)
        return self.mark(name)

    def barrier(self, name="bar"):
        allw = [(nm, rec[1]) for nm, rec in self._sems.items() if rec[1] > 0 and nm != name]
        for qn in QUEUES:
            self.emit(qn, lambda e: e.nop(), waits=allw, inc=[name])
        total = self.sem(name)[1]
        for qn in QUEUES:
            self.emit(qn, lambda e: e.nop(), waits=[(name, total)])

    def build_blocks(self):
        nc = self.nc
        with nc.Block() as blk:
            for qn in QUEUES:
                ops = self.q[qn]
                def mk(ops):
                    def body(eng):
                        for fn, waits, incs in ops:
                            for h, v in waits:
                                eng.wait_ge(h, v)
                            inst = fn(eng)
                            for j, (h, n) in enumerate(incs):
                                if inst is None or j > 0:
                                    eng.nop().then_inc(h, n)
                                else:
                                    inst.then_inc(h, n)
                    return body
                getattr(blk, ENG_ATTR[qn])(mk(ops))


def chunk_owner(g):
    gb, j = divmod(g, 32)
    return gb * 4 + j % 4, j // 4


def build_core():
    nc = bass.Bass("TRN2", target_bir_lowering=False)
    def dram(n, s, d=F32, k=None):
        t = nc.dram_tensor(n, s, d, kind=k) if k else nc.dram_tensor(n, s, d)
        return t.ap()
    x_d = dram("x", [T_ALL, C], F32, "ExternalInput")
    xo_d = dram("xo", [1024, C], F32, "ExternalInput")
    tok_d = dram("tok", [128, 8], U32, "ExternalInput")
    eix_d = dram("eix", [128, 16], U32, "ExternalInput")
    rv_d = dram("rv", [128, 1], F32, "ExternalInput")
    wq_d = dram("wq", [C, C], F32, "ExternalInput")
    wk_d = dram("wk", [C, C], F32, "ExternalInput")
    wv_d = dram("wv", [C, C], F32, "ExternalInput")
    wo_d = dram("wo", [C, C], F32, "ExternalInput")
    bq_d = dram("bq", [1, C], F32, "ExternalInput")
    bk_d = dram("bk", [1, C], F32, "ExternalInput")
    bv_d = dram("bv", [1, C], F32, "ExternalInput")
    bo_d = dram("bo", [1, C], F32, "ExternalInput")
    gw_d = dram("gw", [E, C], F32, "ExternalInput")
    wg_d = dram("wg", [2, C, I_DIM], F32, "ExternalInput")
    wu_d = dram("wu", [2, C, I_DIM], F32, "ExternalInput")
    wd_d = dram("wd", [2, I_DIM, C], F32, "ExternalInput")
    wsg_d = dram("wsg", [C, 2816], F32, "ExternalInput")
    wsu_d = dram("wsu", [C, 2816], F32, "ExternalInput")
    wsd_d = dram("wsd", [2816, C], F32, "ExternalInput")
    y_d = dram("y", [1024, C], F32, "ExternalOutput")

    xn_d = dram("xn_dd", [1024, C], BF16)
    lo_d = dram("lo_dd", [1024, C], BF16)
    r_in = dram("r_in", [1024, 4], F32)
    r_all = dram("r_all", [T_ALL, 4], F32)
    q_dd = dram("q_dd", [C, 1024], BF16)
    kvk_in = dram("kvk_in", [C, 1024], BF16)
    kvv_in = dram("kvv_in", [1024, C], BF16)
    kt_all = dram("kt_all", [4 * C, 1024], BF16)
    v_all = dram("v_all", [4096, C], BF16)
    o_dd = dram("o_dd", [1024, C], BF16)
    g_dd = dram("g_dd", [E * CAP, 4], U32)
    ci_dd = dram("ci_dd", [T_ALL, 2], U32)
    cw_dd = dram("cw_dd", [T_ALL, 2], F32)
    e_dd = dram("e_dd", [CAP, C], BF16)
    eo_in = dram("eo_in", [2 * CAP, C], BF16)
    eo_all = dram("eo_all", [NC * 2 * CAP, C], BF16)

    es = ExitStack()
    T = lambda n, s, d: es.enter_context(nc.sbuf_tensor(n, s, d))
    PT = lambda n: es.enter_context(nc.psum_tensor(n, [128, 512], F32))

    out_res = T("out_res", [128, 8, C], F32)
    xnT = T("xnT", [128, 16, 1024], BF16)
    a_at = T("a_at", [128, 11264], BF16)
    at_v = a_at[:, 0:11264].rearrange("p (a t) -> p a t", a=11)
    brow3 = a_at.tensor.bitcast(BF16) if False else a_at[0:1, 0:6144]
    kth = a_at[:, 0:4096].rearrange("p (a t) -> p a t", a=4)
    vh = a_at[:, 4096:8224].rearrange("p (a m t) -> p a m t", a=4, m=8)
    pt_v = a_at[:, 8224:9248].rearrange("p (a t) -> p a t", a=8)
    big16 = T("big16", [128, 16, 512], BF16)
    wrhs = T("wrhs", [128, 2, 16, 512], BF16)
    wlhs = T("wlhs", [128, 4, 16, 128], BF16)
    sx = T("sx", [128, C], F32)
    sbf = T("sbf", [128, C], BF16)
    f32s = T("f32s", [128, 128], F32)
    sfp = f32s[:, 0:128]
    qh = T("qh", [128, 1, 1024], BF16)
    brow_o = T("brow_o", [1, C], BF16)
    alib = T("alib", [128, 512], F32)
    mtab = T("mtab", [128, 4, 128], BF16)
    ltri = T("ltri", [128, 128], F32)
    e16 = T("e16", [128, 16], F32)
    e16b = T("e16b", [128, 16], F32)
    iop = T("iop", [128, 1], F32)
    rvp = T("rvp", [128, 1], F32)
    idf = T("idf", [128, 16], F32)
    ones_rb = T("ones_rb", [1, 512], BF16)
    ones_cf = T("ones_cf", [128, 1], F32)
    gthi = a_at[:, 6144:6400].rearrange("p (a t) -> p a t", a=16)
    gtlo = a_at[:, 6400:6656].rearrange("p (a t) -> p a t", a=16)
    s16a = f32s[:, 0:16]
    s16b = f32s[:, 16:32]
    s16c = f32s[:, 32:48]
    w8 = f32s[:, 48:56]
    rt4 = f32s[:, 56:64].rearrange("p (a t) -> p a t", a=2)
    srun = f32s[0:1, 64:80]
    cisb = T("cisb", [128, 64, 2], U32)
    cwsb = T("cwsb", [128, 64, 2], F32)
    tok_sb = T("tok_sb", [128, 8], U32)
    eix_sb = T("eix_sb", [128, 16], U32)
    ci2 = T("ci2", [128, 2], U32)
    cw2 = T("cw2", [128, 2], F32)
    su32 = T("su32", [128, 4], U32)
    su32b = T("su32b", [128, 4], U32)
    gi_t = T("gi_t", [128, 4], U32)
    stile = T("stile", [128, 512], BF16)
    vec1 = T("vec1", [128, 1], F32)
    vec2 = T("vec2", [128, 1], F32)
    vec3 = T("vec3", [128, 1], F32)
    vec4 = T("vec4", [128, 1], F32)
    dvec = T("dvec", [128, 2], F32)
    dn = T("dn", [128, 16], F32)
    vals = T("vals", [128, 4], U32)
    zv = T("zv", [128, 1], F32)
    epsv = T("epsv", [128, 1], F32)

    psA = PT("psA"); psB = PT("psB")
    psO = [PT("psO%d" % i) for i in range(4)]
    psX = PT("psX"); psT = PT("psT")

    p = Prog(nc)
    def warm(func):
        p.emit("act", lambda e, func=func: e.activation(dn[:, 14:15], epsv[:, :], func,
                                                        bias=zv[:, :], scale=1.0), inc=["warmx"])
    regs = {}
    def setup_regs(e):
        r1 = e.alloc_register()
        e.reg_mov(r1, E * CAP - 1)
        regs["b16k"] = r1
        r2 = e.alloc_register()
        e.reg_mov(r2, T_ALL - 1)
        regs["b8k"] = r2
        return None
    p.emit("pool", setup_regs)

    p.emit("pool", lambda e: e.iota(e16[:, :], [[1, 16]], base=0, channel_multiplier=0,
                                    allow_small_or_imprecise_dtypes=True), inc=["k"])
    p.emit("pool", lambda e: e.iota(iop[:, :], [[1, 1]], base=0, channel_multiplier=1,
                                    allow_small_or_imprecise_dtypes=True), inc=["k"])
    p.emit("pool", lambda e: e.memset(ones_rb[:, :], 1.0), inc=["k"])
    p.emit("pool", lambda e: e.memset(ones_cf[:, :], 1.0), inc=["k"])
    p.emit("pool", lambda e: e.memset(srun[:, :], 0.0), inc=["k"])
    p.emit("pool", lambda e: e.memset(zv[:, :], 0.0), inc=["k"])
    p.emit("pool", lambda e: e.memset(epsv[:, :], EPS), inc=["k"])
    p.emit("pool", lambda e: e.memset(cisb[:, :, :], 0xFFFFFFFF), inc=["k"])
    p.emit("pool", lambda e: e.iota(sx[:, 0:128], [[1, 128]], base=0, channel_multiplier=0,
                                    allow_small_or_imprecise_dtypes=True), inc=["k"])
    mrv = p.dma("sp", rvp[:, :], rv_d)
    m_tok = p.dma("sp", tok_sb[:, :], tok_d)
    m_eix = p.dma("sp", eix_sb[:, :], eix_d)
    kc0 = p.mark("k")
    p.emit("dve", lambda e: e.tensor_scalar(e16b[:, :], e16[:, :], 65536.0, None, ALU.add),
           waits=[kc0], inc=["kd"])
    p.emit("dve", lambda e: e.tensor_scalar(idf[:, :], e16[:, :], iop[:, :], None, ALU.is_equal), inc=["kd"])
    p.emit("dve", lambda e: e.tensor_scalar(ltri[:, :], sx[:, 0:128], iop[:, :], None, ALU.is_gt), inc=["kd"])
    for md in range(4):
        p.emit("dve", lambda e, md=md: e.tensor_scalar(vec1[:, :], iop[:, :], rvp[:, :], float(128 * md),
                                                       ALU.subtract, ALU.add), waits=[mrv], inc=["kd"])
        p.emit("dve", lambda e, md=md: e.tensor_scalar(mtab[:, md, :], sx[:, 0:128], vec1[:, :], -BIG,
                                                       ALU.is_lt, ALU.mult), inc=["kd"])
    mdc = p.mark("kd")
    p.emit("pool", lambda e: e.iota(sx[:, 128:640], [[0, 16], [1, 32]], base=-3, channel_multiplier=0,
                                    allow_small_or_imprecise_dtypes=True), waits=[mdc], inc=["k"])
    p.emit("pool", lambda e: e.iota(sx[:, 640:1152], [[1, 16], [0, 32]], base=1, channel_multiplier=0,
                                    allow_small_or_imprecise_dtypes=True), inc=["k"])
    p.emit("pool", lambda e: e.iota(sx[:, 1152:1664], [[0, 512]], base=0, channel_multiplier=1,
                                    allow_small_or_imprecise_dtypes=True), inc=["k"])
    kc1 = p.mark("k")
    p.emit("act", lambda e: e.activation(sx[:, 640:1152], sx[:, 640:1152], AF.Exp, bias=zv[:, :], scale=LN2H),
           waits=[kc1], inc=["ka"])
    p.emit("dve", lambda e: e.tensor_scalar(sx[:, 1152:1664], sx[:, 1152:1664], rvp[:, :], -64.0,
                                            ALU.subtract, ALU.add), waits=[kc1], inc=["kd"])
    p.emit("dve", lambda e: e.scalar_tensor_tensor(alib[:, :], sx[:, 128:640], -128.0, sx[:, 1152:1664],
                                                   ALU.mult, ALU.add), inc=["kd"])
    p.emit("dve", lambda e: e.tensor_tensor(alib[:, :], alib[:, :], sx[:, 640:1152], ALU.mult),
           waits=["ka"], inc=["kd"])
    mbq = p.dma("pool", brow3[0:1, 0:C], bq_d)
    p.dma("pool", brow3[0:1, C:2 * C], bk_d)
    p.dma("pool", brow3[0:1, 2 * C:3 * C], bv_d)
    mbr = p.dma("pool", brow_o[0:1, :], bo_d)
    mg_ = p.dma("sp", out_res[0:16, 0, :], gw_d)
    idm = p.mark("kd")
    for fc in range(16):
        def trg(e, fc=fc):
            return e.transpose(psB[0:128, 0:16], out_res[0:16, 0, fc * 128:(fc + 1) * 128], idf[0:16, 0:16])
        p.emit("pe", trg, waits=[mg_, idm, ("kgt", 2 * fc)], inc=["kp"])
        p.emit("act", lambda e, fc=fc: e.activation(gthi[:, fc, :], psB[:, 0:16], AF.Copy, bias=0.0, scale=1.0),
               waits=[("kp", fc + 1)], inc=["kgt"])
        p.emit("dve", lambda e, fc=fc: e.scalar_tensor_tensor(gtlo[:, fc, :], gthi[:, fc, :], -1.0, psB[:, 0:16],
                                                              ALU.mult, ALU.add),
               waits=[("kgt", 2 * fc + 1)], inc=["kgt"])
    mzp = p.mark("k")
    for _q in range(4):
        p.dma("pool", g_dd[_q * 4096:(_q + 1) * 4096, :].rearrange("(p a) f -> p (a f)", p=128), cisb[:, :, :].rearrange("p a k -> p (a k)"),
              waits=[mzp], sem="g_init")
    p.barrier("b0")

    def emit_ln(src_ap, waitm):
        p.emit("dve", lambda e: e.tensor_reduce(vec1[:, :], src_ap, axis=AX.X, op=ALU.add),
               waits=[waitm], inc=["ln_d"])
        p.emit("act", lambda e: e.activation(psX[:, :], src_ap[:, 0:512], AF.Square, bias=zv[:, :], scale=1.0,
                                             accum_out=vec2[:, :]), waits=[waitm, "d4run", "d4"], inc=["ln_a"])
        p.emit("act", lambda e: e.activation(psX[:, :], src_ap[:, 512:1024], AF.Square, bias=zv[:, :], scale=1.0,
                                             accum_out=vec3[:, :]), inc=["ln_a"])
        p.emit("act", lambda e: e.activation(psT[:, :], src_ap[:, 1024:1536], AF.Square, bias=zv[:, :], scale=1.0,
                                             accum_out=vec4[:, :]), inc=["ln_a"])
        p.emit("act", lambda e: e.activation(psT[:, :], src_ap[:, 1536:2048], AF.Square, bias=zv[:, :], scale=1.0,
                                             accum_out=dvec[:, 0:1]), inc=["ln_a"])
        p.emit("dve", lambda e: e.tensor_scalar(vec1[:, :], vec1[:, :], 1.0 / C, None, ALU.mult), inc=["ln_d"])
        am = p.mark("ln_a")
        p.emit("dve", lambda e: e.tensor_tensor(vec2[:, :], vec2[:, :], vec3[:, :], ALU.add), waits=[am], inc=["ln_d"])
        p.emit("dve", lambda e: e.tensor_tensor(vec4[:, :], vec4[:, :], dvec[:, 0:1], ALU.add), inc=["ln_d"])
        p.emit("dve", lambda e: e.tensor_tensor(vec2[:, :], vec2[:, :], vec4[:, :], ALU.add), inc=["ln_d"])
        p.emit("dve", lambda e: e.tensor_tensor(vec3[:, :], vec1[:, :], vec1[:, :], ALU.mult), inc=["ln_d"])
        p.emit("dve", lambda e: e.scalar_tensor_tensor(vec3[:, :], vec2[:, :], 1.0 / C, vec3[:, :],
                                                       ALU.mult, ALU.subtract), inc=["ln_d"])
        p.emit("act", lambda e: e.activation(vec3[:, :], vec3[:, :], AF.Sqrt, bias=epsv[:, :], scale=1.0),
               waits=["ln_d"], inc=["ln_a"])
        p.emit("dve", lambda e: e.reciprocal(vec4[:, :], vec3[:, :]), waits=["ln_a"], inc=["ln_d"])
        p.emit("dve", lambda e: e.tensor_scalar(sbf[:, :], src_ap, vec1[:, :], vec4[:, :],
                                                ALU.subtract, ALU.mult), waits=["sbf_fr", "qkv_st"], inc=["ln_d"])
        return p.mark("ln_d")

    warm(AF.Sqrt)
    sx_free = None
    for m in range(8):
        mload = p.dma("sp", sx[:, :], xo_d[m * 128:(m + 1) * 128, :], waits=[sx_free])
        p.emit("dve", lambda e, m=m: e.tensor_copy(out_res[:, m, :], sx[:, :]), waits=[mload], inc=["ln_d"])
        mhi = emit_ln(sx[:, :], mload)
        mw1 = p.dma("sp", xn_d[m * 128:(m + 1) * 128, :], sbf[:, :], waits=[mhi], sem="sbf_fr")
        p.emit("dve", lambda e: e.tensor_scalar(sx[:, :], sx[:, :], vec1[:, :], vec4[:, :],
                                                ALU.subtract, ALU.mult), inc=["ln_d"])
        p.emit("dve", lambda e: e.tensor_tensor(sx[:, :], sx[:, :], sbf[:, :], ALU.subtract),
               waits=[mw1], inc=["ln_d"])
        p.emit("act", lambda e: e.activation(sbf[:, :], sx[:, :], AF.Copy, bias=0.0, scale=1.0),
               waits=["ln_d"], inc=["ln_a"])
        mlo = p.mark("ln_a")
        mw2 = p.dma("sp", lo_d[m * 128:(m + 1) * 128, :], sbf[:, :], waits=[mlo], sem="sbf_fr")
        sx_free = p.mark("ln_d")

    m_st1 = p.mark("sbf_fr")
    for fc in range(16):
        p.dma("act", xnT[:, fc, :], xn_d[0:1024, fc * 128:(fc + 1) * 128], waits=[m_st1], sem="tp_dma", transpose=True)
        p.dma("act", big16[:, fc, :], lo_d[0:512, fc * 128:(fc + 1) * 128], sem="tp_dma", transpose=True)
        p.dma("act", wrhs[:, 0, fc, :], lo_d[512:1024, fc * 128:(fc + 1) * 128], sem="tp_dma", transpose=True)
    mtp = p.mark("tp_dma")
    warm(AF.Exp)
    for m in range(8):
        loT = big16 if m < 4 else wrhs
        mloc = m % 4
        first = True
        for fc in range(16):
            for which in range(3):
                def mml(e, which=which, fc=fc, m=m, mloc=mloc, loT=loT, first=first):
                    gt = gthi if which != 2 else gtlo
                    lh = (xnT[:, fc, m * 128:(m + 1) * 128] if which != 1 else
                          (big16[:, fc, mloc * 128:(mloc + 1) * 128] if m < 4 else wrhs[:, 0, fc, mloc * 128:(mloc + 1) * 128]))
                    return e.matmul(psB[:, 0:16], lh, gt[:, fc, :], start=first, stop=False)
                p.emit("pe", mml, waits=[mtp, "rt_d", ("kgt", 32), "kd"] if first else [],
                       inc=["lg_p"] if (fc == 15 and which == 2) else [])
                first = False
        p.emit("dve", lambda e: e.tensor_copy(s16a[:, :], psB[:, 0:16]), waits=["lg_p"], inc=["rt_d"])
        p.emit("dve", lambda e: e.tensor_reduce(vec1[:, :], s16a[:, :], axis=AX.X, op=ALU.max, negate=True),
               inc=["rt_d"])
        p.emit("act", lambda e: e.activation(s16b[:, :], s16a[:, :], AF.Exp, bias=vec1[:, :], scale=1.0),
               waits=["rt_d"], inc=["rt_a"])
        p.emit("dve", lambda e: e.tensor_reduce(vec2[:, :], s16b[:, :], axis=AX.X, op=ALU.add),
               waits=["rt_a"], inc=["rt_d"])
        p.emit("dve", lambda e: e.reciprocal(vec3[:, :], vec2[:, :]), inc=["rt_d"])
        p.emit("dve", lambda e: e.tensor_scalar(s16a[:, :], s16b[:, :], vec3[:, :], None, ALU.mult), inc=["rt_d"])
        p.emit("dve", lambda e: e.max(out=w8[:, :], in_=s16a[:, :]), inc=["rt_d"])
        mm_ = m % 2
        p.emit("dve", lambda e: e.tensor_scalar(s16b[:, :], s16a[:, :], w8[:, 0:1], None, ALU.is_equal), inc=["rt_d"])
        p.emit("dve", lambda e: e.scalar_tensor_tensor(s16c[:, :], s16b[:, :], -65536.0, e16b[:, :], ALU.mult, ALU.add),
               inc=["rt_d"])
        p.emit("dve", lambda e: e.tensor_reduce(vec1[:, :], s16c[:, :], axis=AX.X, op=ALU.min), inc=["rt_d"])
        p.emit("dve", lambda e, mm_=mm_: e.tensor_copy(rt4[:, mm_, 0:1], vec1[:, :]),
               inc=["rt_d"])
        p.emit("dve", lambda e, mm_=mm_: e.tensor_scalar(s16b[:, :], e16[:, :], rt4[:, mm_, 0:1], None, ALU.is_equal),
               inc=["rt_d"])
        p.emit("dve", lambda e: e.scalar_tensor_tensor(s16c[:, :], s16b[:, :], -BIG, s16a[:, :], ALU.mult, ALU.add),
               inc=["rt_d"])
        p.emit("dve", lambda e, mm_=mm_: e.tensor_reduce(rt4[:, mm_, 3:4], s16c[:, :], axis=AX.X, op=ALU.max),
               inc=["rt_d"])
        p.emit("dve", lambda e, mm_=mm_: e.tensor_copy(rt4[:, mm_, 2:3], w8[:, 0:1]), inc=["rt_d"])
        p.emit("dve", lambda e, mm_=mm_: e.tensor_scalar(s16b[:, :], s16c[:, :], rt4[:, mm_, 3:4], None, ALU.is_equal),
               inc=["rt_d"])
        p.emit("dve", lambda e: e.scalar_tensor_tensor(s16c[:, :], s16b[:, :], -65536.0, e16b[:, :], ALU.mult, ALU.add),
               inc=["rt_d"])
        p.emit("dve", lambda e: e.tensor_reduce(vec1[:, :], s16c[:, :], axis=AX.X, op=ALU.min), inc=["rt_d"])
        p.emit("dve", lambda e, mm_=mm_: e.tensor_copy(rt4[:, mm_, 1:2], vec1[:, :]),
               inc=["rt_d"])
        mrt = p.mark("rt_d")
        p.dma("sp", r_in[m * 128:(m + 1) * 128, :], rt4[:, mm_, :], waits=[mrt], sem="rt_w")
    p.emit("pool", lambda e: e.collective_compute(
        "AllGather", ALU.bypass, replica_groups=[list(range(NC))],
        ins=[r_in.tensor.ap().opt()], outs=[r_all.tensor.ap().opt()]),
        waits=["rt_w"], inc=["cc"])
    cc_rt = p.mark("cc")

    ps_free = {0: p.mark("ps_ab"), 1: p.mark("rt_d")}
    wl_free = [None, None]
    for wi, (w_src, brow_i, dst_dram) in enumerate(((wq_d, 0, q_dd), (wk_d, 1, kvk_in))):
        for hdc in range(16):
            slot = hdc % 2
            mw = p.dma("pool", wlhs[:, slot, :, :],
                       w_src[:, hdc * 128:(hdc + 1) * 128].rearrange("(a p) c -> p a c", p=128),
                       waits=[wl_free[slot]], sem="w_dma")
            for tc in range(2):
                pi = tc
                ps = psA if pi == 0 else psB
                def mmb(e, hdc=hdc, brow_i=brow_i, ps=ps):
                    return e.matmul(ps[:, :], brow3[0:1, brow_i * C + hdc * 128: brow_i * C + (hdc + 1) * 128],
                                    ones_rb[0:1, :], start=True, stop=False)
                p.emit("pe", mmb, waits=[mw, mbr, ps_free[pi]], inc=[])
                for fc in range(16):
                    def mmq(e, fc=fc, tc=tc, slot=slot, ps=ps):
                        return e.matmul(ps[:, :], wlhs[:, slot, fc, :], xnT[:, fc, tc * 512:(tc + 1) * 512],
                                        start=False, stop=(fc == 15))
                    p.emit("pe", mmq, inc=["qk_p"] if fc == 15 else [])
                mqk = p.mark("qk_p")
                bslot = tc
                def ev(e, bslot=bslot, ps=ps):
                    return e.activation(sbf[:, bslot * 512:(bslot + 1) * 512], ps[:, :], AF.Copy, bias=0.0, scale=1.0)
                p.emit("act", ev, waits=[mqk, ("qkv_st", max(0, p.cnt("qkv_st") - 16))], inc=["qkv_ev"])
                ps_free[pi] = p.mark("qkv_ev")
                p.dma("sp", dst_dram[hdc * 128:(hdc + 1) * 128, tc * 512:(tc + 1) * 512],
                      sbf[:, bslot * 512:(bslot + 1) * 512], waits=[ps_free[pi]], sem="qkv_st")
            wl_free[slot] = p.mark("qk_p")
    wr3 = [None, None]
    for hg in range(4):
        ws3 = hg % 2
        mwv = p.dma("pool", wrhs[:, ws3, :, :],
                    wv_d[:, hg * 512:(hg + 1) * 512].rearrange("(a p) c -> p a c", p=128),
                    waits=[wr3[ws3], ("lg_p", p.cnt("lg_p"))], sem="w_dma")
        for tchunk in range(8):
            pi = tchunk % 2
            ps = psA if pi == 0 else psB
            def mmbv(e, hg=hg, ps=ps):
                return e.matmul(ps[:, :], ones_rb[0:1, 0:128], brow3[0:1, 2 * C + hg * 512: 2 * C + (hg + 1) * 512],
                                start=True, stop=False)
            p.emit("pe", mmbv, waits=[mwv, ps_free[pi]], inc=[])
            for fc in range(16):
                def mmv(e, fc=fc, tchunk=tchunk, ps=ps, ws3=ws3):
                    return e.matmul(ps[:, :], xnT[:, fc, tchunk * 128:(tchunk + 1) * 128], wrhs[:, ws3, fc, :],
                                    start=False, stop=(fc == 15))
                p.emit("pe", mmv, inc=["v_p"] if fc == 15 else [])
            mv = p.mark("v_p")
            bslot = 2 + pi
            def evv(e, bslot=bslot, ps=ps):
                return e.activation(sbf[:, bslot * 512:(bslot + 1) * 512], ps[:, :], AF.Copy, bias=0.0, scale=1.0)
            p.emit("act", evv, waits=[mv, ("qkv_st", max(0, p.cnt("qkv_st") - 16))], inc=["qkv_ev"])
            ps_free[pi] = p.mark("qkv_ev")
            p.dma("sp", kvv_in[tchunk * 128:(tchunk + 1) * 128, hg * 512:(hg + 1) * 512],
                  sbf[:, bslot * 512:(bslot + 1) * 512], waits=[ps_free[pi]], sem="qkv_st")
        wr3[ws3] = p.mark("v_p")
    p.emit("pool", lambda e: e.collective_compute(
        "AllGather", ALU.bypass, replica_groups=[[0, 1, 2, 3], [4, 5, 6, 7]],
        ins=[kvk_in.tensor.ap().opt()], outs=[kt_all.tensor.ap().opt()]),
        waits=["qkv_st"], inc=["cc"])
    p.emit("pool", lambda e: e.collective_compute(
        "AllGather", ALU.bypass, replica_groups=[[0, 1, 2, 3], [4, 5, 6, 7]],
        ins=[kvv_in.tensor.ap().opt()], outs=[v_all.tensor.ap().opt()]),
        inc=["cc"])
    cc_kv = p.mark("cc")

    p.emit("pool", lambda e: e.memset(sx[:, 0:128], 1.0), waits=["ln_d"], inc=["k5"])
    run_step = [0]
    for g in range(64):
        rk, mg = chunk_owner(g)
        gm = g % 2
        mrt = p.dma("sp", rt4[:, gm, :], r_all[rk * 1024 + mg * 128: rk * 1024 + (mg + 1) * 128, :],
                    waits=[cc_rt, ("d4", p.cnt("d4"))], sem="r_ld")
        p.emit("dve", lambda e, gm=gm: e.tensor_scalar(s16a[:, :], e16[:, :], rt4[:, gm, 0:1], None, ALU.is_equal),
               waits=[mrt], inc=["d4"])
        p.emit("dve", lambda e, gm=gm: e.tensor_scalar(s16b[:, :], e16[:, :], rt4[:, gm, 1:2], None, ALU.is_equal),
               inc=["d4"])
        p.emit("dve", lambda e: e.tensor_tensor(s16c[:, :], s16a[:, :], s16b[:, :], ALU.add), inc=["d4"])
        mB = p.mark("d4")
        def mmbc(e):
            return e.matmul(psX[0:128, 0:16], sx[0:1, 0:128], srun[0:1, :], start=True, stop=False)
        p.emit("pe", mmbc, waits=[mB, ("d4run", run_step[0]), ("k5", 1)], inc=[])
        def mmtri(e):
            return e.matmul(psX[0:128, 0:16], ltri[:, :], s16c[:, :], start=False, stop=True)
        p.emit("pe", mmtri, inc=["p4"])
        def mmtot(e, g=g):
            return e.matmul(psT[0:128, 0:16], sx[:, 0:128], s16c[:, :], start=(g == 0), stop=(g == 63))
        p.emit("pe", mmtot, inc=["p4"])
        mp4 = p.mark("p4")
        p.emit("dve", lambda e: e.tensor_copy(s16c[:, :], psX[:, 0:16]), waits=[mp4], inc=["d4"])
        p.emit("dve", lambda e: e.tensor_copy(srun[0:1, :], psT[0:1, 0:16]), inc=["d4run"])
        run_step[0] += 1
        p.emit("dve", lambda e: e.tensor_tensor(s16a[:, :], s16a[:, :], s16c[:, :], ALU.mult), inc=["d4"])
        p.emit("dve", lambda e: e.tensor_reduce(vec1[:, :], s16a[:, :], axis=AX.X, op=ALU.add), inc=["d4"])
        p.emit("dve", lambda e: e.tensor_tensor(s16b[:, :], s16b[:, :], s16c[:, :], ALU.mult), inc=["d4"])
        p.emit("dve", lambda e: e.tensor_reduce(vec2[:, :], s16b[:, :], axis=AX.X, op=ALU.add), inc=["d4"])
        for kk, posv, dst in ((0, vec1, su32), (1, vec2, su32b)):
            p.emit("dve", lambda e, posv=posv: e.tensor_scalar(vec3[:, :], posv[:, :], float(CAP), 1.0e6,
                                                               ALU.is_ge, ALU.mult),
                   waits=[("sc4", p.cnt("sc4"))], inc=["d4"])
            p.emit("dve", lambda e, gm=gm, kk=kk, posv=posv: e.scalar_tensor_tensor(
                vec4[:, :], rt4[:, gm, kk:kk + 1], float(CAP), posv[:, :], ALU.mult, ALU.add), inc=["d4"])
            p.emit("dve", lambda e: e.tensor_tensor(vec4[:, :], vec4[:, :], vec3[:, :], ALU.add), inc=["d4"])
            p.emit("dve", lambda e, dst=dst: e.tensor_copy(dst[:, 0:1], vec4[:, :]), inc=["d4"])
        mid = p.mark("d4")
        p.emit("pool", lambda e, g=g: e.iota(vals[:, :], [[0, 4]], base=g * 128, channel_multiplier=1),
               waits=[("sc4", p.cnt("sc4"))], inc=["k4"])
        mvv = p.mark("k4")
        def sc1(e):
            return e.indirect_dma_start(g_dd, bass.IndirectOffsetOnAxis(su32[:, 0:1], 0), vals[:, :], None,
                                        bounds_check=regs["b16k"], oob_is_err=False)
        def sc2(e):
            return e.indirect_dma_start(g_dd, bass.IndirectOffsetOnAxis(su32b[:, 0:1], 0), vals[:, :], None,
                                        bounds_check=regs["b16k"], oob_is_err=False)
        p.emit("pool", sc1, waits=[mid, mvv, "g_init"], inc=[("sc4", 16)])
        p.emit("pool", sc2, inc=[("sc4", 16)])
        p.emit("dve", lambda e, g=g, gm=gm: e.tensor_copy(cwsb[:, g, 0:1], rt4[:, gm, 2:3]), inc=["d4"])
        p.emit("dve", lambda e, g=g, gm=gm: e.tensor_copy(cwsb[:, g, 1:2], rt4[:, gm, 3:4]), inc=["d4"])
        for kk, posv in ((0, vec1), (1, vec2)):
            p.emit("dve", lambda e, posv=posv: e.tensor_scalar(vec3[:, :], posv[:, :], float(CAP - 1), None,
                                                               ALU.min), inc=["d4"])
            p.emit("dve", lambda e, gm=gm, kk=kk: e.scalar_tensor_tensor(
                vec3[:, :], rt4[:, gm, kk:kk + 1], float(CAP), vec3[:, :], ALU.mult, ALU.add), inc=["d4"])
            p.emit("dve", lambda e, g=g, kk=kk: e.tensor_copy(cisb[:, g, kk:kk + 1], vec3[:, :]), inc=["d4"])
    mtab4 = p.mark("d4")
    p.dma("sp", ci_dd.rearrange("(a p) k -> p a k", p=128), cisb[:, :, :], waits=[mtab4], sem="ct_w")
    p.dma("sp", cw_dd.rearrange("(a p) k -> p a k", p=128), cwsb[:, :, :], waits=[mtab4], sem="ct_w")

    psS_free = dict(ps_free)
    wl_free2 = [None, None]
    wr_last = [None, None]
    evst_free = None
    for le in range(2):
        warm(AF.Sqrt)
        egath = None
        for st in range(8):
            def ggi(e, le=le, st=st):
                return e.indirect_dma_start(gi_t[:, :], None, g_dd,
                                            bass.IndirectOffsetOnAxis(eix_sb[:, le * 8 + st: le * 8 + st + 1], 0),
                                            bounds_check=regs["b16k"], oob_is_err=False)
            p.emit("pool", ggi, waits=[("sc4", p.cnt("sc4")), egath, m_eix], inc=[("g6", 16)])
            p.emit("pool", lambda e: e.memset(sx[:, :], 0.0), waits=[egath, "ln_d", ("p4", p.cnt("p4"))], inc=["k6"])
            mz = p.mark("k6")
            def gat(e):
                return e.indirect_dma_start(sx[:, :], None, x_d,
                                            bass.IndirectOffsetOnAxis(gi_t[:, 0:1], 0),
                                            bounds_check=regs["b8k"], oob_is_err=False)
            p.emit("pool", gat, waits=[("g6", p.cnt("g6")), mz], inc=[("g6", 16)])
            mgat = p.mark("g6")
            mhi = emit_ln(sx[:, :], mgat)
            egath = p.dma("sp", e_dd[st * 128:(st + 1) * 128, :], sbf[:, :], waits=[mhi], sem="sbf_fr")
        warm(AF.Silu)
        for sc in range(2):
            et_ld = None
            for fc in range(16):
                et_ld = p.dma("act", big16[:, fc, :], e_dd[sc * 512:(sc + 1) * 512, fc * 128:(fc + 1) * 128],
                              waits=[("sbf_fr", p.cnt("sbf_fr")), ("p6", p.cnt("p6")), ("lg_p", p.cnt("lg_p"))],
                              sem="tp_dma", transpose=True)
            for ic in range(11):
                pr = ic % 2
                psg = psA if pr == 0 else psO[0]
                psu = psB if pr == 0 else psO[1]
                kg, ku = (0, 1) if pr == 0 else (2, 3)
                mwg = p.dma("pool", wlhs[:, 2 * pr, :, :],
                            wg_d[le, :, ic * 128:(ic + 1) * 128].rearrange("(a p) i -> p a i", p=128),
                            waits=[wl_free2[pr]], sem="w_dma")
                mwu = p.dma("pool", wlhs[:, 2 * pr + 1, :, :],
                            wu_d[le, :, ic * 128:(ic + 1) * 128].rearrange("(a p) i -> p a i", p=128),
                            sem="w_dma")
                for fc in range(16):
                    def mmg(e, fc=fc, pr=pr, psg=psg):
                        return e.matmul(psg[:, :], wlhs[:, 2 * pr, fc, :], big16[:, fc, :],
                                        start=(fc == 0), stop=(fc == 15))
                    p.emit("pe", mmg, waits=([mwg, et_ld, psS_free.get(kg)] if fc == 0 else []), inc=[])
                    def mmu(e, fc=fc, pr=pr, psu=psu):
                        return e.matmul(psu[:, :], wlhs[:, 2 * pr + 1, fc, :], big16[:, fc, :],
                                        start=(fc == 0), stop=(fc == 15))
                    p.emit("pe", mmu, waits=([mwu, psS_free.get(ku)] if fc == 0 else []),
                           inc=["p6"] if fc == 15 else [])
                mgu = p.mark("p6")
                p.emit("act", lambda e, psg=psg: e.activation(stile[:, :], psg[:, :], AF.Silu, bias=zv[:, :], scale=1.0),
                       waits=[mgu, "d6"], inc=["a6"])
                p.emit("dve", lambda e, ic=ic, sc=sc, psu=psu: e.tensor_tensor(
                    at_v[:, ic, sc * 512:(sc + 1) * 512], stile[:, :], psu[:, :], ALU.mult),
                    waits=["a6", ("p6", p.cnt("p6"))], inc=["d6"])
                psS_free[kg] = p.mark("a6")
                psS_free[ku] = p.mark("d6")
                wl_free2[pr] = p.mark("p6")
        for hg in range(4):
            ws = hg % 2
            mwd = p.dma("pool", wrhs[:, ws, 0:11, :],
                        wd_d[le, :, hg * 512:(hg + 1) * 512].rearrange("(a p) c -> p a c", p=128),
                        waits=[wr_last[ws], ("lg_p", p.cnt("lg_p")), ("v_p", p.cnt("v_p"))], sem="w_dma")
            for st in range(8):
                pi = st % 4
                ps = [psA, psB, psO[0], psO[1]][pi]
                for ic in range(11):
                    def mmd(e, ic=ic, st=st, ps=ps, ws=ws):
                        return e.matmul(ps[:, :], at_v[:, ic, st * 128:(st + 1) * 128], wrhs[:, ws, ic, :],
                                        start=(ic == 0), stop=(ic == 10))
                    p.emit("pe", mmd, waits=([mwd, "d6", psS_free.get(pi)] if ic == 0 else []),
                           inc=["p6"] if ic == 10 else [])
                mdn6 = p.mark("p6")
                p.emit("act", lambda e, ps=ps: e.activation(stile[:, :], ps[:, :], AF.Copy, bias=0.0, scale=1.0),
                       waits=[mdn6, evst_free], inc=["a6"])
                psS_free[pi] = p.mark("a6")
                evst_free = p.dma("sp", eo_in[le * CAP + st * 128: le * CAP + (st + 1) * 128,
                                              hg * 512:(hg + 1) * 512],
                                  stile[:, :], waits=[psS_free[pi]], sem="eo_st")
            wr_last[ws] = p.mark("p6")
    p.emit("pool", lambda e: e.collective_compute(
        "AllGather", ALU.bypass, replica_groups=[list(range(NC))],
        ins=[eo_in.tensor.ap().opt()], outs=[eo_all.tensor.ap().opt()]),
        waits=["eo_st"], inc=["cc"])
    cc_eo = p.mark("cc")

    SKIPT = [2, 3, 3, 4, 5, 7, 10, 13, 18, 25, 99, 99, 99, 99, 99, 99]
    p.emit("pool", lambda e: e.memset(vh[:, :, :, 128:129], 1.0),
           waits=[("p6", p.cnt("p6")), "d6", ("eo_st", p.cnt("eo_st"))], inc=["k5"])
    p.emit("act", lambda e: e.activation(dn[:, 12:13], epsv[:, :], AF.Exp, bias=zv[:, :], scale=1.0),
           inc=["warm5"])
    qh_free = [None, None]
    kv_free = None
    exp_marks = {}
    psO_free = [p.mark("a6"), p.mark("a6"), p.mark("d6"), p.mark("d6")]
    stile_free = p.mark("eo_st")
    for h in range(16):
        par = 0
        mq = p.dma("sp", qh[:, par, :], q_dd[h * 128:(h + 1) * 128, :], waits=[qh_free[par]], sem="kv_ld")
        mks = []
        for rho in range(4):
            mk1 = p.dma("sp", kth[:, rho, :], kt_all[rho * C + h * 128: rho * C + (h + 1) * 128, :],
                        waits=[cc_kv, kv_free, "p6", "d6"], sem="kv_ld")
            mk2 = p.dma("sp", vh[:, rho, :, 0:128],
                        v_all[rho * 1024:(rho + 1) * 1024, h * 128:(h + 1) * 128]
                        .rearrange("(m p) d -> p m d", p=128),
                        waits=[cc_kv, kv_free, ("k5", 2)], sem="kv_ld")
            mks += [mk1, mk2]
        for qg in range(2):
            cis = list(range(4))
            mis = [qg * 4 + i for i in range(4)]
            kmax = 4 * mis[-1] + 3
            pend = []
            for kc in range(kmax + 2):
                live = [4 * mi - kc for mi in mis if -3 <= 4 * mi - kc < SKIPT[h]]
                if kc <= kmax and live:
                    pi = kc % 2
                    ps = psA if pi == 0 else psB
                    def mms(e, kc=kc, qg=qg, par=par, ps=ps):
                        return e.matmul(ps[:, :], kth[:, kc % 4, (kc // 4) * 128:(kc // 4 + 1) * 128],
                                        qh[:, par, qg * 512:(qg + 1) * 512], start=True, stop=True)
                    p.emit("pe", mms, waits=[mq] + mks + [psS_free[pi]], inc=["p5"])
                    msc = p.mark("p5")
                    newpend = []
                    for ci, mi in zip(cis, mis):
                        dcp = 4 * mi - kc
                        if dcp < -3 or dcp >= SKIPT[h]:
                            continue
                        slot = (kc % 2) * 4 + ci
                        dcol = h * 32 + dcp + 3
                        if dcp <= 0:
                            p.emit("dve", lambda e, ci=ci, ps=ps, dcp=dcp: e.tensor_tensor(
                                sfp[:, :], ps[:, ci * 128:(ci + 1) * 128], mtab[:, -dcp, :], ALU.add),
                                waits=[msc, exp_marks.get(("pv", slot))], inc=["d5"])
                            md = p.mark("d5")
                            p.emit("act", lambda e, slot=slot, dcol=dcol: e.activation(
                                pt_v[:, slot, :], sfp[:, :], AF.Exp, bias=alib[:, dcol:dcol + 1], scale=SCALE),
                                waits=[md], inc=["a5"])
                        else:
                            p.emit("act", lambda e, slot=slot, dcol=dcol, ci=ci, ps=ps: e.activation(
                                pt_v[:, slot, :], ps[:, ci * 128:(ci + 1) * 128], AF.Exp,
                                bias=alib[:, dcol:dcol + 1], scale=SCALE),
                                waits=[msc, exp_marks.get(("pv", slot))], inc=["a5"])
                        mex = p.mark("a5")
                        first_kc = max(0, 4 * mi - SKIPT[h] + 1)
                        def mmpv(e, slot=slot, ci=ci, kc=kc, kmax=kmax, mi=mi, first_kc=first_kc):
                            return e.matmul(psO[ci][:, 0:129], pt_v[:, slot, :], vh[:, kc % 4, kc // 4, :],
                                            start=(kc == first_kc), stop=(kc == min(kmax, 4 * mi + 3)))
                        newpend.append((mmpv, mex, ci, kc == first_kc, slot))
                    psS_free[pi] = p.mark("a5")
                else:
                    newpend = []
                for fn, mex, ci, isfirst, slot in pend:
                    p.emit("pe", fn, waits=[mex] + ([psO_free[ci]] if isfirst else []), inc=["p5"])
                    exp_marks[("pv", slot)] = p.mark("p5")
                pend = newpend
            for ci, mi in zip(cis, mis):
                last_kc = min(kmax, 4 * mi + 3)
                mpv = exp_marks[("pv", (last_kc % 2) * 4 + ci)]
                p.emit("dve", lambda e, ci=ci: e.tensor_copy(dn[:, ci:ci + 1], psO[ci][:, 128:129]),
                       waits=[mpv], inc=["d5"])
                p.emit("dve", lambda e, ci=ci: e.reciprocal(dn[:, 8 + ci:9 + ci], dn[:, ci:ci + 1]), inc=["d5"])
                md5 = p.mark("d5")
                p.emit("act", lambda e, ci=ci: e.activation(
                    stile[:, ci * 128:(ci + 1) * 128], psO[ci][:, 0:128], AF.Copy,
                    bias=0.0, scale=dn[:, 8 + ci:9 + ci]),
                    waits=[md5, stile_free], inc=["a5"])
                psO_free[ci] = p.mark("a5")
            mev = p.mark("a5")
            stile_free = p.dma("sp", o_dd[qg * 512: (qg + 1) * 512, h * 128:(h + 1) * 128]
                               .rearrange("(a p) d -> p a d", p=128),
                               stile[:, :].rearrange("p (a d) -> p a d", d=128),
                               waits=[mev], sem="o_st")
        qh_free[par] = p.mark("p5")
        kv_free = p.mark("p5")

    psS_free[2] = p.mark("a5")
    psS_free[3] = p.mark("a5")
    wr7 = [p.mark("p6"), p.mark("p6")]
    for mig in range(2):
        ot_ld = None
        for fc in range(16):
            ot_ld = p.dma("act", big16[:, fc, :], o_dd[mig * 512:(mig + 1) * 512, fc * 128:(fc + 1) * 128],
                          waits=[("o_st", p.cnt("o_st")), ("p7", p.cnt("p7")), ("p6", p.cnt("p6"))],
                          sem="tp_dma", transpose=True)
        for cg in range(4):
            ws7 = cg % 2
            mwo = p.dma("pool", wrhs[:, ws7, :, :],
                        wo_d[:, cg * 512:(cg + 1) * 512].rearrange("(a p) c -> p a c", p=128),
                        waits=[wr7[ws7]], sem="w_dma")
            for mi2 in range(4):
                mi = mig * 4 + mi2
                pi = mi2 % 4
                ps = [psA, psB, psO[0], psO[1]][pi]
                def mmbo(e, cg=cg, ps=ps):
                    return e.matmul(ps[:, :], ones_rb[0:1, 0:128], brow_o[0:1, cg * 512:(cg + 1) * 512],
                                    start=True, stop=False)
                p.emit("pe", mmbo, waits=[mwo, ot_ld, psS_free.get(pi)], inc=[])
                for fc in range(16):
                    def mmo(e, fc=fc, mi2=mi2, ps=ps, ws7=ws7):
                        return e.matmul(ps[:, :], big16[:, fc, mi2 * 128:(mi2 + 1) * 128], wrhs[:, ws7, fc, :],
                                        start=False, stop=(fc == 15))
                    p.emit("pe", mmo, inc=["p7"] if fc == 15 else [])
                mo7 = p.mark("p7")
                p.emit("dve", lambda e, mi=mi, cg=cg, ps=ps: e.tensor_tensor(
                    out_res[:, mi, cg * 512:(cg + 1) * 512], out_res[:, mi, cg * 512:(cg + 1) * 512],
                    ps[:, :], ALU.add), waits=[mo7], inc=["d7"])
                psS_free[pi] = p.mark("d7")
            wr7[ws7] = p.mark("p7")

    for ih in range(2):
        warm(AF.Silu)
        for ic in range(11):
            ica = ih * 11 + ic
            pr = ic % 2
            mwg = p.dma("pool", wlhs[:, 2 * pr, :, :],
                        wsg_d[:, ica * 128:(ica + 1) * 128].rearrange("(a p) i -> p a i", p=128),
                        waits=[wl_free2[pr]], sem="w_dma")
            mwu = p.dma("pool", wlhs[:, 2 * pr + 1, :, :],
                        wsu_d[:, ica * 128:(ica + 1) * 128].rearrange("(a p) i -> p a i", p=128),
                        sem="w_dma")
            psg = psA if pr == 0 else psO[0]
            psu = psB if pr == 0 else psO[1]
            kg, ku = (0, 1) if pr == 0 else (2, 3)
            for sc in range(2):
                for fc in range(16):
                    def mmg(e, fc=fc, sc=sc, pr=pr, psg=psg):
                        return e.matmul(psg[:, :], wlhs[:, 2 * pr, fc, :], xnT[:, fc, sc * 512:(sc + 1) * 512],
                                        start=(fc == 0), stop=(fc == 15))
                    p.emit("pe", mmg, waits=([mwg, psS_free.get(kg)] if fc == 0 else []), inc=[])
                    def mmu(e, fc=fc, sc=sc, pr=pr, psu=psu):
                        return e.matmul(psu[:, :], wlhs[:, 2 * pr + 1, fc, :], xnT[:, fc, sc * 512:(sc + 1) * 512],
                                        start=(fc == 0), stop=(fc == 15))
                    p.emit("pe", mmu, waits=([mwu, psS_free.get(ku)] if fc == 0 else []),
                           inc=["p8"] if fc == 15 else [])
                mgu = p.mark("p8")
                p.emit("act", lambda e, psg=psg: e.activation(stile[:, :], psg[:, :], AF.Silu, bias=zv[:, :], scale=1.0),
                       waits=[mgu, "d8", ("o_st", p.cnt("o_st"))], inc=["a8"])
                p.emit("dve", lambda e, ic=ic, sc=sc, psu=psu: e.tensor_tensor(
                    at_v[:, ic, sc * 512:(sc + 1) * 512], stile[:, :], psu[:, :], ALU.mult),
                    waits=["a8", ("p8", p.cnt("p8")), ("p6", p.cnt("p6")), "p5"], inc=["d8"])
                psS_free[kg] = p.mark("a8")
                psS_free[ku] = p.mark("d8")
            wl_free2[pr] = p.mark("p8")
        for hg in range(4):
            ws8 = hg % 2
            mwd = p.dma("pool", wrhs[:, ws8, 0:11, :],
                        wsd_d[ih * 11 * 128:(ih + 1) * 11 * 128, hg * 512:(hg + 1) * 512]
                        .rearrange("(a p) c -> p a c", p=128),
                        waits=[wr7[ws8]], sem="w_dma")
            for st in range(8):
                pi = st % 4
                ps = [psA, psB, psO[0], psO[1]][pi]
                for ic in range(11):
                    def mmd(e, ic=ic, st=st, ps=ps, ws8=ws8):
                        return e.matmul(ps[:, :], at_v[:, ic, st * 128:(st + 1) * 128], wrhs[:, ws8, ic, :],
                                        start=(ic == 0), stop=(ic == 10))
                    p.emit("pe", mmd, waits=([mwd, "d8", psS_free.get(pi)] if ic == 0 else []),
                           inc=["p8"] if ic == 10 else [])
                mdn8 = p.mark("p8")
                p.emit("dve", lambda e, st=st, hg=hg, ps=ps: e.tensor_tensor(
                    out_res[:, st, hg * 512:(hg + 1) * 512], out_res[:, st, hg * 512:(hg + 1) * 512],
                    ps[:, :], ALU.add), waits=[mdn8], inc=["d8"])
                psS_free[pi] = p.mark("d8")
            wr7[ws8] = p.mark("p8")

    gath_free = None
    for m in range(8):
        def gci(e, m=m):
            return e.indirect_dma_start(ci2[:, :], None, ci_dd,
                                        bass.IndirectOffsetOnAxis(tok_sb[:, m:m + 1], 0),
                                        bounds_check=regs["b8k"], oob_is_err=False)
        def gcw(e, m=m):
            return e.indirect_dma_start(cw2[:, :], None, cw_dd,
                                        bass.IndirectOffsetOnAxis(tok_sb[:, m:m + 1], 0),
                                        bounds_check=regs["b8k"], oob_is_err=False)
        p.emit("pool", gci, waits=["ct_w", gath_free, m_tok], inc=[("g9", 16)])
        p.emit("pool", gcw, inc=[("g9", 16)])
        mci = p.mark("g9")
        for kk in range(2):
            p.emit("dve", lambda e, kk=kk: e.tensor_copy(su32[:, 0:1], ci2[:, kk:kk + 1]),
                   waits=[mci, ("g9", p.cnt("g9"))], inc=["d9"])
            mi9 = p.mark("d9")
            def gat9(e):
                return e.indirect_dma_start(sbf[:, :], None, eo_all,
                                            bass.IndirectOffsetOnAxis(su32[:, 0:1], 0),
                                            bounds_check=regs["b16k"], oob_is_err=False)
            p.emit("pool", gat9, waits=[mi9, cc_eo, ("d9", p.cnt("d9"))], inc=[("g9", 16)])
            mg9 = p.mark("g9")
            p.emit("dve", lambda e, m=m, kk=kk: e.scalar_tensor_tensor(
                out_res[:, m, :], sbf[:, :], cw2[:, kk:kk + 1], out_res[:, m, :], ALU.mult, ALU.add),
                waits=[mg9], inc=["d9"])
            gath_free = p.mark("d9")
        p.dma("sp", y_d[m * 128:(m + 1) * 128, :], out_res[:, m, :], waits=[("d9", p.cnt("d9"))], sem="y_st")
    p.emit("sp", lambda e: e.nop(), waits=[("y_st", p.cnt("y_st"))])
    p.emit("pool", lambda e: e.nop(), waits=[("y_st", p.cnt("y_st"))])

    p.build_blocks()
    es.close()
    return nc


_CACHE = {}


def kernel(**inputs):
    xs = np.ascontiguousarray(np.asarray(inputs["L0_attn_x"], np.float32).reshape(T_ALL, C))
    shared = {
        "x": xs,
        "wq": np.ascontiguousarray(inputs["L0_attn_Wq"], np.float32),
        "wk": np.ascontiguousarray(inputs["L0_attn_Wk"], np.float32),
        "wv": np.ascontiguousarray(inputs["L0_attn_Wv"], np.float32),
        "wo": np.ascontiguousarray(inputs["L0_attn_Wo"], np.float32),
        "bq": np.ascontiguousarray(inputs["L0_attn_bq"], np.float32).reshape(1, C),
        "bk": np.ascontiguousarray(inputs["L0_attn_bk"], np.float32).reshape(1, C),
        "bv": np.ascontiguousarray(inputs["L0_attn_bv"], np.float32).reshape(1, C),
        "bo": np.ascontiguousarray(inputs["L0_attn_bo"], np.float32).reshape(1, C),
        "gw": np.ascontiguousarray(inputs["L0_ffn_gate_w"], np.float32),
        "wsg": np.ascontiguousarray(inputs["L0_ffn_ws_gate"], np.float32),
        "wsu": np.ascontiguousarray(inputs["L0_ffn_ws_up"], np.float32),
        "wsd": np.ascontiguousarray(inputs["L0_ffn_ws_down"], np.float32),
    }
    in_maps = []
    own_all = []
    for core in range(NC):
        b, r = core // 4, core % 4
        own = np.array([b * 32 + r + 4 * m for m in range(8)], np.int64)
        own_all.append(own)
        tok = (own[None, :] * 128 + np.arange(128)[:, None]).astype(np.uint32)
        eix = (2 * core * CAP + (np.arange(16)[None, :] * 128 + np.arange(128)[:, None])).astype(np.uint32)
        rv = np.full((128, 1), 128.0 * r, np.float32)
        m = dict(shared)
        xo_rows = (own[:, None] * 128 + np.arange(128)[None, :]).reshape(1024).astype(np.int64)
        m["xo"] = np.ascontiguousarray(xs[xo_rows])
        m["tok"] = tok
        m["eix"] = eix
        m["rv"] = rv
        m["wg"] = np.ascontiguousarray(inputs["L0_ffn_w_gate"][2 * core:2 * core + 2], np.float32)
        m["wu"] = np.ascontiguousarray(inputs["L0_ffn_w_up"][2 * core:2 * core + 2], np.float32)
        m["wd"] = np.ascontiguousarray(inputs["L0_ffn_w_down"][2 * core:2 * core + 2], np.float32)
        in_maps.append(m)
    if "nc" not in _CACHE:
        _CACHE["nc"] = build_core()
    res = run_bass_kernel_spmd(_CACHE["nc"], in_maps, core_ids=list(range(NC)))
    out = np.empty((T_ALL, C), np.float32)
    for core in range(NC):
        y = np.asarray(res.results[core]["y"], np.float32)
        for mi in range(8):
            g = own_all[core][mi]
            out[g * 128:(g + 1) * 128] = y[mi * 128:(mi + 1) * 128]
    return out.reshape(2, 4096, C)
